# version 02
# baseline (speedup 1.0000x reference)
import numpy as np
import ml_dtypes
import contextlib

import concourse.bass as bass
import concourse.mybir as mybir
from concourse.bass_utils import run_bass_kernel_spmd

F32 = mybir.dt.float32
BF16 = mybir.dt.bfloat16
AF = mybir.ActivationFunctionType
ALU = mybir.AluOpType
AX = mybir.AxisListType
BF = ml_dtypes.bfloat16
MAGIC = float(np.float32(1.5 * 2 ** 23))

T = 2048
D = 1024
DH = 4096
H = 16
HD = 64
NT = T // 128
FT = 256
NFT = T // FT
N_SUB = 6
DBG_NCH = 16
DEBUG_DUMP = False
ENGINES = ("tensor", "vector", "scalar", "gpsimd", "sync")


def _tf2x32(k0, k1, x0, x1):
    rot = [13, 15, 26, 6, 17, 29, 16, 24]
    x0 = x0.astype(np.uint32).copy()
    x1 = x1.astype(np.uint32).copy()
    ks = [np.uint32(k0), np.uint32(k1), np.uint32(np.uint32(k0) ^ np.uint32(k1) ^ np.uint32(0x1BD11BDA))]
    x0 += ks[0]; x1 += ks[1]
    for r in range(20):
        x0 += x1
        rr = rot[r % 8]
        x1 = (x1 << np.uint32(rr)) | (x1 >> np.uint32(32 - rr))
        x1 ^= x0
        if (r + 1) % 4 == 0:
            s = (r + 1) // 4
            x0 += ks[s % 3]
            x1 += ks[(s + 1) % 3] + np.uint32(s)
    return x0, x1


def _keep_mask(seed, p_keep, shape):
    n = int(np.prod(shape))
    idx = np.arange(n, dtype=np.uint64)
    o0, o1 = _tf2x32(0, seed, (idx >> np.uint64(32)).astype(np.uint32),
                     (idx & np.uint64(0xFFFFFFFF)).astype(np.uint32))
    u = (((o0 ^ o1) >> np.uint32(9)) | np.uint32(0x3F800000)).view(np.float32) - np.float32(1.0)
    return (u < np.float32(p_keep)).reshape(shape)


class Sched:
    def __init__(self):
        self.cnt = {e: 0 for e in ENGINES}
        self.ops = {e: [] for e in ENGINES}

    def E(self, eng, inc, opname, *a, **k):
        self.cnt[eng] += inc
        self.ops[eng].append(("i", inc, opname, a, k))
        return self.cnt[eng]

    def EX(self, eng, semname, inc, opname, *a, **k):
        self.ops[eng].append(("x", semname, inc, opname, a, k))

    def W(self, eng, on, tok):
        if tok > 0:
            self.ops[eng].append(("w", on, tok))

    def DR(self, eng):
        self.ops[eng].append(("w", eng, self.cnt[eng]))

    def barrier(self):
        toks = dict(self.cnt)
        for e in ENGINES:
            for o in ENGINES:
                if o != e:
                    self.W(e, o, toks[o])

    def replay(self, blk, sems):
        for e in ENGINES:
            items = self.ops[e]
            if not items:
                continue

            def body(h, e=e, items=items):
                for it in items:
                    if it[0] == "w":
                        h.wait_ge(sems[it[1]], it[2])
                    elif it[0] == "x":
                        _, semname, inc, opname, a, k = it
                        ins = getattr(h, opname)(*a, **k)
                        ins.then_inc(sems[semname], inc)
                    else:
                        _, inc, opname, a, k = it
                        ins = getattr(h, opname)(*a, **k)
                        if inc:
                            ins.then_inc(sems[e], inc)
            getattr(blk, e)(body)


def build_program(n_sub=N_SUB):
    nc = bass.Bass("TRN2", target_bir_lowering=False, detect_race_conditions=False)

    def din(name, shape, dt=F32):
        return nc.dram_tensor(name, shape, dt, kind="ExternalInput").ap()

    x_in = din("x_in", [T + 2, D])
    w1T = din("w1T", [D, DH])
    w2T = din("w2T", [DH, D])
    convw = din("convw", [DH, 3])
    convb = din("convb", [DH, 1])
    if n_sub >= 3:
        sel_lo = din("sel_lo", [8, 1])
        sel_hi = din("sel_hi", [8, 1])
    if n_sub >= 4:
        W1T1 = din("W1T1", [D, DH]); W2T1 = din("W2T1", [DH, D])
        b1v = din("b1v", [DH, 1]); b2v = din("b2v", [D, 1])
        maskC = din("maskC", [T, D], BF16)
    if n_sub >= 5:
        WqT1 = din("WqT1", [D, D]); WkT1 = din("WkT1", [D, D]); WvT1 = din("WvT1", [D, D])
        WoT1 = din("WoT1", [D, D])
        bqc1 = din("bqc1", [D, 1]); bkr1 = din("bkr1", [1, D]); bvr1 = din("bvr1", [1, D])
        bor1 = din("bor1", [1, D])
        gam_row = din("gam_row", [1, H])
        poscol = din("poscol", [128, 1])
    if n_sub >= 2:
        WqT0 = din("WqT0", [D, D]); WkT0 = din("WkT0", [D, D]); WvT0 = din("WvT0", [D, D])
        WoT0 = din("WoT0", [D, D]); WfT0 = din("WfT0", [D, H])
        Wg1T0 = din("Wg1T0", [D, HD]); Wg2T0 = din("Wg2T0", [HD, D])
        lnw0 = din("lnw0", [D, 1])
        MrankT = din("MrankT", [4, 4]); rm4B = din("rm4B", [64, 4])
        rmcol = din("rmcol", [4, 1])
    idn_bf = din("idn_bf", [128, 128], BF16)
    idn_f = din("idn_f", [128, 128])
    sel2 = din("sel2", [2, 256])
    triu_f = din("triu_f", [128, 128])
    triu_b = din("triu_b", [128, 128], BF16)
    out_d = nc.dram_tensor("out", [T, D], F32, kind="ExternalOutput").ap()
    if DEBUG_DUMP:
        dbg_xqT = nc.dram_tensor("dbg_xqT", [128, 8 * 2050], F32, kind="ExternalOutput").ap()
        dbg_sful = nc.dram_tensor("dbg_sful", [128, 32 * FT], F32, kind="ExternalOutput").ap()
        dbg_s2B = nc.dram_tensor("dbg_s2B", [128, FT], F32, kind="ExternalOutput").ap()
        dbg_cols = nc.dram_tensor("dbg_cols", [128, 10], F32, kind="ExternalOutput").ap()
        dbg_w1q = nc.dram_tensor("dbg_w1q", [32, 128, 8, 128], F32, kind="ExternalOutput").ap()
        dbg_hh = nc.dram_tensor("dbg_hh", [128, 32 * 16], F32, kind="ExternalOutput").ap()

    w1q_d = nc.dram_tensor("w1q_d", [32, 128, 8, 128], BF16, kind="Internal").ap()
    w2q_d = nc.dram_tensor("w2q_d", [DH, D], BF16, kind="Internal").ap()
    if n_sub >= 2:
        q_sp = nc.dram_tensor("q_sp", [16, 128, 8, 128], BF16, kind="Internal").ap()
        k_sp = nc.dram_tensor("k_sp", [16, 128, D], BF16, kind="Internal").ap()
        v_sp = nc.dram_tensor("v_sp", [16, 128, D], BF16, kind="Internal").ap()
        o_sp = nc.dram_tensor("o_sp", [16, 128, D], BF16, kind="Internal").ap()
        cc1_in = nc.dram_tensor("cc1_in", [66, D], F32, kind="Internal").ap()
        cc1_out = nc.dram_tensor("cc1_out", [4 * 66, D], F32, kind="Internal").ap()
    if n_sub >= 3:
        cc2_in = nc.dram_tensor("cc2_in", [2, D], F32, kind="Internal").ap()
        cc2_out = nc.dram_tensor("cc2_out", [8, D], F32, kind="Internal").ap()
    if n_sub >= 4:
        w1q8_d = nc.dram_tensor("w1q8_d", [32, 128, 8, 128], BF16, kind="Internal").ap()
        w2q8_d = nc.dram_tensor("w2q8_d", [DH, D], BF16, kind="Internal").ap()
    if n_sub >= 5:
        cc3_in = nc.dram_tensor("cc3_in", [66, D], F32, kind="Internal").ap()
        cc3_out = nc.dram_tensor("cc3_out", [4 * 66, D], F32, kind="Internal").ap()

    stack = contextlib.ExitStack()
    with stack:
        S = stack.enter_context

        def sb(name, shape, dt=F32):
            return S(nc.sbuf_tensor(name, shape, dt))

        xres = sb("xres", [128, 18, D])
        cols = sb("cols", [128, 64])
        scol = sb("scol", [128, 16])
        hcols = sb("hcols", [2, 16])
        ones_row = sb("ones_row", [1, 128])
        c_idb = sb("c_idb", [128, 128], BF16)
        c_idf = sb("c_idf", [128, 128])
        c_sel2 = sb("c_sel2", [2, 256])
        c_trf = sb("c_trf", [128, 128])
        c_trb = sb("c_trb", [128, 128], BF16)
        xh = sb("xh", [128, 2, D])
        xqt = sb("xqt", [128, 2, D], BF16)
        t1b = sb("t1b", [128, D])
        sqdump = sb("sqdump", [128, D])
        cwt = sb("cwt", [128, 32, 3])
        b1c32 = sb("b1c32", [128, 32])
        b2row = t1b[0:1, :]
        cbt = sb("cbt", [128, 32])
        halo_rows = xres[0:2, 17, :]

        sems = {}
        blk = S(nc.Block())
        for e in ENGINES:
            sems[e] = S(nc.semaphore("sem_" + e))
        sems["ccsem"] = S(nc.semaphore("ccsem"))
        P = Sched()
        E, W, DR = P.E, P.W, P.DR

        slot = list(range(NT))
        spare = [16]

        E("sync", 16, "dma_start", out=c_idb[:, :], in_=idn_bf[:, :])
        E("sync", 16, "dma_start", out=c_idf[:, :], in_=idn_f[:, :])
        E("sync", 16, "dma_start", out=c_sel2[:, :], in_=sel2[:, :])
        E("sync", 16, "dma_start", out=c_trf[:, :], in_=triu_f[:, :])
        E("sync", 16, "dma_start", out=c_trb[:, :], in_=triu_b[:, :])
        for i in range(NT):
            E("sync", 16, "dma_start", out=xres[:, i, :], in_=x_in[1 + 128 * i:129 + 128 * i, :])
        E("sync", 16, "dma_start", out=halo_rows[0:1, :], in_=x_in[0:1, :])
        E("sync", 16, "dma_start", out=halo_rows[1:2, :], in_=x_in[T + 1:T + 2, :])
        for j in range(32):
            E("sync", 16, "dma_start", out=cwt[:, j, :], in_=convw[128 * j:128 * (j + 1), :])
            E("sync", 16, "dma_start", out=cbt[:, j:j + 1], in_=convb[128 * j:128 * (j + 1), :])
        tok_x = P.cnt["sync"]
        tok_ones = E("vector", 1, "memset", ones_row[0:1, :], 1.0)

        stackQ = contextlib.ExitStack()
        p_sc = [stackQ.enter_context(nc.psum_tensor(f"p_sc{i}", [1, 1], F32)) for i in range(2)]
        p_bc = [stackQ.enter_context(nc.psum_tensor(f"p_bc{i}", [128, 1], F32)) for i in range(2)]
        qm = stackQ.enter_context(nc.sbuf_tensor("qm", [128, 2, D], F32))
        qsg = stackQ.enter_context(nc.sbuf_tensor("qsg", [128, 2, D], F32))

        wq_chunks = []
        for j in range(8):
            for q in range(4):
                wq_chunks.append(("w1", w1T[128 * j:128 * (j + 1), 1024 * q:1024 * (q + 1)], (j, q)))
        for j in range(32):
            wq_chunks.append(("w2", w2T[128 * j:128 * (j + 1), :], (j,)))

        nwq = len(wq_chunks)
        sync_base = P.cnt["sync"]
        vec_base = P.cnt["vector"] + 4
        tok_ld = [sync_base + 16 * (i + 1) for i in range(nwq)]
        tok_red = [vec_base + 2 * (i + 1) for i in range(nwq)]
        for i, (w, src, _) in enumerate(wq_chunks):
            if i >= 2:
                W("sync", "vector", tok_red[i - 2])
            tok_ld2 = E("sync", 16, "dma_start", out=xh[:, i % 2, :], in_=src)
            assert tok_ld2 == tok_ld[i]
        E("vector", 1, "memset", cols[:, 0:1], 0.0)
        E("vector", 1, "memset", cols[:, 1:2], 0.0)
        E("vector", 1, "memset", cols[:, 2:3], 0.0)
        tok_msets = E("vector", 1, "memset", cols[:, 3:4], 0.0)
        W("vector", "vector", tok_msets)
        pp = {"w1": 0, "w2": 0}
        acc_final = {}
        for i, (w, src, _) in enumerate(wq_chunks):
            W("vector", "sync", tok_ld[i])
            E("vector", 1, "tensor_reduce", cols[:, 4:5], xh[:, i % 2, :], AX.X, ALU.add,
              apply_absolute_value=True)
            DR("vector")
            base = 0 if w == "w1" else 2
            a, b = base + pp[w] % 2, base + (pp[w] + 1) % 2
            pp[w] += 1
            tok_red2 = E("vector", 1, "tensor_tensor", cols[:, b:b + 1],
                         cols[:, a:a + 1], cols[:, 4:5], ALU.add)
            DR("vector")
            assert tok_red2 == tok_red[i]
            acc_final[w] = b
        tok_ocol = E("vector", 1, "memset", cols[:, 5:6], 1.0)
        tok_sums = P.cnt["vector"]
        W("tensor", "vector", tok_sums)
        E("tensor", 1, "matmul", p_sc[0][:, :], cols[:, acc_final["w1"]:acc_final["w1"] + 1],
          cols[:, 5:6], start=True, stop=True)
        tok_tot = E("tensor", 1, "matmul", p_sc[1][:, :],
                    cols[:, acc_final["w2"]:acc_final["w2"] + 1], cols[:, 5:6],
                    start=True, stop=True)
        W("vector", "tensor", tok_tot)
        E("vector", 1, "tensor_scalar", cols[0:1, 6:7], p_sc[0][:, :],
          1.0 / (D * DH), 1e-5, ALU.mult, ALU.max)
        tok_invs = E("vector", 1, "tensor_scalar", cols[0:1, 7:8], p_sc[1][:, :],
                     1.0 / (D * DH), 1e-5, ALU.mult, ALU.max)
        DR("vector")
        W("tensor", "vector", tok_invs)
        E("tensor", 1, "matmul", p_bc[0][:, :], ones_row[0:1, :], cols[0:1, 6:7],
          start=True, stop=True)
        tok_bc = E("tensor", 1, "matmul", p_bc[1][:, :], ones_row[0:1, :], cols[0:1, 7:8],
                   start=True, stop=True)
        W("vector", "tensor", tok_bc)
        E("vector", 1, "tensor_copy", cols[:, 48:49], p_bc[0][:, :])
        E("vector", 1, "tensor_copy", cols[:, 49:50], p_bc[1][:, :])
        E("vector", 1, "tensor_scalar", cols[:, 50:51], p_bc[0][:, :], 0.5, None, ALU.mult)
        tok_thr = E("vector", 1, "tensor_scalar", cols[:, 51:52], p_bc[1][:, :],
                    0.5, None, ALU.mult)
        DR("vector")
        invs1_col, invs2_col = cols[:, 48:49], cols[:, 49:50]
        thrcol = {"w1": cols[:, 50:51], "w2": cols[:, 51:52]}

        s_base = P.cnt["scalar"]
        v2_base = P.cnt["vector"]
        g_base = P.cnt["gpsimd"]
        sy_base = P.cnt["sync"]
        tok_qld = [sy_base + 16 * (i + 1) for i in range(nwq)]
        tok_qsc = [s_base + (i + 1) for i in range(nwq)]
        tok_qv = [v2_base + (i + 1) for i in range(nwq)]
        tok_qst = []
        acc = g_base
        for i, (w, src, _) in enumerate(wq_chunks):
            acc += 16 * (8 if w == "w1" else 1)
            tok_qst.append(acc)
        for i, (w, src, _) in enumerate(wq_chunks):
            if i >= 2:
                W("sync", "scalar", tok_qsc[i - 2])
            t_ = E("sync", 16, "dma_start", out=xh[:, i % 2, :], in_=src)
            assert t_ == tok_qld[i]
        for i, (w, src, _) in enumerate(wq_chunks):
            W("scalar", "sync", tok_qld[i])
            if i >= 2:
                W("scalar", "vector", tok_qv[i - 2])
            E("scalar", 0, "activation", qm[:, i % 2, :], xh[:, i % 2, :], AF.Abs)
            t_ = E("scalar", 1, "activation", qsg[:, i % 2, :], xh[:, i % 2, :], AF.Sign)
            assert t_ == tok_qsc[i]
        for i, (w, src, _) in enumerate(wq_chunks):
            W("vector", "scalar", tok_qsc[i])
            if i == 0:
                W("vector", "vector", tok_thr)
            if i >= 2:
                W("vector", "gpsimd", tok_qst[i - 2])
            E("vector", 0, "tensor_scalar", t1b[:, :], qm[:, i % 2, :], thrcol[w], None, ALU.is_gt)
            t_ = E("vector", 1, "tensor_tensor", xqt[:, i % 2, :],
                   t1b[:, :], qsg[:, i % 2, :], ALU.mult)
            assert t_ == tok_qv[i]
        for i, (w, src, key) in enumerate(wq_chunks):
            W("gpsimd", "vector", tok_qv[i])
            if w == "w1":
                j, q = key
                for t in range(8):
                    E("gpsimd", 16, "dma_start",
                      out=w1q_d[8 * q + t, :, j, :], in_=xqt[:, i % 2, 128 * t:128 * (t + 1)])
                assert P.cnt["gpsimd"] == tok_qst[i]
            else:
                (j,) = key
                t_ = E("gpsimd", 16, "dma_start",
                       out=w2q_d[128 * j:128 * (j + 1), :], in_=xqt[:, i % 2, :])
                assert t_ == tok_qst[i]

        P.barrier()
        stackQ.close()

        def emit_l0_ffn(pfx):
            sub = contextlib.ExitStack()
            with sub:
                SS = sub.enter_context

                def ssb(name, shape, dt=F32):
                    return SS(nc.sbuf_tensor(pfx + name, shape, dt))

                xqT = ssb("xqT", [128, 8 * 2052], BF16)
                sful = ssb("sful", [128, 32, FT], BF16)
                w1c = ssb("w1c", [128, 2, 8, 128], BF16)
                w2c = ssb("w2c", [128, 2, D], BF16)
                hch = ssb("hch", [128, 2, FT + 2], BF16)
                ybuf = ssb("ybuf", [128, 2, FT])
                absb = ssb("absb", [128, FT])
                tqb = ssb("tqb", [128, FT])
                m1 = ssb("m1", [128, 2, FT])
                m1bf = ssb("m1bf", [128, FT], BF16)
                s2B = ssb("s2B", [128, FT])
                s2tm = ssb("s2tm", [128, 16])
                rec16 = ssb("rec16", [128, 16])
                s2rows = ssb("s2rows", [2, 128])
                sqc = ssb("sqc", [128, 2, FT], BF16)
                hh_halo = ssb("hh_halo", [128, 32, 16], BF16)

                stackA = contextlib.ExitStack()
                p_tr = [stackA.enter_context(nc.psum_tensor(pfx + f"ptr{i}", [128, 128], BF16))
                        for i in range(2)]
                p_h2 = [stackA.enter_context(nc.psum_tensor(pfx + f"ph2{i}", [128, 2], F32))
                        for i in range(2)]

                def xqa(d, a, b, step=None):
                    if step is None:
                        return xqT[:, 2052 * d + a:2052 * d + b]
                    return xqT[:, 2052 * d + a:2052 * d + b:step]


                tok_xq = [0] * (NT + 1)
                tok_ev = []
                tr_last = [0] * (NT + 1)
                for i in range(NT + 1):
                    ht = (i == NT)
                    Pn = 2 if ht else 128
                    src = halo_rows if ht else xres[:, slot[i], :]
                    ccl = hcols if ht else cols
                    scl = hcols if ht else scol
                    cof = 0 if ht else 52
                    sof = 8 if ht else 0
                    xhb = xh[0:Pn, i % 2, :]
                    xqb = xqt[0:Pn, i % 2, :]
                    t1 = t1b[0:Pn, :]
                    dump = sqdump[0:Pn, :]

                    def CL(j):
                        return ccl[0:Pn, cof + j:cof + j + 1]

                    def SL(j):
                        return scl[0:Pn, sof + j:sof + j + 1]

                    if i >= 2:
                        W("scalar", "vector", tok_xq[i - 2])
                    if i == 0:
                        W("scalar", "sync", tok_x)
                    tok_ms = E("scalar", 1, "activation", dump, src, AF.Identity,
                               accum_out=SL(0))
                    W("vector", "scalar", tok_ms)
                    E("vector", 1, "tensor_scalar", CL(0), SL(0), 1.0 / D, None, ALU.mult)
                    DR("vector")
                    tok_nm = E("vector", 1, "tensor_scalar", CL(1), CL(0), -1.0, None, ALU.mult)
                    W("scalar", "vector", tok_nm)
                    tok_ss = E("scalar", 1, "activation", dump, src, AF.Square,
                               bias=CL(1), scale=1.0, accum_out=SL(1))
                    W("vector", "scalar", tok_ss)
                    tok_ve = E("vector", 1, "tensor_scalar", CL(2), SL(1), 1.0 / D, 1e-5,
                               ALU.mult, ALU.add)
                    W("scalar", "vector", tok_ve)
                    E("scalar", 1, "activation", SL(2), CL(2), AF.Ln)
                    DR("scalar")
                    tok_rs = E("scalar", 1, "activation", SL(3), SL(2), AF.Exp, scale=-0.5)
                    W("vector", "scalar", tok_rs)
                    E("vector", 1, "tensor_copy", CL(3), SL(3))
                    DR("vector")
                    E("vector", 0, "tensor_scalar", xhb, src, CL(0), CL(3),
                      ALU.subtract, ALU.mult)
                    E("vector", 1, "tensor_reduce", CL(4), xhb, AX.X, ALU.max,
                      apply_absolute_value=True)
                    DR("vector")
                    tok_am = E("vector", 1, "tensor_scalar", CL(5), CL(4), 1e-5, None, ALU.max)
                    DR("vector")
                    W("scalar", "vector", tok_am)
                    E("scalar", 1, "activation", SL(4), CL(5), AF.Ln)
                    DR("scalar")
                    tok_s1 = E("scalar", 1, "activation", SL(5), SL(4), AF.Exp, scale=-1.0)
                    W("vector", "scalar", tok_s1)
                    E("vector", 1, "tensor_scalar", CL(6), SL(5), 127.0, None, ALU.mult)
                    E("vector", 1, "tensor_scalar", CL(7), CL(5), invs1_col[0:Pn, 0:1],
                      1.0 / 127.0, ALU.mult, ALU.mult)
                    DR("vector")
                    E("vector", 0, "tensor_scalar", t1, xhb, CL(6), MAGIC, ALU.mult, ALU.add)
                    if i >= 2:
                        W("vector", "tensor", tr_last[i - 2])
                    tok_xq[i] = E("vector", 1, "tensor_scalar", xqb, t1, MAGIC, CL(7),
                                  ALU.subtract, ALU.mult)
                    for d8 in range(8):
                        k8 = len(tok_ev)
                        W("tensor", "vector", tok_xq[i])
                        if k8 >= 2:
                            W("tensor", "scalar", tok_ev[k8 - 2])
                        if not ht:
                            tok_t = E("tensor", 1, "transpose", p_tr[k8 % 2][:, :],
                                      xqb[:, 128 * d8:128 * (d8 + 1)], c_idb[:, :])
                            W("scalar", "tensor", tok_t)
                            tok_ev.append(E("scalar", 1, "copy",
                                            xqa(d8, 1 + 128 * i, 129 + 128 * i),
                                            p_tr[k8 % 2][:, :]))
                            tr_last[i] = P.cnt["tensor"]
                        else:
                            tok_t = E("tensor", 1, "matmul", p_h2[k8 % 2][:, :],
                                      xqb[0:2, 128 * d8:128 * (d8 + 1)], c_idb[0:2, 0:2],
                                      start=True, stop=True)
                            W("scalar", "tensor", tok_t)
                            E("scalar", 0, "copy", xqa(d8, 0, 1), p_h2[k8 % 2][:, 0:1])
                            tok_ev.append(E("scalar", 1, "copy", xqa(d8, 2049, 2050),
                                            p_h2[k8 % 2][:, 1:2]))
                        tr_last[i] = P.cnt["tensor"]
                tok_xqT = tok_ev[-1]
                stackA.close()
                p_mm = [SS(nc.psum_tensor(pfx + f"pmm{i}", [128, FT], F32)) for i in range(2)]
                p_y = [SS(nc.psum_tensor(pfx + f"py{i}", [128, 512], F32)) for i in range(4)]
                stackH = contextlib.ExitStack()
                p_hh = stackH.enter_context(nc.psum_tensor(pfx + "phh", [128, 16], F32))

                xq_hal = ssb("xq_hal", [128, 8, 16], BF16)
                W("vector", "scalar", tok_xqT)
                for d8 in range(8):
                    E("vector", 0, "tensor_copy", xq_hal[:, d8, 0:8],
                      xqa(d8, 0, 1793, 256))
                    tok_halg = E("vector", 1, "tensor_copy", xq_hal[:, d8, 8:16],
                                 xqa(d8, 257, 2050, 256))

                w1_tok, w2_tok = {}, {}
                w1_use, w2_use = {}, {}
                conv_done, silu_done, m1_done, evac_done = {}, {}, {}, {}
                halo_done = {}
                yv_done = {}

                def load_w1(i, c):
                    k = i * 32 + c
                    t_use = w1_use.get(k - 2, 0)
                    if t_use:
                        W("sync", "tensor", t_use)
                    w1_tok[k] = E("sync", 16, "dma_start",
                                  out=w1c[:, k % 2, :, :], in_=w1q_d[c, :, :, :])

                def load_w2(i, c):
                    k = i * 32 + c
                    t_use = w2_use.get(k - 2, 0)
                    if t_use:
                        W("sync", "tensor", t_use)
                    w2_tok[k] = E("sync", 16, "dma_start",
                                  out=w2c[:, k % 2, :], in_=w2q_d[128 * c:128 * (c + 1), :])

                def mm1_tile(i):
                    for c in range(32):
                        k = i * 32 + c
                        load_w1(i, c)
                        par = k % 2
                        pm = p_mm[c % 2]
                        W("tensor", "sync", w1_tok[k])
                        if i == 0 and c == 0:
                            W("tensor", "scalar", tok_xqT)
                            W("tensor", "vector", tok_halg)
                        pe = evac_done.get((i, c - 2), 0)
                        if pe:
                            W("tensor", "scalar", pe)
                        for d8 in range(8):
                            E("tensor", 0 if d8 < 7 else 1, "matmul", pm[:, :],
                              w1c[:, par, d8, :], xqa(d8, 1 + FT * i, 1 + FT * i + FT),
                              start=(d8 == 0), stop=(d8 == 7))
                        tok_mm = P.cnt["tensor"]
                        if i == 0:
                            hd = halo_done.get(c - 1, 0)
                            if hd:
                                W("tensor", "scalar", hd)
                            for d8 in range(8):
                                E("tensor", 0 if d8 < 7 else 1, "matmul", p_hh[:, :],
                                  w1c[:, par, d8, :], xq_hal[:, d8, :],
                                  start=(d8 == 0), stop=(d8 == 7))
                        w1_use[k] = P.cnt["tensor"]
                        tok_hal = P.cnt["tensor"]
                        W("scalar", "tensor", tok_mm)
                        cd = conv_done.get((i, c - 2), 0)
                        if cd:
                            W("scalar", "vector", cd)
                        E("scalar", 0, "copy", hch[:, c % 2, 1:FT + 1], pm[:, :])
                        if i == 0:
                            W("scalar", "tensor", tok_hal)
                            halo_done[c] = E("scalar", 1, "copy", hh_halo[:, c, :], p_hh[:, :])
                            DR("scalar")
                        E("scalar", 0, "copy", hch[:, c % 2, 0:1], hh_halo[:, c, i:i + 1])
                        tok_hcopy = E("scalar", 1, "copy", hch[:, c % 2, FT + 1:FT + 2],
                                      hh_halo[:, c, 8 + i:9 + i])
                        evac_done[(i, c)] = tok_hcopy
                        DR("scalar")
                        tok_y1 = E("scalar", 1, "activation", ybuf[:, 0, :],
                                   hch[:, c % 2, 1:FT + 1], AF.Identity,
                                   bias=cbt[:, c:c + 1], scale=cwt[:, c, 1:2])
                        W("vector", "scalar", tok_y1)
                        E("vector", 0, "scalar_tensor_tensor", ybuf[:, 1, :],
                          hch[:, c % 2, 0:FT], cwt[:, c, 0:1], ybuf[:, 0, :],
                          ALU.mult, ALU.add)
                        tok_y3 = E("vector", 1, "scalar_tensor_tensor", ybuf[:, 0, :],
                                   hch[:, c % 2, 2:FT + 2], cwt[:, c, 2:3], ybuf[:, 1, :],
                                   ALU.mult, ALU.add)
                        conv_done[(i, c)] = tok_y3
                        W("scalar", "vector", tok_y3)
                        sqp = sq_done.get((i - 1, c), 0)
                        if sqp:
                            W("scalar", "vector", sqp)
                        tok_sil = E("scalar", 1, "activation", sful[:, c, :],
                                    ybuf[:, 0, :], AF.Silu)
                        silu_done[(i, c)] = tok_sil
                        tok_abs = E("scalar", 1, "activation", absb[:, :],
                                    sful[:, c, :], AF.Abs)
                        W("vector", "scalar", tok_abs)
                        if c == 0 and (i, "m1b") in m1_done:
                            W("vector", "vector", m1_done[(i, "m1b")])
                        if c == 0:
                            tms = E("vector", 1, "memset", m1[:, 1, :], 0.0)
                            W("vector", "vector", tms)
                        m1_done[(i, c)] = E("vector", 1, "tensor_tensor", m1[:, c % 2, :],
                                            m1[:, (c + 1) % 2, :], absb[:, :], ALU.max)

                prev_mm2 = {"d2": 0, "s2rw": 0, "s2B": 0}

                def mm2_tile(i):
                    W("vector", "vector", m1_done[(i, 31)])
                    tok_m1b = E("vector", 1, "tensor_copy", m1bf[:, :], m1[:, 1, :])
                    m1_done[(i + 1, "m1b")] = tok_m1b
                    ts_stack = contextlib.ExitStack()
                    p_trx = ts_stack.enter_context(
                        nc.psum_tensor(pfx + f"ptx{i}", [128, 128], BF16))
                    for t2 in range(2):
                        W("tensor", "vector", tok_m1b)
                        if prev_mm2["d2"]:
                            W("tensor", "vector", prev_mm2["d2"])
                        if t2 == 1:
                            W("tensor", "vector", P.cnt["vector"])
                        tok_tr2 = E("tensor", 1, "transpose", p_trx[:, :],
                                    m1bf[:, 128 * t2:128 * (t2 + 1)], c_idb[:, :])
                        W("vector", "tensor", tok_tr2)
                        E("vector", 1, "tensor_reduce", cols[:, 60 + t2:61 + t2],
                          p_trx[:, :], AX.X, ALU.max)
                        DR("vector")
                    E("vector", 1, "tensor_scalar", rec16[:, 0:2], cols[:, 60:62],
                      1e-5, None, ALU.max)
                    tms2 = E("vector", 1, "memset", rec16[:, 2:16], 1.0)
                    W("vector", "vector", tms2)
                    E("vector", 1, "reciprocal", s2tm[:, 0:16], rec16[:, 0:16])
                    DR("vector")
                    E("vector", 1, "tensor_scalar", s2tm[:, 2:4], s2tm[:, 0:2],
                      127.0, None, ALU.mult)
                    tok_d2 = E("vector", 1, "tensor_scalar", cols[:, 62:64], rec16[:, 0:2],
                               invs2_col, 0.5 / 127.0, ALU.mult, ALU.mult)
                    DR("vector")
                    prev_mm2["d2"] = tok_d2
                    ts_stack.close()
                    ts2 = contextlib.ExitStack()
                    p_s2r = ts2.enter_context(nc.psum_tensor(pfx + f"ps2r{i}", [2, 128], F32))
                    tok_s2c = P.cnt["vector"]
                    W("tensor", "vector", tok_s2c)
                    if prev_mm2["s2rw"]:
                        W("tensor", "vector", prev_mm2["s2rw"])
                    tok_s2r = E("tensor", 1, "matmul", p_s2r[:, :], s2tm[:, 2:4],
                                c_idf[:, :], start=True, stop=True)
                    W("vector", "tensor", tok_s2r)
                    tok_s2rw = E("vector", 1, "tensor_copy", s2rows[:, :], p_s2r[0:2, :])
                    prev_mm2["s2rw"] = tok_s2rw
                    ts2.close()
                    ts3 = contextlib.ExitStack()
                    p_s2x = ts3.enter_context(nc.psum_tensor(pfx + f"ps2x{i}", [128, 128], F32))
                    for t2 in range(2):
                        W("tensor", "vector", tok_s2rw)
                        if prev_mm2["s2B"]:
                            W("tensor", "scalar", prev_mm2["s2B"])
                        if t2 == 1:
                            W("tensor", "scalar", P.cnt["scalar"])
                        tok_s2b = E("tensor", 1, "matmul", p_s2x[:, :],
                                    c_sel2[0:2, 128 * t2:128 * (t2 + 1)], s2rows[:, :],
                                    start=True, stop=True)
                        W("scalar", "tensor", tok_s2b)
                        tok_s2B = E("scalar", 1, "copy", s2B[:, 128 * t2:128 * (t2 + 1)],
                                    p_s2x[:, :])
                    ts3.close()
                    prev_mm2["s2B"] = tok_s2B

                    pv = yv_done.get(i - 1, 0)
                    for c in range(32):
                        k = i * 32 + c
                        load_w2(i, c)
                        W("vector", "scalar", silu_done[(i, c)])
                        if c == 0:
                            W("vector", "scalar", tok_s2B)
                        pu = w2_use.get(k - 2, 0)
                        if pu:
                            W("vector", "tensor", pu)
                        E("vector", 0, "tensor_tensor", tqb[:, :], sful[:, c, :],
                          s2B[:, :], ALU.mult)
                        tok_sq = E("vector", 1, "tensor_scalar", sqc[:, c % 2, :],
                                   tqb[:, :], MAGIC, MAGIC, ALU.add, ALU.subtract)
                        sq_done[(i, c)] = tok_sq
                        W("tensor", "vector", tok_sq)
                        W("tensor", "sync", w2_tok[k])
                        if c == 0 and pv:
                            W("tensor", "vector", pv)
                        for tsub in range(2):
                            for oh in range(2):
                                E("tensor", 0 if (tsub, oh) != (1, 1) else 1, "matmul",
                                  p_y[2 * tsub + oh][:, :],
                                  sqc[:, c % 2, 128 * tsub:128 * (tsub + 1)],
                                  w2c[:, k % 2, 512 * oh:512 * (oh + 1)],
                                  start=(c == 0), stop=(c == 31))
                        w2_use[k] = P.cnt["tensor"]
                    tok_y = P.cnt["tensor"]
                    W("vector", "tensor", tok_y)
                    for tsub in range(2):
                        r = 2 * i + tsub
                        d2s = cols[:, 62 + tsub:63 + tsub]
                        E("vector", 0, "scalar_tensor_tensor", xres[:, spare[0], 0:512],
                          p_y[2 * tsub][:, :], d2s, xres[:, slot[r], 0:512],
                          ALU.mult, ALU.add)
                        tok_res = E("vector", 1, "scalar_tensor_tensor",
                                    xres[:, spare[0], 512:1024], p_y[2 * tsub + 1][:, :],
                                    d2s, xres[:, slot[r], 512:1024], ALU.mult, ALU.add)
                        slot[r], spare[0] = spare[0], slot[r]
                    yv_done[i] = P.cnt["vector"]

                sq_done = {}
                mm1_tile(0)
                stackH.close()
                mm2_tile(0)
                for i in range(1, NFT):
                    mm1_tile(i)
                    mm2_tile(i)
                if DEBUG_DUMP:
                    P.barrier()
                    for ddbg in range(8):
                        E("gpsimd", 16, "dma_start", out=dbg_xqT[:, 2050 * ddbg:2050 * (ddbg + 1)],
                          in_=xqT[:, 2052 * ddbg:2052 * ddbg + 2050])
                    E("gpsimd", 16, "dma_start", out=dbg_sful[:, :], in_=sful[:, :, :])
                    E("gpsimd", 16, "dma_start", out=dbg_s2B[:, :], in_=s2B[:, :])
                    E("gpsimd", 16, "dma_start", out=dbg_cols[:, 0:10], in_=cols[:, 48:58])
                    E("gpsimd", 16, "dma_start", out=dbg_hh[:, :], in_=hh_halo[:, :, :])
                    for cdbg in range(32):
                        E("gpsimd", 16, "dma_start", out=dbg_w1q[cdbg, :, :, :],
                          in_=w1q_d[cdbg, :, :, :])
                    P.barrier()


        def emit_attn(pfx, kind, W_qkvo, extra):
            WqT_d, WkT_d, WvT_d, WoT_d = W_qkvo
            sub = contextlib.ExitStack()
            with sub:
                SS = sub.enter_context

                def ssb(name, shape, dt=F32):
                    return SS(nc.sbuf_tensor(pfx + name, shape, dt))

                wbufA = ssb("wbufA", [128, 8, D], BF16)
                wbufB = ssb("wbufB", [128, 8, D], BF16)
                wtmp = ssb("wtmp", [128, D])
                xhT_r = ssb("xhT_r", [128, 1, 8, 256], BF16)
                qstg = ssb("qstg", [128, 8, 256], BF16)
                kstg = ssb("kstg", [128, 2, D], BF16)
                khstg = ssb("khstg", [128, 2, D], BF16)
                if kind == "gla":
                    ksq = ssb("ksq", [128, 16, 64])
                    kss = ssb("kss", [128, 16])
                    kss2 = ssb("kss2", [128, 16])
                qr = ssb("qr", [128, 4, 8, 128], BF16)
                khr = ssb("khr", [128, 4, D], BF16)
                vr = ssb("vr", [128, 4, D], BF16)
                khT = ssb("khT", [128, 1, 8, 128], BF16)
                ShHi = ssb("ShHi", [128, D], BF16)
                crow = ssb("crow", [1, 16])
                Or = ssb("Or", [128, 2, D], BF16)
                Pt = ssb("Pt", [128, 2, 128], BF16)
                Sa = ssb("Sa", [64, D]); Sb = ssb("Sb", [64, D])
                Sh = ssb("Sh", [64, D], BF16)
                Shin = ssb("Shin", [64, D], BF16)
                fga = ssb("fga", [4, 16])
                coefs = ssb("coefs", [4, 16])
                coefr = ssb("coefr", [1, 4, 16])
                coefB = ssb("coefB", [64, 4, 16])
                if kind == "gla":
                    fstore = ssb("fstore", [128, 16, 16])
                    Fst = ssb("Fst", [128, 16, 16])
                zst = ssb("zst", [128, 16])
                zt1 = ssb("zt1", [128, 16])
                zt2 = ssb("zt2", [128, 16])
                EexpT = ssb("EexpT", [128, 16])
                EinvT = ssb("EinvT", [128, 16])
                if kind == "gla":
                    cBs = ssb("cBs", [128, 16])
                    Ftl = ssb("Ftl", [128, 16])
                    EcB = ssb("EcB", [128, 16])
                    EendB = ssb("EendB", [128, 16])
                    ElamB = ssb("ElamB", [128, 16])
                    lam_s = ssb("lam_s", [1, 16])
                    labs = ssb("labs", [1, 2, 16])
                dstmp = ssb("dstmp", [64, 64])
                EA = ssb("EA", [128, 16])
                eab = ssb("eab", [128, 16])
                if kind == "gla":
                    g1f = ssb("g1f", [64, 8, 256], BF16)
                    wf_s = ssb("wf_s", [128, 8, 16], BF16)
                    wg1_s = ssb("wg1_s", [128, 8, 64], BF16)
                    wg2_s = ssb("wg2_s", [64, D], BF16)
                    lnwc = ssb("lnwc", [128, 8])
                if kind == "dec":
                    gamr = ssb("gamr", [1, 16])
                    lngB = ssb("lngB", [128, 16])
                    posc = ssb("posc", [128, 2])

                Sg = kstg
                Kt = khstg
                gatet = kstg
                ogb = xh
                ohT = khT
                is_gla = kind == "gla"

                if is_gla:
                    for d8 in range(8):
                        E("gpsimd", 16, "dma_start", out=wf_s[:, d8, :],
                          in_=extra["WfT"][128 * d8:128 * (d8 + 1), :])
                        E("gpsimd", 16, "dma_start", out=wg1_s[:, d8, :],
                          in_=extra["Wg1T"][128 * d8:128 * (d8 + 1), :])
                        E("gpsimd", 16, "dma_start", out=lnwc[:, d8:d8 + 1],
                          in_=extra["lnw"][128 * d8:128 * (d8 + 1), :])
                    E("gpsimd", 16, "dma_start", out=wg2_s[:, :], in_=extra["Wg2T"][:, :])
                else:
                    bq_s = ssb("bq_s", [128, 8])
                    bk_s = ssb("bk_s", [1, D])
                    bv_s = ssb("bv_s", [1, D])
                    bo_s = ssb("bo_s", [1, D])
                    E("gpsimd", 16, "dma_start", out=gamr[0:1, :], in_=extra["gam_row"][:, :])
                    E("gpsimd", 16, "dma_start", out=posc[:, 0:1], in_=extra["poscol"][:, :])
                    for d8 in range(8):
                        E("gpsimd", 16, "dma_start", out=bq_s[:, d8:d8 + 1],
                          in_=extra["bqc"][128 * d8:128 * (d8 + 1), :])
                    E("gpsimd", 16, "dma_start", out=bk_s[0:1, :], in_=extra["bkr"][:, :])
                    E("gpsimd", 16, "dma_start", out=bv_s[0:1, :], in_=extra["bvr"][:, :])
                    E("gpsimd", 16, "dma_start", out=bo_s[0:1, :], in_=extra["bor"][:, :])
                tok_wres = P.cnt["gpsimd"]

                stA = contextlib.ExitStack()
                p_tr2 = [stA.enter_context(nc.psum_tensor(pfx + f"ptr{i}", [128, 128], BF16))
                         for i in range(2)]
                p_kv = [stA.enter_context(nc.psum_tensor(pfx + f"pkv{i}", [128, 512], F32))
                        for i in range(2)]
                p_q = [stA.enter_context(nc.psum_tensor(pfx + f"pq{i}", [128, 256], F32))
                       for i in range(2)]
                p_z = stA.enter_context(nc.psum_tensor(pfx + "pz", [128, 16], F32))
                p_g1 = stA.enter_context(nc.psum_tensor(pfx + "pg1", [64, 256], F32))

                tok_xh = [0] * (NT + 20)
                tok_tr = [0] * (NT + 20)
                tr_lastA = [0] * (NT + 20)
                tok_wldA = {}
                tok_wuseA = {}
                tok_qdma = {}
                tok_kvdma = {}

                def ln_tile_plain(i, src_ap, dep_v=0):
                    Pn = 128
                    xhb = xh[:, i % 2, :]
                    xqb = xqt[:, i % 2, :]
                    dump = sqdump[:, :]

                    def CL(j):
                        return cols[:, 52 + j:53 + j]

                    def SL(j):
                        return scol[:, j:j + 1]

                    if i >= 2:
                        W("scalar", "vector", tok_xh[i - 2])
                    if dep_v:
                        W("scalar", "vector", dep_v)
                    tok_ms = E("scalar", 1, "activation", dump, src_ap, AF.Identity,
                               accum_out=SL(0))
                    W("vector", "scalar", tok_ms)
                    E("vector", 1, "tensor_scalar", CL(0), SL(0), 1.0 / D, None, ALU.mult)
                    DR("vector")
                    tok_nm = E("vector", 1, "tensor_scalar", CL(1), CL(0), -1.0, None, ALU.mult)
                    W("scalar", "vector", tok_nm)
                    tok_ss = E("scalar", 1, "activation", dump, src_ap, AF.Square,
                               bias=CL(1), scale=1.0, accum_out=SL(1))
                    W("vector", "scalar", tok_ss)
                    tok_ve = E("vector", 1, "tensor_scalar", CL(2), SL(1), 1.0 / D, 1e-5,
                               ALU.mult, ALU.add)
                    W("scalar", "vector", tok_ve)
                    E("scalar", 1, "activation", SL(2), CL(2), AF.Ln)
                    DR("scalar")
                    tok_rs = E("scalar", 1, "activation", SL(3), SL(2), AF.Exp, scale=-0.5)
                    W("vector", "scalar", tok_rs)
                    E("vector", 1, "tensor_copy", CL(3), SL(3))
                    DR("vector")
                    if i >= 2:
                        W("vector", "tensor", tr_lastA[i - 2])
                    tok_xh[i] = E("vector", 1, "tensor_scalar", xqb, src_ap, CL(0), CL(3),
                                  ALU.subtract, ALU.mult)
                    return tok_xh[i]

                evA = []
                fuse_tok = {}
                g1ev = {}
                projend = {}
                kv_prev = {}
                fz_copy = {}
                for t8 in range(NFT):
                    par = t8 % 2
                    for half in range(2):
                        i = 2 * t8 + half
                        tokx = ln_tile_plain(i, xres[:, slot[i], :])
                        for d8 in range(8):
                            kk = len(evA)
                            W("tensor", "vector", tokx)
                            if kk >= 2:
                                W("tensor", "scalar", evA[kk - 2])
                            tok_t = E("tensor", 1, "transpose", p_tr2[kk % 2][:, :],
                                      xqt[:, i % 2, 128 * d8:128 * (d8 + 1)], c_idb[:, :])
                            W("scalar", "tensor", tok_t)
                            if half == 0 and d8 == 0 and t8 >= 1:
                                W("scalar", "tensor", projend[t8 - 1])
                            evA.append(E("scalar", 1, "copy",
                                         xhT_r[:, 0, d8, 128 * half:128 * (half + 1)],
                                         p_tr2[kk % 2][:, :]))
                        tr_lastA[i] = P.cnt["tensor"]
                    tok_xhT = evA[-1]

                    pu = tok_wuseA.get(("q", t8 - 2), 0)
                    if pu:
                        W("gpsimd", "tensor", pu)
                    for d8 in range(8):
                        tok_wldA[("q", t8)] = E("gpsimd", 16, "dma_start",
                                                out=wbufA[:, d8, :],
                                                in_=WqT_d[128 * d8:128 * (d8 + 1), :])
                    for oc in range(8):
                        W("tensor", "gpsimd", tok_wldA[("q", t8)])
                        W("tensor", "scalar", tok_xhT)
                        if oc >= 2:
                            W("tensor", "scalar", tok_qev[oc - 2])
                        elif t8 > 0:
                            W("tensor", "gpsimd", tok_qdma[t8 - 1])
                        if oc == 0:
                            tok_qev = [0] * 8
                        for d8 in range(8):
                            E("tensor", 0 if d8 < 7 else 1, "matmul", p_q[oc % 2][:, :],
                              wbufA[:, d8, 128 * oc:128 * (oc + 1)],
                              xhT_r[:, 0, d8, :], start=(d8 == 0), stop=(d8 == 7))
                        tok_mmq = P.cnt["tensor"]
                        W("scalar", "tensor", tok_mmq)
                        for half in range(2):
                            fn = AF.Silu if is_gla else AF.Identity
                            kwq = {} if is_gla else {"bias": bq_s[:, oc:oc + 1]}
                            tok_qev[oc] = E("scalar", 1, "activation",
                                            qstg[:, oc, 128 * half:128 * (half + 1)],
                                            p_q[oc % 2][:, 128 * half:128 * (half + 1)],
                                            fn, **kwq)
                    tok_wuseA[("q", t8)] = P.cnt["tensor"]
                    W("gpsimd", "scalar", tok_qev[7])
                    E("gpsimd", 16, "dma_start", out=q_sp[2 * t8, :, :, :],
                      in_=qstg[:, :, 0:128])
                    tok_qdma[t8] = E("gpsimd", 16, "dma_start", out=q_sp[2 * t8 + 1, :, :, :],
                                     in_=qstg[:, :, 128:256])

                    pu = tok_wuseA.get(("k", t8 - 2), 0)
                    if pu:
                        W("gpsimd", "tensor", pu)
                    for d8 in range(8):
                        tok_wldA[("k", t8)] = E("gpsimd", 16, "dma_start",
                                                out=wbufB[:, d8, :],
                                                in_=WkT_d[128 * d8:128 * (d8 + 1), :])
                    tok_kev = {}
                    for st2 in range(2):
                        for oh in range(2):
                            W("tensor", "gpsimd", tok_wldA[("k", t8)])
                            pidx = 2 * st2 + oh
                            if pidx >= 2:
                                W("tensor", "scalar", tok_kev[(pidx // 2 - 1, pidx % 2)])
                            elif t8 >= 1:
                                W("tensor", "scalar", kv_prev.get(("k", t8 - 1), 0))
                            pk = tok_kvdma.get(("k", t8 - 1), 0)
                            if pk and st2 == 0 and oh == 0:
                                W("tensor", "gpsimd", pk)
                            if not is_gla:
                                W("tensor", "vector", tok_ones)
                                W("tensor", "gpsimd", tok_wres)
                                E("tensor", 0, "matmul", p_kv[(2 * st2 + oh) % 2][:, :],
                                  ones_row[0:1, :], bk_s[0:1, 512 * oh:512 * (oh + 1)],
                                  start=True, stop=False)
                            for d8 in range(8):
                                E("tensor", 0 if d8 < 7 else 1, "matmul",
                                  p_kv[(2 * st2 + oh) % 2][:, :],
                                  xhT_r[:, 0, d8, 128 * st2:128 * (st2 + 1)],
                                  wbufB[:, d8, 512 * oh:512 * (oh + 1)],
                                  start=(d8 == 0 and is_gla), stop=(d8 == 7))
                            tok_mmk = P.cnt["tensor"]
                            W("scalar", "tensor", tok_mmk)
                            tok_kev[(st2, oh)] = E("scalar", 1, "activation",
                                                   kstg[:, st2, 512 * oh:512 * (oh + 1)],
                                                   p_kv[(2 * st2 + oh) % 2][:, :],
                                                   AF.Silu if is_gla else AF.Identity)
                    tok_wuseA[("k", t8)] = P.cnt["tensor"]
                    kv_prev[("k", t8)] = tok_kev[(1, 1)]
                    for st2 in range(2):
                        ch = 2 * t8 + st2
                        if is_gla:
                            W("scalar", "scalar", tok_kev[(st2, 1)])
                            W("scalar", "vector", fuse_tok.get(2 * t8 + st2 - 2, 0))
                            tok_sq = E("scalar", 1, "activation", ksq[:, :, :],
                                       kstg[:, st2, :], AF.Square)
                            W("vector", "scalar", tok_sq)
                            E("vector", 1, "tensor_reduce", kss[:, :], ksq[:, :, :],
                              AX.X, ALU.add)
                            DR("vector")
                            tok_kc = E("vector", 1, "tensor_scalar", kss2[:, :], kss[:, :],
                                       1e-24, None, ALU.max)
                            W("scalar", "vector", tok_kc)
                            E("scalar", 1, "activation", zt1[:, :], kss2[:, :], AF.Ln)
                            DR("scalar")
                            tok_sqr = E("scalar", 1, "activation", zt2[:, :], zt1[:, :],
                                        AF.Exp, scale=0.5)
                            W("vector", "scalar", tok_sqr)
                            E("vector", 1, "tensor_scalar", kss[:, :], zt2[:, :],
                              1e-12, None, ALU.max)
                            DR("vector")
                            E("vector", 1, "reciprocal", kss2[:, :], kss[:, :])
                            DR("vector")
                            for hh in range(16):
                                E("vector", 0 if hh < 15 else 1, "tensor_scalar",
                                  khstg[:, st2, 64 * hh:64 * (hh + 1)],
                                  kstg[:, st2, 64 * hh:64 * (hh + 1)],
                                  kss2[:, hh:hh + 1], None, ALU.mult)
                            tok_kh = P.cnt["vector"]
                            W("gpsimd", "vector", tok_kh)
                            tok_kvdma[("k", t8)] = E("gpsimd", 16, "dma_start",
                                                     out=k_sp[ch, :, :], in_=khstg[:, st2, :])
                        else:
                            W("gpsimd", "scalar", tok_kev[(st2, 1)])
                            tok_kvdma[("k", t8)] = E("gpsimd", 16, "dma_start",
                                                     out=k_sp[ch, :, :], in_=kstg[:, st2, :])

                    pu = tok_wuseA.get(("q", t8), 0)
                    W("gpsimd", "tensor", pu)
                    for d8 in range(8):
                        tok_wldA[("v", t8)] = E("gpsimd", 16, "dma_start",
                                                out=wbufA[:, d8, :],
                                                in_=WvT_d[128 * d8:128 * (d8 + 1), :])
                    tok_vev = {}
                    for st2 in range(2):
                        for oh in range(2):
                            W("tensor", "gpsimd", tok_wldA[("v", t8)])
                            pidx = 2 * st2 + oh
                            if pidx >= 2:
                                W("tensor", "scalar", tok_vev[(pidx // 2 - 1, pidx % 2)])
                            else:
                                W("tensor", "scalar", tok_kev[(1, pidx)])
                            pk = tok_kvdma.get(("v", t8 - 1), 0)
                            if pk and st2 == 0 and oh == 0:
                                W("tensor", "gpsimd", pk)
                            if not is_gla:
                                W("tensor", "gpsimd", tok_wres)
                                E("tensor", 0, "matmul", p_kv[(2 * st2 + oh) % 2][:, :],
                                  ones_row[0:1, :], bv_s[0:1, 512 * oh:512 * (oh + 1)],
                                  start=True, stop=False)
                            for d8 in range(8):
                                E("tensor", 0 if d8 < 7 else 1, "matmul",
                                  p_kv[(2 * st2 + oh) % 2][:, :],
                                  xhT_r[:, 0, d8, 128 * st2:128 * (st2 + 1)],
                                  wbufA[:, d8, 512 * oh:512 * (oh + 1)],
                                  start=(d8 == 0 and is_gla), stop=(d8 == 7))
                            tok_mmv = P.cnt["tensor"]
                            W("scalar", "tensor", tok_mmv)
                            W("scalar", "gpsimd", tok_kvdma[("k", t8)])
                            tok_vev[(st2, oh)] = E("scalar", 1, "activation",
                                                   kstg[:, st2, 512 * oh:512 * (oh + 1)]
                                                   if False else khstg[:, st2, 512 * oh:512 * (oh + 1)],
                                                   p_kv[(2 * st2 + oh) % 2][:, :],
                                                   AF.Silu if is_gla else AF.Identity)
                        ch = 2 * t8 + st2
                        W("gpsimd", "scalar", tok_vev[(st2, 1)])
                        tok_kvdma[("v", t8)] = E("gpsimd", 16, "dma_start",
                                                 out=v_sp[ch, :, :], in_=khstg[:, st2, :])
                    tok_wuseA[("v", t8)] = P.cnt["tensor"]

                    if is_gla:
                        for st2 in range(2):
                            ch = 2 * t8 + st2
                            W("tensor", "gpsimd", tok_wres)
                            if ch >= 1:
                                W("tensor", "scalar", fz_copy.get(ch - 1, 0))
                            for d8 in range(8):
                                E("tensor", 0 if d8 < 7 else 1, "matmul", p_z[:, :],
                                  xhT_r[:, 0, d8, 128 * st2:128 * (st2 + 1)],
                                  wf_s[:, d8, :], start=(d8 == 0), stop=(d8 == 7))
                            tok_zmm = P.cnt["tensor"]
                            W("scalar", "tensor", tok_zmm)
                            if ch >= 1:
                                W("scalar", "vector", fuse_tok.get(ch - 1, 0))
                            fz_copy[ch] = E("scalar", 1, "copy", zst[:, :], p_z[:, :])
                            DR("scalar")
                            E("scalar", 1, "activation", zt1[:, :], zst[:, :], AF.Abs)
                            DR("scalar")
                            E("scalar", 1, "activation", zt2[:, :], zt1[:, :], AF.Exp,
                              scale=-1.0)
                            DR("scalar")
                            tok_lg = E("scalar", 1, "activation", zt1[:, :], zt2[:, :],
                                       AF.Ln, bias=1.0)
                            W("vector", "scalar", tok_lg)
                            fuse_tok[ch] = E("vector", 1, "scalar_tensor_tensor",
                                             fstore[:, ch, :], zst[:, :], 0.0,
                                             zt1[:, :], ALU.min, ALU.subtract)
                        W("tensor", "gpsimd", tok_wres)
                        if t8 >= 1:
                            W("tensor", "scalar", g1ev.get(t8 - 1, 0))
                        for d8 in range(8):
                            E("tensor", 0 if d8 < 7 else 1, "matmul", p_g1[:, :],
                              wg1_s[:, d8, :], xhT_r[:, 0, d8, :],
                              start=(d8 == 0), stop=(d8 == 7))
                        tok_g1 = P.cnt["tensor"]
                        W("scalar", "tensor", tok_g1)
                        g1ev[t8] = E("scalar", 1, "copy", g1f[:, t8, :], p_g1[:, :])
                    projend[t8] = P.cnt["tensor"]

                tok_phaseA_t = P.cnt["tensor"]
                tok_phaseA_g = P.cnt["gpsimd"]
                P.barrier()
                stA.close()

                stB = contextlib.ExitStack()
                pSC2 = [stB.enter_context(nc.psum_tensor(pfx + f"pSC{i}", [128, 128], F32))
                        for i in range(2)]
                pOV2 = [stB.enter_context(nc.psum_tensor(pfx + f"pOV{i}", [128, 64], F32))
                        for i in range(2)]
                pDS = stB.enter_context(nc.psum_tensor(pfx + "pDS", [64, 64], F32))
                pF2 = stB.enter_context(nc.psum_tensor(pfx + "pF2", [128, 16], F32))
                pB1 = stB.enter_context(nc.psum_tensor(pfx + "pB1", [1, 16], F32))
                pTRb = stB.enter_context(nc.psum_tensor(pfx + "pTRb", [128, 128], BF16))

                tmsS = E("vector", 1, "memset", Sa[:, :], 0.0)
                W("vector", "vector", tmsS)
                if not is_gla:
                    W("scalar", "gpsimd", tok_wres)
                    E("scalar", 1, "activation", zt1[0:1, :], gamr[0:1, :], AF.Ln)
                    DR("scalar")
                    tok_lgm = P.cnt["scalar"]
                    W("tensor", "scalar", tok_lgm)
                    W("tensor", "vector", tok_ones)
                    tok_lgb = E("tensor", 1, "matmul", pF2[:, :], ones_row[0:1, :],
                                zt1[0:1, :], start=True, stop=True)
                    W("vector", "tensor", tok_lgb)
                    E("vector", 1, "tensor_copy", lngB[:, :], pF2[:, :])
                    DR("vector")
                if is_gla:
                    tmsL = E("vector", 1, "memset", labs[0:1, 0, :], 0.0)
                    W("vector", "vector", tmsL)
                    tok_lab = tmsL
                else:
                    tok_lab = 0

                chunk_done = {}
                odma = {}
                fcopy_tok = {}
                khT_ev = {}
                shhi_use = {}
                prev_E3 = 0
                prev_Fv = 0
                ring_ld = {}
                vtok_P = 0
                vtok_O = 0
                vtok_S = 0
                for ch in range(DBG_NCH):
                    t8 = ch // 2
                    W("sync", "gpsimd", tok_qdma.get(t8, 0))
                    W("sync", "gpsimd", tok_phaseA_g)
                    cdp = chunk_done.get(ch - 4, 0)
                    if cdp:
                        W("sync", "tensor", cdp)
                        W("sync", "vector", chunk_done.get((ch - 4, "v"), 0))
                    E("sync", 16, "dma_start", out=qr[:, ch % 4, :, :], in_=q_sp[ch, :, :, :])
                    E("sync", 16, "dma_start", out=khr[:, ch % 4, :], in_=k_sp[ch, :, :])
                    ring_ld[ch] = E("sync", 16, "dma_start", out=vr[:, ch % 4, :],
                                    in_=v_sp[ch, :, :])

                    if is_gla:
                        W("tensor", "vector", fuse_tok[ch])
                        if ch >= 1:
                            W("tensor", "scalar", fcopy_tok[ch - 1])
                            W("tensor", "scalar", prev_E3)
                            W("tensor", "vector", prev_Fv)
                        tok_F = E("tensor", 1, "matmul", pF2[:, :], c_trf[:, :],
                                  fstore[:, ch, :], start=True, stop=True)
                        W("tensor", "vector", tok_ones)
                        tok_lam = E("tensor", 1, "matmul", pB1[:, :], cols[:, 5:6],
                                    fstore[:, ch, :], start=True, stop=True)
                        W("scalar", "tensor", tok_F)
                        fcopy_tok[ch] = E("scalar", 1, "copy", Fst[:, ch, :], pF2[:, :])
                        W("gpsimd", "scalar", fcopy_tok[ch])
                        tok_crow = E("gpsimd", 16, "dma_start", out=crow[0:1, :],
                                     in_=Fst[64:65, ch, :])
                        W("tensor", "gpsimd", tok_crow)
                        W("tensor", "vector", tok_ones)
                        tok_cB = E("tensor", 1, "matmul", pF2[:, :], ones_row[0:1, :],
                                   crow[0:1, :], start=True, stop=True)
                        W("vector", "tensor", tok_cB)
                        E("vector", 1, "tensor_copy", cBs[:, :], pF2[:, :])
                        W("vector", "tensor", tok_lam)
                        E("vector", 1, "tensor_copy", lam_s[0:1, :], pB1[0:1, :])
                        DR("vector")
                        E("vector", 1, "tensor_tensor", Ftl[:, :], Fst[:, ch, :],
                          cBs[:, :], ALU.subtract)
                        E("vector", 1, "tensor_tensor", fga[0:1, :], lam_s[0:1, :],
                          cBs[0:1, :], ALU.subtract)
                        E("vector", 1, "tensor_copy", t1b[0:1, 16 * ch:16 * (ch + 1)],
                          labs[0:1, ch % 2, :])
                        DR("vector")
                        tok_lab = E("vector", 1, "tensor_tensor", labs[0:1, (ch + 1) % 2, :],
                                    labs[0:1, ch % 2, :], lam_s[0:1, :], ALU.add)
                        tok_Ft = P.cnt["vector"]
                        W("scalar", "vector", tok_Ft)
                        if ch >= 1:
                            W("scalar", "vector", chunk_done[(ch - 1, "v")])
                        E("scalar", 1, "activation", EexpT[:, :], Ftl[:, :], AF.Exp)
                        E("scalar", 1, "activation", EinvT[:, :], Ftl[:, :], AF.Exp, scale=-1.0)
                        tok_E1 = E("scalar", 1, "activation", EcB[:, :], cBs[:, :], AF.Exp)
                        W("tensor", "vector", tok_Ft)
                        if True:
                            tok_eb = E("tensor", 1, "matmul", pF2[:, :], ones_row[0:1, :],
                                       fga[0:1, :], start=True, stop=True)
                        W("scalar", "tensor", tok_eb)
                        tok_E2 = E("scalar", 1, "activation", EendB[:, :], pF2[:, :], AF.Exp)
                        W("tensor", "scalar", tok_E2)
                        tok_lb = E("tensor", 1, "matmul", pF2[:, :], ones_row[0:1, :],
                                   lam_s[0:1, :], start=True, stop=True)
                        W("scalar", "tensor", tok_lb)
                        tok_E3 = E("scalar", 1, "activation", ElamB[:, :], pF2[:, :], AF.Exp)
                        prev_E3 = tok_E3
                        prev_Fv = tok_Ft
                        tok_E = tok_E3
                    else:
                        E("vector", 1, "tensor_scalar", posc[:, 1:2], posc[:, 0:1],
                          float(128 * ch), None, ALU.add)
                        DR("vector")
                        tok_g16 = E("vector", 1, "tensor_scalar", zt2[:, :], lngB[:, :],
                                    posc[:, 1:2], None, ALU.mult)
                        if ch >= 1:
                            W("scalar", "vector", chunk_done[(ch - 1, "v")])
                        W("scalar", "vector", tok_g16)
                        tok_E = E("scalar", 1, "activation", EexpT[:, :], zt2[:, :], AF.Exp)

                    Sprev = Sa if ch % 2 == 0 else Sb
                    Snew = Sb if ch % 2 == 0 else Sa
                    W("vector", "scalar", tok_E)
                    if is_gla:
                        for hh in range(16):
                            E("vector", 0 if hh < 15 else 1, "tensor_scalar",
                              Sh[:, 64 * hh:64 * (hh + 1)], Sprev[:, 64 * hh:64 * (hh + 1)],
                              EcB[0:64, hh:hh + 1], None, ALU.mult)
                        tok_Sh = P.cnt["vector"]
                    else:
                        tok_Sh = E("vector", 1, "tensor_copy", Sh[:, :], Sprev[:, :])
                    DR("vector")
                    W("gpsimd", "vector", tok_Sh)
                    if ch >= 1:
                        W("gpsimd", "tensor", shhi_use.get(ch - 1, 0))
                    tok_shhi = E("gpsimd", 16, "dma_start", out=ShHi[64:128, :],
                                 in_=Sh[0:64, :])

                    W("tensor", "sync", ring_ld[ch])
                    for d8 in range(8):
                        prev_ev = khT_ev.get((ch, d8 - 1), 0) if d8 > 0 else \
                            khT_ev.get((ch - 1, 7), 0)
                        if prev_ev:
                            W("tensor", "scalar", prev_ev)
                        tok_tk = E("tensor", 1, "transpose", pTRb[:, :],
                                   khr[:, ch % 4, 128 * d8:128 * (d8 + 1)], c_idb[:, :])
                        W("scalar", "tensor", tok_tk)
                        khT_ev[(ch, d8)] = E("scalar", 1, "copy", khT[:, 0, d8, :],
                                             pTRb[:, :])

                    vtok_P2 = {}
                    vtok_O2 = {}
                    for hh in range(16):
                        po = 64 * (hh % 2)
                        blkh = hh // 2
                        pSC = pSC2[hh % 2]
                        pOV = pOV2[hh % 2]
                        W("tensor", "scalar", khT_ev[(ch, blkh)])
                        pp2 = vtok_P2.get(hh - 2, vtok_P if hh < 2 else 0)
                        if pp2:
                            W("tensor", "vector", pp2)
                        tok_sc = E("tensor", 1, "matmul", pSC[:, :],
                                   khT[po:po + 64, 0, blkh, :],
                                   qr[po:po + 64, ch % 4, blkh, :],
                                   start=True, stop=True)
                        W("vector", "tensor", tok_sc)
                        if is_gla:
                            vtok_P = E("vector", 1, "scalar_tensor_tensor", Pt[:, hh % 2, :],
                                       pSC[:, :], EinvT[:, hh:hh + 1], c_trb[:, :],
                                       ALU.mult, ALU.mult)
                        else:
                            vtok_P = E("vector", 1, "tensor_tensor", Pt[:, hh % 2, :],
                                       pSC[:, :], c_trb[:, :], ALU.mult)
                        vtok_P2[hh] = vtok_P
                        W("tensor", "vector", vtok_P)
                        W("tensor", "vector", tok_Sh)
                        po2 = vtok_O2.get(hh - 2, vtok_O if hh < 2 else 0)
                        if po2:
                            W("tensor", "vector", po2)
                        E("tensor", 0, "matmul", pOV[:, :], Pt[:, hh % 2, :],
                          vr[:, ch % 4, 64 * hh:64 * (hh + 1)], start=True, stop=False)
                        if hh % 2 == 1:
                            W("tensor", "gpsimd", tok_shhi)
                            rhsS = ShHi[64:128, 64 * hh:64 * (hh + 1)]
                        else:
                            rhsS = Sh[0:64, 64 * hh:64 * (hh + 1)]
                        tok_ov = E("tensor", 1, "matmul", pOV[:, :],
                                   qr[po:po + 64, ch % 4, blkh, :],
                                   rhsS, start=False, stop=True)
                        W("vector", "tensor", tok_ov)
                        if hh == 0 and ch >= 2:
                            W("vector", "gpsimd", odma[ch - 2])
                        vtok_O = E("vector", 1, "tensor_scalar",
                                   Or[:, ch % 2, 64 * hh:64 * (hh + 1)], pOV[:, :],
                                   EexpT[:, hh:hh + 1], None, ALU.mult)
                        vtok_O2[hh] = vtok_O
                        if is_gla:
                            ktok = E("vector", 1, "tensor_scalar",
                                     Kt[:, ch % 2, 64 * hh:64 * (hh + 1)],
                                     khr[:, ch % 4, 64 * hh:64 * (hh + 1)],
                                     EinvT[:, hh:hh + 1], None, ALU.mult)
                            DR("vector")
                            W("tensor", "vector", ktok)
                            lhsK = Kt[:, ch % 2, 64 * hh:64 * (hh + 1)]
                        else:
                            lhsK = khr[:, ch % 4, 64 * hh:64 * (hh + 1)]
                        if vtok_S:
                            W("tensor", "vector", vtok_S)
                        tok_ds = E("tensor", 1, "matmul", pDS[:, :], lhsK,
                                   vr[:, ch % 4, 64 * hh:64 * (hh + 1)],
                                   start=True, stop=True)
                        W("vector", "tensor", tok_ds)
                        if is_gla:
                            E("vector", 1, "tensor_scalar", dstmp[:, :], pDS[:, :],
                              EendB[0:64, hh:hh + 1], None, ALU.mult)
                            DR("vector")
                            vtok_S = E("vector", 1, "scalar_tensor_tensor",
                                       Snew[:, 64 * hh:64 * (hh + 1)],
                                       Sprev[:, 64 * hh:64 * (hh + 1)],
                                       ElamB[0:64, hh:hh + 1], dstmp[:, :],
                                       ALU.mult, ALU.add)
                        else:
                            vtok_S = E("vector", 1, "tensor_tensor",
                                       Snew[:, 64 * hh:64 * (hh + 1)],
                                       Sprev[:, 64 * hh:64 * (hh + 1)], pDS[:, :], ALU.add)
                    W("gpsimd", "vector", vtok_O)
                    if ch >= 2:
                        pass
                    odma[ch] = E("gpsimd", 16, "dma_start", out=o_sp[ch, :, :],
                                 in_=Or[:, ch % 2, :])
                    chunk_done[ch] = P.cnt["tensor"]
                    chunk_done[(ch, "v")] = P.cnt["vector"]
                    shhi_use[ch] = P.cnt["tensor"]

                tok_S_final = vtok_S
                tok_B_t = P.cnt["tensor"]
                tok_B_v = P.cnt["vector"]
                stB.close()
                if DBG_NCH < 16:
                    P.barrier()
                    return

                stC = contextlib.ExitStack()
                pC1 = stC.enter_context(nc.psum_tensor(pfx + "pC1", [4, 16], F32))
                pC2 = stC.enter_context(nc.psum_tensor(pfx + "pC2", [64, 16], F32))
                tmsR = E("vector", 1, "memset", wtmp[0:1, :], 0.0)
                W("vector", "vector", tmsR)
                if is_gla:
                    W("vector", "vector", tok_lab)
                    tok_lr = E("vector", 1, "tensor_copy", wtmp[0:1, 0:16],
                               labs[0:1, 0, :])
                else:
                    tok_lr = tmsR
                Sfin = Sa if 16 % 2 == 0 else Sb
                W("gpsimd", "vector", tok_S_final)
                W("gpsimd", "vector", tok_lr)
                cci = extra["cc_in"]; cco = extra["cc_out"]
                E("gpsimd", 16, "dma_start", out=cci[0:64, :], in_=Sfin[:, :])
                E("gpsimd", 16, "dma_start", out=cci[64:65, :], in_=wtmp[0:1, :])
                tok_ccin = E("gpsimd", 16, "dma_start", out=cci[65:66, :], in_=wtmp[0:1, :])
                ccbase = extra["ccbase"][0]
                P.EX("gpsimd", "ccsem", 1, "collective_compute", "AllGather", ALU.bypass,
                     ins=[extra["cc_in"].opt()], outs=[extra["cc_out"].opt()],
                     replica_groups=[[0, 1, 2, 3], [4, 5, 6, 7]])
                P.ops["gpsimd"].append(("w", "ccsem", ccbase + 1))
                extra["ccbase"][0] = ccbase + 1
                tok_sgld = {}
                for j in range(4):
                    E("gpsimd", 16, "dma_start", out=fga[j:j + 1, :],
                      in_=cco[66 * j + 64:66 * j + 65, 0:16])
                tok_gath = P.cnt["gpsimd"]
                if is_gla:
                    W("sync", "gpsimd", tok_gath)
                    tok_mrk = E("sync", 16, "dma_start", out=zt1[0:4, 0:4],
                                in_=extra["MrankT"][:, :])
                    W("tensor", "sync", tok_mrk)
                    W("tensor", "gpsimd", tok_gath)
                    tok_gmm = E("tensor", 1, "matmul", pC1[:, :], zt1[0:4, 0:4],
                                fga[0:4, :], start=True, stop=True)
                    W("scalar", "tensor", tok_gmm)
                    tok_cf = E("scalar", 1, "activation", coefs[:, :], pC1[:, :], AF.Exp)
                    W("sync", "gpsimd", tok_gath)
                    tok_rmc = E("sync", 16, "dma_start", out=zt2[0:4, 0:1],
                                in_=extra["rmcol"][:, :])
                    W("vector", "scalar", tok_cf)
                    W("vector", "sync", tok_rmc)
                    tok_cf2 = E("vector", 1, "tensor_scalar", fga[0:4, :], coefs[:, :],
                                zt2[0:4, 0:1], None, ALU.mult)
                    W("gpsimd", "vector", tok_cf2)
                    for j in range(4):
                        E("gpsimd", 16, "dma_start", out=coefr[0:1, j, :],
                          in_=fga[j:j + 1, :])
                    tok_cr = P.cnt["gpsimd"]
                    prevcb = 0
                    for j in range(4):
                        W("tensor", "gpsimd", tok_cr)
                        W("tensor", "vector", tok_ones)
                        if prevcb:
                            W("tensor", "scalar", prevcb)
                        tok_cb2 = E("tensor", 1, "matmul", pC2[:, :],
                                    ones_row[0:1, 0:64], coefr[0:1, j, :],
                                    start=True, stop=True)
                        W("scalar", "tensor", tok_cb2)
                        prevcb = E("scalar", 1, "copy", coefB[:, j, :], pC2[:, :])
                    tok_coefB = prevcb
                    W("vector", "scalar", tok_coefB)
                    sgacc = {}
                    for j in range(4):
                        if j >= 2:
                            W("gpsimd", "vector", sgacc[j - 2])
                        tok_sgld[j] = E("gpsimd", 16, "dma_start", out=Sg[0:64, j % 2, :],
                                        in_=cco[66 * j:66 * j + 64, :])
                        W("vector", "gpsimd", tok_sgld[j])
                        for hh in range(16):
                            sgs = Sg[0:64, j % 2, 64 * hh:64 * (hh + 1)]
                            if j == 0:
                                E("vector", 0 if hh < 15 else 1, "tensor_scalar",
                                  Sa[:, 64 * hh:64 * (hh + 1)], sgs,
                                  coefB[:, 0, hh:hh + 1], None, ALU.mult)
                            else:
                                E("vector", 0 if hh < 15 else 1, "scalar_tensor_tensor",
                                  (Sa if j % 2 == 0 else Sb)[:, 64 * hh:64 * (hh + 1)],
                                  sgs, coefB[:, j, hh:hh + 1],
                                  (Sb if j % 2 == 0 else Sa)[:, 64 * hh:64 * (hh + 1)],
                                  ALU.mult, ALU.add)
                        sgacc[j] = P.cnt["vector"]
                        DR("vector")
                else:
                    W("sync", "gpsimd", tok_gath)
                    tok_rm4 = E("sync", 16, "dma_start", out=coefB[:, 0, 0:4],
                                in_=extra["rm4B"][:, :])
                    W("vector", "sync", tok_rm4)
                    W("vector", "gpsimd", tok_gath)
                    sgacc = {}
                    for j in range(4):
                        if j >= 2:
                            W("gpsimd", "vector", sgacc[j - 2])
                        tok_sgld[j] = E("gpsimd", 16, "dma_start", out=Sg[0:64, j % 2, :],
                                        in_=cco[66 * j:66 * j + 64, :])
                        W("vector", "gpsimd", tok_sgld[j])
                        for hh in range(16):
                            sgs = Sg[0:64, j % 2, 64 * hh:64 * (hh + 1)]
                            if j == 0:
                                E("vector", 0 if hh < 15 else 1, "tensor_scalar",
                                  Sa[:, 64 * hh:64 * (hh + 1)], sgs,
                                  coefB[:, 0, 0:1], None, ALU.mult)
                            else:
                                E("vector", 0 if hh < 15 else 1, "scalar_tensor_tensor",
                                  (Sa if j % 2 == 0 else Sb)[:, 64 * hh:64 * (hh + 1)],
                                  sgs, coefB[:, 0, j:j + 1],
                                  (Sb if j % 2 == 0 else Sa)[:, 64 * hh:64 * (hh + 1)],
                                  ALU.mult, ALU.add)
                        sgacc[j] = P.cnt["vector"]
                        DR("vector")
                tok_sin = E("vector", 1, "tensor_copy", Shin[:, :], Sb[:, :])
                W("gpsimd", "vector", tok_sin)
                W("gpsimd", "tensor", tok_B_t)
                tok_sinhi = E("gpsimd", 16, "dma_start", out=ShHi[64:128, :],
                              in_=Shin[0:64, :])
                stC.close()

                stD = contextlib.ExitStack()
                pFX2 = [stD.enter_context(nc.psum_tensor(pfx + f"pFX{i}", [128, 64], F32))
                        for i in range(2)]
                pW2 = [stD.enter_context(nc.psum_tensor(pfx + f"pW{i}", [128, 512], F32))
                       for i in range(2)]
                pF3 = stD.enter_context(nc.psum_tensor(pfx + "pF3", [128, 16], F32))
                pTRd = stD.enter_context(nc.psum_tensor(pfx + "pTRd", [128, 128], BF16))
                pGT = stD.enter_context(nc.psum_tensor(pfx + "pGT", [128, 512], F32))

                tok_wo = 0
                for d8 in range(8):
                    W("sync", "tensor", tok_phaseA_t)
                    if tok_wo:
                        W("sync", "vector", tok_wo)
                    tok_wold = E("sync", 16, "dma_start", out=wtmp[:, :],
                                 in_=WoT_d[128 * d8:128 * (d8 + 1), :])
                    W("vector", "sync", tok_wold)
                    if is_gla:
                        W("vector", "gpsimd", tok_wres)
                        tok_wo = E("vector", 1, "tensor_scalar", wbufA[:, d8, :],
                                   wtmp[:, :], lnwc[:, d8:d8 + 1], None, ALU.mult)
                    else:
                        tok_wo = E("vector", 1, "tensor_copy", wbufA[:, d8, :], wtmp[:, :])

                vtok_fx = 0
                prev_fx_v = 0
                ln_i = NT
                fix_done = {}
                og_done = {}
                ea_done = {}
                gate_ev = {}
                ohT_ev = {}
                res_done = {}
                for ch in range(16):
                    t8 = ch // 2
                    W("sync", "gpsimd", odma[ch])
                    if ch < 4:
                        W("sync", "tensor", tok_B_t)
                        W("sync", "vector", tok_B_v)
                    if ch >= 4:
                        W("sync", "tensor", fix_done.get(ch - 4, 0))
                    E("sync", 16, "dma_start", out=qr[:, ch % 4, :, :], in_=q_sp[ch, :, :, :])
                    if ch >= 2:
                        W("sync", "vector", og_done.get(ch - 2, 0))
                    tok_old = E("sync", 16, "dma_start", out=Or[:, ch % 2, :],
                                in_=o_sp[ch, :, :])
                    if is_gla:
                        W("tensor", "vector", tok_lr)
                        if ch >= 1:
                            W("tensor", "vector", ea_done.get(ch - 1, 0))
                        tok_fb = E("tensor", 1, "matmul", pF3[:, :], ones_row[0:1, :],
                                   t1b[0:1, 16 * ch:16 * (ch + 1)], start=True, stop=True)
                        W("vector", "tensor", tok_fb)
                        tok_ea1 = E("vector", 1, "tensor_tensor", eab[:, :],
                                    Fst[:, ch, :], pF3[:, :], ALU.add)
                        ea_done[ch] = tok_ea1
                        W("scalar", "vector", tok_ea1)
                        if ch >= 1:
                            W("scalar", "vector", prev_fx_v)
                        tok_EA = E("scalar", 1, "activation", EA[:, :], eab[:, :], AF.Exp)
                    else:
                        if ch >= 1:
                            W("vector", "vector", ea_done.get(ch - 1, 0))
                        E("vector", 1, "tensor_scalar", posc[:, 1:2], posc[:, 0:1],
                          float(128 * ch), None, ALU.add)
                        DR("vector")
                        tok_g16 = E("vector", 1, "tensor_scalar", zt2[:, :], lngB[:, :],
                                    posc[:, 1:2], None, ALU.mult)
                        ea_done[ch] = tok_g16
                        W("scalar", "vector", tok_g16)
                        if ch >= 1:
                            W("scalar", "vector", prev_fx_v)
                        tok_EA = E("scalar", 1, "activation", EA[:, :], zt2[:, :], AF.Exp)
                    W("tensor", "vector", tok_sin)
                    W("tensor", "sync", tok_old)
                    W("vector", "scalar", tok_EA)
                    W("vector", "sync", tok_old)
                    vtok_fx2 = {}
                    for hh in range(16):
                        po = 64 * (hh % 2)
                        blkh = hh // 2
                        pFX = pFX2[hh % 2]
                        pfx2 = vtok_fx2.get(hh - 2, vtok_fx if hh < 2 else 0)
                        if pfx2:
                            W("tensor", "vector", pfx2)
                        if hh % 2 == 1:
                            W("tensor", "gpsimd", tok_sinhi)
                            rhsS2 = ShHi[64:128, 64 * hh:64 * (hh + 1)]
                        else:
                            rhsS2 = Shin[0:64, 64 * hh:64 * (hh + 1)]
                        tok_fx = E("tensor", 1, "matmul", pFX[:, :],
                                   qr[po:po + 64, ch % 4, blkh, :],
                                   rhsS2, start=True, stop=True)
                        W("vector", "tensor", tok_fx)
                        vtok_fx = E("vector", 1, "scalar_tensor_tensor",
                                    Kt[:, ch % 2, 64 * hh:64 * (hh + 1)], pFX[:, :],
                                    EA[:, hh:hh + 1],
                                    Or[:, ch % 2, 64 * hh:64 * (hh + 1)],
                                    ALU.mult, ALU.add)
                        vtok_fx2[hh] = vtok_fx
                    tok_Ofix = vtok_fx
                    prev_fx_v = tok_Ofix

                    if is_gla:
                        for oh in range(2):
                            W("tensor", "scalar", g1ev[t8])
                            W("tensor", "gpsimd", tok_wres)
                            if oh == 1:
                                W("tensor", "scalar", gate_ev[ch])
                            elif ch >= 1:
                                W("tensor", "scalar", gate_ev.get(ch - 1, 0))
                            tok_g = E("tensor", 1, "matmul", pGT[:, :],
                                      g1f[:, t8, 128 * (ch % 2):128 * (ch % 2 + 1)],
                                      wg2_s[:, 512 * oh:512 * (oh + 1)],
                                      start=True, stop=True)
                            W("scalar", "tensor", tok_g)
                            gate_ev[ch] = E("scalar", 1, "activation",
                                            gatet[:, ch % 2, 512 * oh:512 * (oh + 1)],
                                            pGT[:, :], AF.Sigmoid)
                        W("vector", "scalar", gate_ev[ch])
                        W("vector", "vector", tok_Ofix)
                        tok_og = E("vector", 1, "tensor_tensor", ogb[:, ch % 2, :],
                                   Kt[:, ch % 2, :], gatet[:, ch % 2, :], ALU.mult)
                        og_done[ch] = tok_og
                        tok_oh = ln_tile_plain(ln_i, ogb[:, ch % 2, :], dep_v=tok_og)
                        ln_i += 1
                        src_oh = xqt[:, (ln_i - 1) % 2, :]
                    else:
                        og_done[ch] = tok_Ofix
                        tok_oh = tok_Ofix
                        src_oh = Kt[:, ch % 2, :]
                    for d8 in range(8):
                        kk = ch * 8 + d8
                        W("tensor", "vector", tok_oh)
                        prev_ev2 = ohT_ev.get((ch, d8 - 1), 0) if d8 > 0 else \
                            ohT_ev.get((ch - 1, 7), 0)
                        if prev_ev2:
                            W("tensor", "scalar", prev_ev2)
                        tok_t2 = E("tensor", 1, "transpose", pTRd[:, :],
                                   src_oh[:, 128 * d8:128 * (d8 + 1)], c_idb[:, :])
                        W("scalar", "tensor", tok_t2)
                        ohT_ev[(ch, d8)] = E("scalar", 1, "copy", khT[:, 0, d8, :]
                                             if False else ohT[:, 0, d8, :],
                                             pTRd[:, :])
                    if is_gla:
                        tr_lastA[ln_i - 1] = P.cnt["tensor"]
                    for jh in range(2):
                        W("tensor", "scalar", ohT_ev[(ch, 7)])
                        W("tensor", "vector", tok_wo)
                        if ch >= 1:
                            W("tensor", "vector", res_done.get(ch - 1, 0))
                        first = True
                        if not is_gla:
                            W("tensor", "vector", tok_ones)
                            E("tensor", 0, "matmul", pW2[jh][:, :], ones_row[0:1, :],
                              bo_s[0:1, 512 * jh:512 * (jh + 1)],
                              start=True, stop=False)
                            first = False
                        for o8 in range(8):
                            E("tensor", 0 if o8 < 7 else 1, "matmul", pW2[jh][:, :],
                              ohT[:, 0, o8, :],
                              wbufA[:, o8, 512 * jh:512 * (jh + 1)],
                              start=(o8 == 0 and first), stop=(o8 == 7))
                    tok_wmm = P.cnt["tensor"]
                    W("vector", "tensor", tok_wmm)
                    r = ch
                    E("vector", 0, "tensor_tensor", xres[:, spare[0], 0:512],
                      pW2[0][:, :], xres[:, slot[r], 0:512], ALU.add)
                    res_done[ch] = E("vector", 1, "tensor_tensor",
                                     xres[:, spare[0], 512:1024], pW2[1][:, :],
                                     xres[:, slot[r], 512:1024], ALU.add)
                    slot[r], spare[0] = spare[0], slot[r]
                    fix_done[ch] = P.cnt["tensor"]
                stD.close()

        if n_sub >= 1:
            emit_l0_ffn("f1_")

        F8 = mybir.dt.float8e5

        def emit_l1_prep():
            sub = contextlib.ExitStack()
            with sub:
                SS = sub.enter_context
                q8 = SS(nc.sbuf_tensor("p8_q8", [128, D], F8))
                b2stage = SS(nc.sbuf_tensor("p8_b2s", [1, D], F32))
                b2r8 = SS(nc.sbuf_tensor("p8_b2r8", [1, D], F8))
                qb = SS(nc.sbuf_tensor("p8_qb", [128, 2, D], BF16))
                b8 = SS(nc.sbuf_tensor("p8_b8", [128, 1], F8))
                chunks = []
                for j in range(8):
                    for q in range(4):
                        chunks.append(("w1", w1T if False else W1T1, (j, q)))
                for j in range(32):
                    chunks.append(("w2", W2T1, (j,)))
                nq = len(chunks)
                v_base = P.cnt["vector"]
                g_base = P.cnt["gpsimd"]
                tok_v = [v_base + 2 * (i + 1) for i in range(nq)]
                tok_s = []
                acc = g_base
                for i, (w, src_t, key) in enumerate(chunks):
                    acc += 16 * (8 if w == "w1" else 1)
                    tok_s.append(acc)
                tok_l = []
                for i, (w, src_t, key) in enumerate(chunks):
                    if w == "w1":
                        j, q = key
                        src_ap = src_t[128 * j:128 * (j + 1), 1024 * q:1024 * (q + 1)]
                    else:
                        (j,) = key
                        src_ap = src_t[128 * j:128 * (j + 1), :]
                    if i >= 2:
                        W("sync", "vector", tok_v[i - 2])
                    tok_l.append(E("sync", 16, "dma_start", out=xh[:, i % 2, :], in_=src_ap))
                for i, (w, src_t, key) in enumerate(chunks):
                    W("vector", "sync", tok_l[i])
                    if i >= 2:
                        W("vector", "gpsimd", tok_s[i - 2])
                    E("vector", 1, "tensor_copy", q8[:, :], xh[:, i % 2, :])
                    DR("vector")
                    t_ = E("vector", 1, "tensor_copy", qb[:, i % 2, :], q8[:, :])
                    assert t_ == tok_v[i]
                for i, (w, src_t, key) in enumerate(chunks):
                    W("gpsimd", "vector", tok_v[i])
                    if w == "w1":
                        j, q = key
                        for t in range(8):
                            E("gpsimd", 16, "dma_start", out=w1q8_d[8 * q + t, :, j, :],
                              in_=qb[:, i % 2, 128 * t:128 * (t + 1)])
                        assert P.cnt["gpsimd"] == tok_s[i]
                    else:
                        (j,) = key
                        t_ = E("gpsimd", 16, "dma_start",
                               out=w2q8_d[128 * j:128 * (j + 1), :],
                               in_=qb[:, i % 2, :])
                        assert t_ == tok_s[i]
                for j in range(32):
                    tok_b = E("sync", 16, "dma_start", out=cols[:, 4:5],
                              in_=b1v[128 * j:128 * (j + 1), :])
                    W("vector", "sync", tok_b)
                    E("vector", 1, "tensor_copy", b8[:, 0:1], cols[:, 4:5])
                    DR("vector")
                    tok_bv = E("vector", 1, "tensor_copy", b1c32[:, j:j + 1], b8[:, 0:1])
                    W("sync", "vector", tok_bv)
                tok_b2l = E("sync", 16, "dma_start", out=cols[:, 4:5], in_=b2v[0:128, :])
                for j in range(8):
                    tok_b2 = E("sync", 16, "dma_start", out=b2stage[0:1, 128 * j:128 * (j + 1)],
                               in_=b2v[128 * j:128 * (j + 1), 0:1])
                W("vector", "sync", tok_b2)
                E("vector", 1, "tensor_copy", b2r8[0:1, :], b2stage[0:1, :])
                DR("vector")
                tok_fin = E("vector", 1, "tensor_copy", b2row[0:1, :], b2r8[0:1, :])
            return tok_fin

        def emit_l1_ffn(pfx, tok_prep):
            sub = contextlib.ExitStack()
            with sub:
                SS = sub.enter_context

                def ssb(name, shape, dt=F32):
                    return SS(nc.sbuf_tensor(pfx + name, shape, dt))

                xqT = ssb("xqT", [128, 8 * 2048], BF16)
                hfull = ssb("hfull", [128, 32, FT], BF16)
                h8b = ssb("h8b", [128, 2, FT], BF16)
                h8f = ssb("h8f", [128, FT], F8)
                x8f = ssb("x8f", [128, D], F8)
                w1c = ssb("w1c", [128, 2, 8, 128], BF16)
                w2c = ssb("w2c", [128, 2, D], BF16)
                mstg = ssb("mstg", [128, 2, 2, D], BF16)
                ybb = ssb("ybb", [128, 512], BF16)
                ybm = ssb("ybm", [128, 512])

                p_tr = [SS(nc.psum_tensor(pfx + f"ptr{i}", [128, 128], BF16)) for i in range(2)]
                p_mm = [SS(nc.psum_tensor(pfx + f"pmm{i}", [128, FT], F32)) for i in range(2)]
                p_y = [SS(nc.psum_tensor(pfx + f"py{i}", [128, 512], F32)) for i in range(4)]

                def xqa(d, a, b):
                    return xqT[:, 2048 * d + a:2048 * d + b]

                tok_xq = [0] * (NT + 4)
                tok_ev = []
                tr_last = [0] * (NT + 4)
                for i in range(NT):
                    Pn = 128
                    src = xres[:, slot[i], :]
                    xhb = xh[:, i % 2, :]
                    xqb = xqt[:, i % 2, :]
                    dump = sqdump[:, :]

                    def CL(j):
                        return cols[:, 52 + j:53 + j]

                    def SL(j):
                        return scol[:, j:j + 1]

                    if i >= 2:
                        W("scalar", "vector", tok_xq[i - 2])
                    tok_ms = E("scalar", 1, "activation", dump, src, AF.Identity,
                               accum_out=SL(0))
                    W("vector", "scalar", tok_ms)
                    E("vector", 1, "tensor_scalar", CL(0), SL(0), 1.0 / D, None, ALU.mult)
                    DR("vector")
                    tok_nm = E("vector", 1, "tensor_scalar", CL(1), CL(0), -1.0, None, ALU.mult)
                    W("scalar", "vector", tok_nm)
                    tok_ss = E("scalar", 1, "activation", dump, src, AF.Square,
                               bias=CL(1), scale=1.0, accum_out=SL(1))
                    W("vector", "scalar", tok_ss)
                    tok_ve = E("vector", 1, "tensor_scalar", CL(2), SL(1), 1.0 / D, 1e-5,
                               ALU.mult, ALU.add)
                    W("scalar", "vector", tok_ve)
                    E("scalar", 1, "activation", SL(2), CL(2), AF.Ln)
                    DR("scalar")
                    tok_rs = E("scalar", 1, "activation", SL(3), SL(2), AF.Exp, scale=-0.5)
                    W("vector", "scalar", tok_rs)
                    E("vector", 1, "tensor_copy", CL(3), SL(3))
                    DR("vector")
                    E("vector", 1, "tensor_scalar", xhb, src, CL(0), CL(3),
                      ALU.subtract, ALU.mult)
                    DR("vector")
                    E("vector", 1, "tensor_copy", x8f[:, :], xhb)
                    DR("vector")
                    if i >= 2:
                        W("vector", "tensor", tr_last[i - 2])
                    tok_xq[i] = E("vector", 1, "tensor_copy", xqb, x8f[:, :])
                    for d8 in range(8):
                        kk = len(tok_ev)
                        W("tensor", "vector", tok_xq[i])
                        if kk >= 2:
                            W("tensor", "scalar", tok_ev[kk - 2])
                        tok_t = E("tensor", 1, "transpose", p_tr[kk % 2][:, :],
                                  xqb[:, 128 * d8:128 * (d8 + 1)], c_idb[:, :])
                        W("scalar", "tensor", tok_t)
                        tok_ev.append(E("scalar", 1, "copy",
                                        xqa(d8, 128 * i, 128 * (i + 1)),
                                        p_tr[kk % 2][:, :]))
                    tr_last[i] = P.cnt["tensor"]
                tok_xqT = tok_ev[-1]

                w1_tok, w2_tok = {}, {}
                w1_use, w2_use = {}, {}
                relu_done, q8_done, yv_done, m_tok = {}, {}, {}, {}
                for i in range(NFT):
                    pv = yv_done.get(i - 2, 0)
                    if pv:
                        W("sync", "vector", pv)
                    E("sync", 16, "dma_start", out=mstg[:, i % 2, 0, :],
                      in_=maskC[FT * i:FT * i + 128, :])
                    m_tok[i] = E("sync", 16, "dma_start", out=mstg[:, i % 2, 1, :],
                                 in_=maskC[FT * i + 128:FT * (i + 1), :])
                    for c in range(32):
                        k = i * 32 + c
                        t_use = w1_use.get(k - 2, 0)
                        if t_use:
                            W("sync", "tensor", t_use)
                        w1_tok[k] = E("sync", 16, "dma_start", out=w1c[:, k % 2, :, :],
                                      in_=w1q8_d[c, :, :, :])
                        W("tensor", "sync", w1_tok[k])
                        if i == 0 and c == 0:
                            W("tensor", "scalar", tok_xqT)
                            W("tensor", "vector", tok_prep)
                        pe = relu_done.get((i, c - 2), 0)
                        if pe:
                            W("tensor", "scalar", pe)
                        pq = q8_done.get((i - 1, c), 0)
                        if pq:
                            W("tensor", "vector", pq)
                        for d8 in range(8):
                            E("tensor", 0 if d8 < 7 else 1, "matmul", p_mm[c % 2][:, :],
                              w1c[:, k % 2, d8, :], xqa(d8, FT * i, FT * (i + 1)),
                              start=(d8 == 0), stop=(d8 == 7))
                        w1_use[k] = P.cnt["tensor"]
                        tok_mm = P.cnt["tensor"]
                        W("scalar", "tensor", tok_mm)
                        relu_done[(i, c)] = E("scalar", 1, "activation", hfull[:, c, :],
                                              p_mm[c % 2][:, :], AF.Relu,
                                              bias=b1c32[:, c:c + 1], scale=1.0)
                    for c in range(32):
                        k = i * 32 + c
                        t_use = w2_use.get(k - 2, 0)
                        if t_use:
                            W("sync", "tensor", t_use)
                        w2_tok[k] = E("sync", 16, "dma_start", out=w2c[:, k % 2, :],
                                      in_=w2q8_d[128 * c:128 * (c + 1), :])
                        W("vector", "scalar", relu_done[(i, c)])
                        pu = w2_use.get(k - 2, 0)
                        if pu:
                            W("vector", "tensor", pu)
                        E("vector", 1, "tensor_copy", h8f[:, :], hfull[:, c, :])
                        DR("vector")
                        tok_h8 = E("vector", 1, "tensor_copy", h8b[:, c % 2, :], h8f[:, :])
                        q8_done[(i, c)] = tok_h8
                        W("tensor", "vector", tok_h8)
                        W("tensor", "sync", w2_tok[k])
                        if c == 0:
                            pv2 = yv_done.get(i - 1, 0)
                            if pv2:
                                W("tensor", "vector", pv2)
                            W("tensor", "vector", tok_ones)
                            W("tensor", "vector", tok_prep)
                            for tsub in range(2):
                                for ohh in range(2):
                                    E("tensor", 0, "matmul", p_y[2 * tsub + ohh][:, :],
                                      ones_row[0:1, :],
                                      b2row[0:1, 512 * ohh:512 * (ohh + 1)],
                                      start=True, stop=False)
                        for tsub in range(2):
                            for ohh in range(2):
                                E("tensor", 0 if (tsub, ohh) != (1, 1) else 1, "matmul",
                                  p_y[2 * tsub + ohh][:, :],
                                  h8b[:, c % 2, 128 * tsub:128 * (tsub + 1)],
                                  w2c[:, k % 2, 512 * ohh:512 * (ohh + 1)],
                                  start=False, stop=(c == 31))
                        w2_use[k] = P.cnt["tensor"]
                    tok_y = P.cnt["tensor"]
                    W("vector", "tensor", tok_y)
                    W("vector", "sync", m_tok[i])
                    for tsub in range(2):
                        r = 2 * i + tsub
                        for ohh in range(2):
                            E("vector", 1, "tensor_copy", ybb[:, 0:512],
                              p_y[2 * tsub + ohh][:, :])
                            DR("vector")
                            E("vector", 1, "tensor_tensor", ybm[:, 0:512], ybb[:, 0:512],
                              mstg[:, i % 2, tsub, 512 * ohh:512 * (ohh + 1)], ALU.mult)
                            DR("vector")
                            E("vector", 1, "tensor_tensor",
                              xres[:, spare[0], 512 * ohh:512 * (ohh + 1)], ybm[:, 0:512],
                              xres[:, slot[r], 512 * ohh:512 * (ohh + 1)], ALU.add)
                        slot[r], spare[0] = spare[0], slot[r]
                    yv_done[i] = P.cnt["vector"]
            return None

        ccbase_track = [0]
        if n_sub >= 2:
            P.barrier()
            emit_attn("a0_", "gla", (WqT0, WkT0, WvT0, WoT0), {
                "WfT": WfT0, "Wg1T": Wg1T0, "Wg2T": Wg2T0, "lnw": lnw0,
                "MrankT": MrankT, "rmcol": rmcol, "rm4B": rm4B,
                "cc_in": cc1_in, "cc_out": cc1_out, "ccbase": ccbase_track,
            })
        if n_sub >= 3:
            P.barrier()
            stH = contextlib.ExitStack()
            pHL = stH.enter_context(nc.psum_tensor("pHL", [1, 512], F32))
            gbuf = stH.enter_context(nc.sbuf_tensor("gbuf", [8, D], F32))
            selb = stH.enter_context(nc.sbuf_tensor("selb", [8, 2], F32))
            hstg = stH.enter_context(nc.sbuf_tensor("hstg", [1, D], F32))
            E("gpsimd", 16, "dma_start", out=cc2_in[0:1, :], in_=xres[0:1, slot[0], :])
            tok_h0 = E("gpsimd", 16, "dma_start", out=cc2_in[1:2, :],
                       in_=xres[127:128, slot[15], :])
            ccb = ccbase_track[0]
            P.EX("gpsimd", "ccsem", 1, "collective_compute", "AllGather", ALU.bypass,
                 ins=[cc2_in.opt()], outs=[cc2_out.opt()],
                 replica_groups=[[0, 1, 2, 3], [4, 5, 6, 7]])
            P.ops["gpsimd"].append(("w", "ccsem", ccb + 1))
            ccbase_track[0] = ccb + 1
            E("gpsimd", 16, "dma_start", out=gbuf[:, :], in_=cc2_out[:, :])
            E("sync", 16, "dma_start", out=selb[:, 0:1], in_=sel_lo[:, :])
            tok_hg = E("sync", 16, "dma_start", out=selb[:, 1:2], in_=sel_hi[:, :])
            W("tensor", "gpsimd", P.cnt["gpsimd"])
            W("tensor", "sync", tok_hg)
            tok_hrow = 0
            for which in range(2):
                for half in range(2):
                    tok_mmh = E("tensor", 1, "matmul", pHL[:, :],
                                selb[:, which:which + 1],
                                gbuf[:, 512 * half:512 * (half + 1)],
                                start=True, stop=True)
                    W("vector", "tensor", tok_mmh)
                    if which == 0:
                        tok_hrow = E("vector", 1, "tensor_copy",
                                     halo_rows[0:1, 512 * half:512 * (half + 1)], pHL[:, :])
                    else:
                        tok_hrow = E("vector", 1, "tensor_copy",
                                     hstg[0:1, 512 * half:512 * (half + 1)], pHL[:, :])
                    W("tensor", "vector", tok_hrow)
            W("gpsimd", "vector", tok_hrow)
            tok_hdma = E("gpsimd", 16, "dma_start", out=halo_rows[1:2, :], in_=hstg[0:1, :])
            P.barrier()
            stH.close()
            emit_l0_ffn("f2_")
        if n_sub >= 4:
            P.barrier()
            tok_prep = emit_l1_prep()
            P.barrier()
            emit_l1_ffn("g1_", tok_prep)
        if n_sub >= 5:
            P.barrier()
            emit_attn("a1_", "dec", (WqT1, WkT1, WvT1, WoT1), {
                "gam_row": gam_row, "poscol": poscol, "rm4B": rm4B,
                "bqc": bqc1, "bkr": bkr1, "bvr": bvr1, "bor": bor1,
                "cc_in": cc3_in, "cc_out": cc3_out, "ccbase": ccbase_track,
            })
        if n_sub >= 6:
            P.barrier()
            emit_l1_ffn("g2_", tok_prep)

        P.barrier()
        tok_out = 0
        for i in range(NT):
            tok_out = E("sync", 16, "dma_start", out=out_d[128 * i:128 * (i + 1), :],
                        in_=xres[:, slot[i], :])
        P.ops["sync"].append(("w", "sync", tok_out))

        P.replay(blk, sems)
    return nc


_IDN_BF = np.eye(128).astype(BF)
_IDN_F = np.eye(128, dtype=np.float32)
_SEL2 = np.zeros((2, 256), np.float32)
_SEL2[0, 0:128] = 1.0
_SEL2[1, 128:256] = 1.0
_TRIU_F = np.triu(np.ones((128, 128), np.float32))
_TRIU_B = np.triu(np.ones((128, 128))).astype(BF)
_MASKC = None


def _get_maskc():
    global _MASKC
    if _MASKC is None:
        keep = _keep_mask(42, 0.8, (2, 4 * T, D))
        _MASKC = np.ascontiguousarray((keep.astype(np.float32) * np.float32(1.25)
                                       * np.float32(0.5)).astype(BF))
    return _MASKC


def _prep_inputs(inputs):
    if N_SUB >= 4:
        _get_maskc()
    x = np.asarray(inputs["L0_attn_x"], np.float32)
    w1T = np.ascontiguousarray(np.asarray(inputs["L0_ffn_w1"], np.float32).T)
    w2T = np.ascontiguousarray(np.asarray(inputs["L0_ffn_w2"], np.float32).T)
    convw = np.ascontiguousarray(np.asarray(inputs["L0_ffn_conv_w"], np.float32).reshape(DH, 3))
    convb = np.ascontiguousarray(np.asarray(inputs["L0_ffn_conv_b"], np.float32).reshape(DH, 1))
    in_maps = []
    for c in range(8):
        b, q = c // 4, c % 4
        xs = np.zeros((T + 2, D), np.float32)
        xs[1:T + 1] = x[b, q * T:(q + 1) * T]
        if q > 0:
            xs[0] = x[b, q * T - 1]
        if q < 3:
            xs[T + 1] = x[b, (q + 1) * T]
        m = {
            "x_in": xs, "w1T": w1T, "w2T": w2T, "convw": convw, "convb": convb,
            "idn_bf": _IDN_BF, "idn_f": _IDN_F, "sel2": _SEL2,
            "triu_f": _TRIU_F, "triu_b": _TRIU_B,
        }
        if N_SUB >= 2:
            rank = q
            mrT = np.zeros((4, 4), np.float32)
            for jj in range(4):
                for ll in range(jj + 1, rank):
                    mrT[ll, jj] = 1.0
            rm = np.zeros((4, 1), np.float32)
            rm[:rank, 0] = 1.0
            sl = np.zeros((8, 1), np.float32)
            sh = np.zeros((8, 1), np.float32)
            if q > 0:
                sl[2 * (q - 1) + 1, 0] = 1.0
            if q < 3:
                sh[2 * (q + 1), 0] = 1.0
            m["sel_lo"] = sl
            m["sel_hi"] = sh
            m.update({
                "WqT0": np.ascontiguousarray(np.asarray(inputs["L0_attn_Wq"], np.float32).T),
                "WkT0": np.ascontiguousarray(np.asarray(inputs["L0_attn_Wk"], np.float32).T),
                "WvT0": np.ascontiguousarray(np.asarray(inputs["L0_attn_Wv"], np.float32).T),
                "WoT0": np.ascontiguousarray(np.asarray(inputs["L0_attn_Wo"], np.float32).T),
                "WfT0": np.ascontiguousarray(np.asarray(inputs["L0_attn_Wf"], np.float32).T),
                "Wg1T0": np.ascontiguousarray(np.asarray(inputs["L0_attn_Wg1"], np.float32).T),
                "Wg2T0": np.ascontiguousarray(np.asarray(inputs["L0_attn_Wg2"], np.float32).T),
                "lnw0": np.ascontiguousarray(np.asarray(inputs["L0_attn_ln_w"], np.float32).reshape(D, 1)),
                "MrankT": mrT, "rm4B": np.broadcast_to(rm.T, (64, 4)).copy(),
                "rmcol": rm,
            })
        if N_SUB >= 4:
            m.update({
                "W1T1": np.ascontiguousarray(np.asarray(inputs["L1_ffn_W1"], np.float32).T),
                "W2T1": np.ascontiguousarray(np.asarray(inputs["L1_ffn_W2"], np.float32).T),
                "b1v": np.ascontiguousarray(np.asarray(inputs["L1_ffn_b1"], np.float32).reshape(DH, 1)),
                "b2v": np.ascontiguousarray(np.asarray(inputs["L1_ffn_b2"], np.float32).reshape(D, 1)),
                "maskC": _MASKC[b, q * T:(q + 1) * T],
            })
        if N_SUB >= 5:
            m.update({
                "WqT1": np.ascontiguousarray(np.asarray(inputs["L1_attn_Wq"], np.float32).T),
                "WkT1": np.ascontiguousarray(np.asarray(inputs["L1_attn_Wk"], np.float32).T),
                "WvT1": np.ascontiguousarray(np.asarray(inputs["L1_attn_Wv"], np.float32).T),
                "WoT1": np.ascontiguousarray(np.asarray(inputs["L1_attn_Wo"], np.float32).T),
                "bqc1": np.ascontiguousarray(np.asarray(inputs["L1_attn_bq"], np.float32).reshape(D, 1)),
                "bkr1": np.ascontiguousarray(np.asarray(inputs["L1_attn_bk"], np.float32).reshape(1, D)),
                "bvr1": np.ascontiguousarray(np.asarray(inputs["L1_attn_bv"], np.float32).reshape(1, D)),
                "bor1": np.ascontiguousarray(np.asarray(inputs["L1_attn_bo"], np.float32).reshape(1, D)),
                "gam_row": np.ascontiguousarray(np.asarray(inputs["L1_attn_gamma"], np.float32).reshape(1, H)),
                "poscol": (q * T + np.arange(128, dtype=np.float32)).reshape(128, 1),
            })
        in_maps.append(m)
    return in_maps


def kernel(**inputs):
    in_maps = _prep_inputs(inputs)
    nc = build_program(N_SUB)
    res = run_bass_kernel_spmd(nc, in_maps, core_ids=list(range(8)))
    out = np.zeros((2, 4 * T, D), np.float32)
    for c in range(8):
        b, q = c // 4, c % 4
        out[b, q * T:(q + 1) * T] = np.asarray(res.results[c]["out"], np.float32)
    return out


# version 03
# speedup vs baseline: 1.0387x; 1.0387x over previous
import numpy as np
import ml_dtypes
import contextlib

import concourse.bass as bass
import concourse.mybir as mybir
from concourse.bass_utils import run_bass_kernel_spmd

F32 = mybir.dt.float32
BF16 = mybir.dt.bfloat16
AF = mybir.ActivationFunctionType
ALU = mybir.AluOpType
AX = mybir.AxisListType
BF = ml_dtypes.bfloat16
MAGIC = float(np.float32(1.5 * 2 ** 23))

T = 2048
D = 1024
DH = 4096
H = 16
HD = 64
NT = T // 128
FT = 256
NFT = T // FT
N_SUB = 6
DBG_NCH = 16
DEBUG_DUMP = False
ENGINES = ("tensor", "vector", "scalar", "gpsimd", "sync")


def _tf2x32(k0, k1, x0, x1):
    rot = [13, 15, 26, 6, 17, 29, 16, 24]
    x0 = x0.astype(np.uint32).copy()
    x1 = x1.astype(np.uint32).copy()
    ks = [np.uint32(k0), np.uint32(k1), np.uint32(np.uint32(k0) ^ np.uint32(k1) ^ np.uint32(0x1BD11BDA))]
    x0 += ks[0]; x1 += ks[1]
    for r in range(20):
        x0 += x1
        rr = rot[r % 8]
        x1 = (x1 << np.uint32(rr)) | (x1 >> np.uint32(32 - rr))
        x1 ^= x0
        if (r + 1) % 4 == 0:
            s = (r + 1) // 4
            x0 += ks[s % 3]
            x1 += ks[(s + 1) % 3] + np.uint32(s)
    return x0, x1


def _keep_mask(seed, p_keep, shape):
    n = int(np.prod(shape))
    idx = np.arange(n, dtype=np.uint64)
    o0, o1 = _tf2x32(0, seed, (idx >> np.uint64(32)).astype(np.uint32),
                     (idx & np.uint64(0xFFFFFFFF)).astype(np.uint32))
    u = (((o0 ^ o1) >> np.uint32(9)) | np.uint32(0x3F800000)).view(np.float32) - np.float32(1.0)
    return (u < np.float32(p_keep)).reshape(shape)


class Sched:
    def __init__(self):
        self.cnt = {e: 0 for e in ENGINES}
        self.ops = {e: [] for e in ENGINES}

    def E(self, eng, inc, opname, *a, **k):
        self.cnt[eng] += inc
        self.ops[eng].append(("i", inc, opname, a, k))
        return self.cnt[eng]

    def EX(self, eng, semname, inc, opname, *a, **k):
        self.ops[eng].append(("x", semname, inc, opname, a, k))

    def W(self, eng, on, tok):
        if tok > 0:
            self.ops[eng].append(("w", on, tok))

    def DR(self, eng):
        self.ops[eng].append(("w", eng, self.cnt[eng]))

    def barrier(self):
        toks = dict(self.cnt)
        for e in ENGINES:
            for o in ENGINES:
                if o != e:
                    self.W(e, o, toks[o])

    def replay(self, blk, sems):
        for e in ENGINES:
            items = self.ops[e]
            if not items:
                continue

            def body(h, e=e, items=items):
                for it in items:
                    if it[0] == "w":
                        h.wait_ge(sems[it[1]], it[2])
                    elif it[0] == "x":
                        _, semname, inc, opname, a, k = it
                        ins = getattr(h, opname)(*a, **k)
                        ins.then_inc(sems[semname], inc)
                    else:
                        _, inc, opname, a, k = it
                        ins = getattr(h, opname)(*a, **k)
                        if inc:
                            ins.then_inc(sems[e], inc)
            getattr(blk, e)(body)


def build_program(n_sub=N_SUB):
    nc = bass.Bass("TRN2", target_bir_lowering=False, detect_race_conditions=False)

    def din(name, shape, dt=F32):
        return nc.dram_tensor(name, shape, dt, kind="ExternalInput").ap()

    x_in = din("x_in", [T + 2, D])
    w1T = din("w1T", [D, DH])
    w2T = din("w2T", [DH, D])
    convw = din("convw", [DH, 3])
    convb = din("convb", [DH, 1])
    if n_sub >= 3:
        sel_lo = din("sel_lo", [8, 1])
        sel_hi = din("sel_hi", [8, 1])
    if n_sub >= 4:
        W1T1 = din("W1T1", [D, DH]); W2T1 = din("W2T1", [DH, D])
        b1v = din("b1v", [DH, 1]); b2v = din("b2v", [D, 1])
        maskC = din("maskC", [T, D], BF16)
    if n_sub >= 5:
        WqT1 = din("WqT1", [D, D]); WkT1 = din("WkT1", [D, D]); WvT1 = din("WvT1", [D, D])
        WoT1 = din("WoT1", [D, D])
        bqc1 = din("bqc1", [D, 1]); bkr1 = din("bkr1", [1, D]); bvr1 = din("bvr1", [1, D])
        bor1 = din("bor1", [1, D])
        gam_row = din("gam_row", [1, H])
        poscol = din("poscol", [128, 1])
    if n_sub >= 2:
        WqT0 = din("WqT0", [D, D]); WkT0 = din("WkT0", [D, D]); WvT0 = din("WvT0", [D, D])
        WoT0 = din("WoT0", [D, D]); WfT0 = din("WfT0", [D, H])
        Wg1T0 = din("Wg1T0", [D, HD]); Wg2T0 = din("Wg2T0", [HD, D])
        lnw0 = din("lnw0", [D, 1])
        MrankT = din("MrankT", [4, 4]); rm4B = din("rm4B", [64, 4])
        rmcol = din("rmcol", [4, 1])
    idn_bf = din("idn_bf", [128, 128], BF16)
    idn_f = din("idn_f", [128, 128])
    sel2 = din("sel2", [2, 256])
    triu_f = din("triu_f", [128, 128])
    triu_b = din("triu_b", [128, 128], BF16)
    out_d = nc.dram_tensor("out", [T, D], F32, kind="ExternalOutput").ap()
    if DEBUG_DUMP:
        dbg_xqT = nc.dram_tensor("dbg_xqT", [128, 8 * 2050], F32, kind="ExternalOutput").ap()
        dbg_sful = nc.dram_tensor("dbg_sful", [128, 32 * FT], F32, kind="ExternalOutput").ap()
        dbg_s2B = nc.dram_tensor("dbg_s2B", [128, FT], F32, kind="ExternalOutput").ap()
        dbg_cols = nc.dram_tensor("dbg_cols", [128, 10], F32, kind="ExternalOutput").ap()
        dbg_w1q = nc.dram_tensor("dbg_w1q", [32, 128, 8, 128], F32, kind="ExternalOutput").ap()
        dbg_hh = nc.dram_tensor("dbg_hh", [128, 32 * 16], F32, kind="ExternalOutput").ap()

    w1q_d = nc.dram_tensor("w1q_d", [32, 128, 8, 128], BF16, kind="Internal").ap()
    w2q_d = nc.dram_tensor("w2q_d", [DH, D], BF16, kind="Internal").ap()
    if n_sub >= 2:
        q_sp = nc.dram_tensor("q_sp", [16, 128, 8, 128], BF16, kind="Internal").ap()
        k_sp = nc.dram_tensor("k_sp", [16, 128, D], BF16, kind="Internal").ap()
        v_sp = nc.dram_tensor("v_sp", [16, 128, D], BF16, kind="Internal").ap()
        o_sp = nc.dram_tensor("o_sp", [16, 128, D], BF16, kind="Internal").ap()
        cc1_in = nc.dram_tensor("cc1_in", [66, D], F32, kind="Internal").ap()
        cc1_out = nc.dram_tensor("cc1_out", [4 * 66, D], F32, kind="Internal").ap()
    if n_sub >= 3:
        cc2_in = nc.dram_tensor("cc2_in", [2, D], F32, kind="Internal").ap()
        cc2_out = nc.dram_tensor("cc2_out", [8, D], F32, kind="Internal").ap()
    if n_sub >= 4:
        w1q8_d = nc.dram_tensor("w1q8_d", [32, 128, 8, 128], BF16, kind="Internal").ap()
        w2q8_d = nc.dram_tensor("w2q8_d", [DH, D], BF16, kind="Internal").ap()
    if n_sub >= 5:
        cc3_in = nc.dram_tensor("cc3_in", [66, D], F32, kind="Internal").ap()
        cc3_out = nc.dram_tensor("cc3_out", [4 * 66, D], F32, kind="Internal").ap()

    stack = contextlib.ExitStack()
    with stack:
        S = stack.enter_context

        def sb(name, shape, dt=F32):
            return S(nc.sbuf_tensor(name, shape, dt))

        xres = sb("xres", [128, 18, D])
        cols = sb("cols", [128, 64])
        scol = sb("scol", [128, 16])
        hcols = sb("hcols", [2, 16])
        ones_row = sb("ones_row", [1, 128])
        c_idb = sb("c_idb", [128, 128], BF16)
        c_idf = sb("c_idf", [128, 128])
        c_sel2 = sb("c_sel2", [2, 256])
        c_trf = sb("c_trf", [128, 128])
        c_trb = sb("c_trb", [128, 128], BF16)
        xh = sb("xh", [128, 2, D])
        xqt = sb("xqt", [128, 2, D], BF16)
        t1b = sb("t1b", [128, D])
        sqdump = sb("sqdump", [128, D])
        cwt = sb("cwt", [128, 32, 3])
        b1c32 = sb("b1c32", [128, 32])
        b2row = t1b[0:1, :]
        cbt = sb("cbt", [128, 32])
        halo_rows = xres[0:2, 17, :]

        sems = {}
        blk = S(nc.Block())
        for e in ENGINES:
            sems[e] = S(nc.semaphore("sem_" + e))
        sems["ccsem"] = S(nc.semaphore("ccsem"))
        P = Sched()
        E, W, DR = P.E, P.W, P.DR

        slot = list(range(NT))
        spare = [16]

        E("sync", 16, "dma_start", out=c_idb[:, :], in_=idn_bf[:, :])
        E("sync", 16, "dma_start", out=c_idf[:, :], in_=idn_f[:, :])
        E("sync", 16, "dma_start", out=c_sel2[:, :], in_=sel2[:, :])
        E("sync", 16, "dma_start", out=c_trf[:, :], in_=triu_f[:, :])
        E("sync", 16, "dma_start", out=c_trb[:, :], in_=triu_b[:, :])
        for i in range(NT):
            E("sync", 16, "dma_start", out=xres[:, i, :], in_=x_in[1 + 128 * i:129 + 128 * i, :])
        E("sync", 16, "dma_start", out=halo_rows[0:1, :], in_=x_in[0:1, :])
        E("sync", 16, "dma_start", out=halo_rows[1:2, :], in_=x_in[T + 1:T + 2, :])
        for j in range(32):
            E("sync", 16, "dma_start", out=cwt[:, j, :], in_=convw[128 * j:128 * (j + 1), :])
            E("sync", 16, "dma_start", out=cbt[:, j:j + 1], in_=convb[128 * j:128 * (j + 1), :])
        tok_x = P.cnt["sync"]
        tok_ones = E("vector", 1, "memset", ones_row[0:1, :], 1.0)

        stackQ = contextlib.ExitStack()
        p_sc = [stackQ.enter_context(nc.psum_tensor(f"p_sc{i}", [1, 1], F32)) for i in range(2)]
        p_bc = [stackQ.enter_context(nc.psum_tensor(f"p_bc{i}", [128, 1], F32)) for i in range(2)]
        qm = stackQ.enter_context(nc.sbuf_tensor("qm", [128, 2, D], F32))
        qsg = stackQ.enter_context(nc.sbuf_tensor("qsg", [128, 2, D], F32))

        wq_chunks = []
        for j in range(8):
            for q in range(4):
                wq_chunks.append(("w1", w1T[128 * j:128 * (j + 1), 1024 * q:1024 * (q + 1)], (j, q)))
        for j in range(32):
            wq_chunks.append(("w2", w2T[128 * j:128 * (j + 1), :], (j,)))

        nwq = len(wq_chunks)
        sync_base = P.cnt["sync"]
        vec_base = P.cnt["vector"] + 4
        tok_ld = [sync_base + 16 * (i + 1) for i in range(nwq)]
        tok_red = [vec_base + 2 * (i + 1) for i in range(nwq)]
        for i, (w, src, _) in enumerate(wq_chunks):
            if i >= 2:
                W("sync", "vector", tok_red[i - 2])
            tok_ld2 = E("sync", 16, "dma_start", out=xh[:, i % 2, :], in_=src)
            assert tok_ld2 == tok_ld[i]
        E("vector", 1, "memset", cols[:, 0:1], 0.0)
        E("vector", 1, "memset", cols[:, 1:2], 0.0)
        E("vector", 1, "memset", cols[:, 2:3], 0.0)
        tok_msets = E("vector", 1, "memset", cols[:, 3:4], 0.0)
        W("vector", "vector", tok_msets)
        pp = {"w1": 0, "w2": 0}
        acc_final = {}
        for i, (w, src, _) in enumerate(wq_chunks):
            W("vector", "sync", tok_ld[i])
            E("vector", 1, "tensor_reduce", cols[:, 4:5], xh[:, i % 2, :], AX.X, ALU.add,
              apply_absolute_value=True)
            DR("vector")
            base = 0 if w == "w1" else 2
            a, b = base + pp[w] % 2, base + (pp[w] + 1) % 2
            pp[w] += 1
            tok_red2 = E("vector", 1, "tensor_tensor", cols[:, b:b + 1],
                         cols[:, a:a + 1], cols[:, 4:5], ALU.add)
            DR("vector")
            assert tok_red2 == tok_red[i]
            acc_final[w] = b
        tok_ocol = E("vector", 1, "memset", cols[:, 5:6], 1.0)
        tok_sums = P.cnt["vector"]
        W("tensor", "vector", tok_sums)
        E("tensor", 1, "matmul", p_sc[0][:, :], cols[:, acc_final["w1"]:acc_final["w1"] + 1],
          cols[:, 5:6], start=True, stop=True)
        tok_tot = E("tensor", 1, "matmul", p_sc[1][:, :],
                    cols[:, acc_final["w2"]:acc_final["w2"] + 1], cols[:, 5:6],
                    start=True, stop=True)
        W("vector", "tensor", tok_tot)
        E("vector", 1, "tensor_scalar", cols[0:1, 6:7], p_sc[0][:, :],
          1.0 / (D * DH), 1e-5, ALU.mult, ALU.max)
        tok_invs = E("vector", 1, "tensor_scalar", cols[0:1, 7:8], p_sc[1][:, :],
                     1.0 / (D * DH), 1e-5, ALU.mult, ALU.max)
        DR("vector")
        W("tensor", "vector", tok_invs)
        E("tensor", 1, "matmul", p_bc[0][:, :], ones_row[0:1, :], cols[0:1, 6:7],
          start=True, stop=True)
        tok_bc = E("tensor", 1, "matmul", p_bc[1][:, :], ones_row[0:1, :], cols[0:1, 7:8],
                   start=True, stop=True)
        W("vector", "tensor", tok_bc)
        E("vector", 1, "tensor_copy", cols[:, 48:49], p_bc[0][:, :])
        E("vector", 1, "tensor_copy", cols[:, 49:50], p_bc[1][:, :])
        E("vector", 1, "tensor_scalar", cols[:, 50:51], p_bc[0][:, :], 0.5, None, ALU.mult)
        tok_thr = E("vector", 1, "tensor_scalar", cols[:, 51:52], p_bc[1][:, :],
                    0.5, None, ALU.mult)
        DR("vector")
        invs1_col, invs2_col = cols[:, 48:49], cols[:, 49:50]
        thrcol = {"w1": cols[:, 50:51], "w2": cols[:, 51:52]}

        s_base = P.cnt["scalar"]
        v2_base = P.cnt["vector"]
        g_base = P.cnt["gpsimd"]
        sy_base = P.cnt["sync"]
        tok_qld = [sy_base + 16 * (i + 1) for i in range(nwq)]
        tok_qsc = [s_base + (i + 1) for i in range(nwq)]
        tok_qv = [v2_base + (i + 1) for i in range(nwq)]
        tok_qst = []
        acc = g_base
        for i, (w, src, _) in enumerate(wq_chunks):
            acc += 16 * (8 if w == "w1" else 1)
            tok_qst.append(acc)
        for i, (w, src, _) in enumerate(wq_chunks):
            if i >= 2:
                W("sync", "scalar", tok_qsc[i - 2])
            t_ = E("sync", 16, "dma_start", out=xh[:, i % 2, :], in_=src)
            assert t_ == tok_qld[i]
        for i, (w, src, _) in enumerate(wq_chunks):
            W("scalar", "sync", tok_qld[i])
            if i >= 2:
                W("scalar", "vector", tok_qv[i - 2])
            E("scalar", 0, "activation", qm[:, i % 2, :], xh[:, i % 2, :], AF.Abs)
            t_ = E("scalar", 1, "activation", qsg[:, i % 2, :], xh[:, i % 2, :], AF.Sign)
            assert t_ == tok_qsc[i]
        for i, (w, src, _) in enumerate(wq_chunks):
            W("vector", "scalar", tok_qsc[i])
            if i == 0:
                W("vector", "vector", tok_thr)
            if i >= 2:
                W("vector", "gpsimd", tok_qst[i - 2])
            E("vector", 0, "tensor_scalar", t1b[:, :], qm[:, i % 2, :], thrcol[w], None, ALU.is_gt)
            t_ = E("vector", 1, "tensor_tensor", xqt[:, i % 2, :],
                   t1b[:, :], qsg[:, i % 2, :], ALU.mult)
            assert t_ == tok_qv[i]
        for i, (w, src, key) in enumerate(wq_chunks):
            W("gpsimd", "vector", tok_qv[i])
            if w == "w1":
                j, q = key
                for t in range(8):
                    E("gpsimd", 16, "dma_start",
                      out=w1q_d[8 * q + t, :, j, :], in_=xqt[:, i % 2, 128 * t:128 * (t + 1)])
                assert P.cnt["gpsimd"] == tok_qst[i]
            else:
                (j,) = key
                t_ = E("gpsimd", 16, "dma_start",
                       out=w2q_d[128 * j:128 * (j + 1), :], in_=xqt[:, i % 2, :])
                assert t_ == tok_qst[i]

        P.barrier()
        stackQ.close()

        def emit_l0_ffn(pfx):
            sub = contextlib.ExitStack()
            with sub:
                SS = sub.enter_context

                def ssb(name, shape, dt=F32):
                    return SS(nc.sbuf_tensor(pfx + name, shape, dt))

                xqT = ssb("xqT", [128, 8 * 2052], BF16)
                sful = ssb("sful", [128, 32, FT], BF16)
                w1c = ssb("w1c", [128, 2, 8, 128], BF16)
                w2c = ssb("w2c", [128, 2, D], BF16)
                hch = ssb("hch", [128, 2, FT + 2], BF16)
                ybuf = ssb("ybuf", [128, 2, FT])
                absb = ssb("absb", [128, FT])
                tqb = ssb("tqb", [128, FT])
                m1 = ssb("m1", [128, 2, FT])
                m1bf = ssb("m1bf", [128, FT], BF16)
                s2B = ssb("s2B", [128, FT])
                s2tm = ssb("s2tm", [128, 16])
                rec16 = ssb("rec16", [128, 16])
                s2rows = ssb("s2rows", [2, 128])
                sqc = ssb("sqc", [128, 2, FT], BF16)
                hh_halo = ssb("hh_halo", [128, 32, 16], BF16)

                stackA = contextlib.ExitStack()
                p_tr = [stackA.enter_context(nc.psum_tensor(pfx + f"ptr{i}", [128, 128], BF16))
                        for i in range(2)]
                p_h2 = [stackA.enter_context(nc.psum_tensor(pfx + f"ph2{i}", [128, 2], F32))
                        for i in range(2)]

                def xqa(d, a, b, step=None):
                    if step is None:
                        return xqT[:, 2052 * d + a:2052 * d + b]
                    return xqT[:, 2052 * d + a:2052 * d + b:step]


                tok_xq = [0] * (NT + 1)
                tok_ev = []
                tr_last = [0] * (NT + 1)
                for i in range(NT + 1):
                    ht = (i == NT)
                    Pn = 2 if ht else 128
                    src = halo_rows if ht else xres[:, slot[i], :]
                    ccl = hcols if ht else cols
                    scl = hcols if ht else scol
                    cof = 0 if ht else 52
                    sof = 8 if ht else 0
                    xhb = xh[0:Pn, i % 2, :]
                    xqb = xqt[0:Pn, i % 2, :]
                    t1 = t1b[0:Pn, :]
                    dump = sqdump[0:Pn, :]

                    def CL(j):
                        return ccl[0:Pn, cof + j:cof + j + 1]

                    def SL(j):
                        return scl[0:Pn, sof + j:sof + j + 1]

                    if i >= 2:
                        W("scalar", "vector", tok_xq[i - 2])
                    if i == 0:
                        W("scalar", "sync", tok_x)
                    tok_ms = E("scalar", 1, "activation", dump, src, AF.Identity,
                               accum_out=SL(0))
                    W("vector", "scalar", tok_ms)
                    E("vector", 1, "tensor_scalar", CL(0), SL(0), 1.0 / D, None, ALU.mult)
                    DR("vector")
                    tok_nm = E("vector", 1, "tensor_scalar", CL(1), CL(0), -1.0, None, ALU.mult)
                    W("scalar", "vector", tok_nm)
                    tok_ss = E("scalar", 1, "activation", dump, src, AF.Square,
                               bias=CL(1), scale=1.0, accum_out=SL(1))
                    W("vector", "scalar", tok_ss)
                    tok_ve = E("vector", 1, "tensor_scalar", CL(2), SL(1), 1.0 / D, 1e-5,
                               ALU.mult, ALU.add)
                    W("scalar", "vector", tok_ve)
                    E("scalar", 1, "activation", SL(2), CL(2), AF.Ln)
                    DR("scalar")
                    tok_rs = E("scalar", 1, "activation", SL(3), SL(2), AF.Exp, scale=-0.5)
                    W("vector", "scalar", tok_rs)
                    E("vector", 1, "tensor_copy", CL(3), SL(3))
                    DR("vector")
                    E("vector", 0, "tensor_scalar", xhb, src, CL(0), CL(3),
                      ALU.subtract, ALU.mult)
                    E("vector", 1, "tensor_reduce", CL(4), xhb, AX.X, ALU.max,
                      apply_absolute_value=True)
                    DR("vector")
                    tok_am = E("vector", 1, "tensor_scalar", CL(5), CL(4), 1e-5, None, ALU.max)
                    DR("vector")
                    W("scalar", "vector", tok_am)
                    E("scalar", 1, "activation", SL(4), CL(5), AF.Ln)
                    DR("scalar")
                    tok_s1 = E("scalar", 1, "activation", SL(5), SL(4), AF.Exp, scale=-1.0)
                    W("vector", "scalar", tok_s1)
                    E("vector", 1, "tensor_scalar", CL(6), SL(5), 127.0, None, ALU.mult)
                    E("vector", 1, "tensor_scalar", CL(7), CL(5), invs1_col[0:Pn, 0:1],
                      1.0 / 127.0, ALU.mult, ALU.mult)
                    DR("vector")
                    E("vector", 0, "tensor_scalar", t1, xhb, CL(6), MAGIC, ALU.mult, ALU.add)
                    if i >= 2:
                        W("vector", "tensor", tr_last[i - 2])
                    tok_xq[i] = E("vector", 1, "tensor_scalar", xqb, t1, MAGIC, CL(7),
                                  ALU.subtract, ALU.mult)
                    for d8 in range(8):
                        k8 = len(tok_ev)
                        W("tensor", "vector", tok_xq[i])
                        if k8 >= 2:
                            W("tensor", "scalar", tok_ev[k8 - 2])
                        if not ht:
                            tok_t = E("tensor", 1, "transpose", p_tr[k8 % 2][:, :],
                                      xqb[:, 128 * d8:128 * (d8 + 1)], c_idb[:, :])
                            W("scalar", "tensor", tok_t)
                            tok_ev.append(E("scalar", 1, "copy",
                                            xqa(d8, 1 + 128 * i, 129 + 128 * i),
                                            p_tr[k8 % 2][:, :]))
                            tr_last[i] = P.cnt["tensor"]
                        else:
                            tok_t = E("tensor", 1, "matmul", p_h2[k8 % 2][:, :],
                                      xqb[0:2, 128 * d8:128 * (d8 + 1)], c_idb[0:2, 0:2],
                                      start=True, stop=True)
                            W("scalar", "tensor", tok_t)
                            E("scalar", 0, "copy", xqa(d8, 0, 1), p_h2[k8 % 2][:, 0:1])
                            tok_ev.append(E("scalar", 1, "copy", xqa(d8, 2049, 2050),
                                            p_h2[k8 % 2][:, 1:2]))
                        tr_last[i] = P.cnt["tensor"]
                tok_xqT = tok_ev[-1]
                stackA.close()
                p_mm = [SS(nc.psum_tensor(pfx + f"pmm{i}", [128, FT], F32)) for i in range(2)]
                p_y = [SS(nc.psum_tensor(pfx + f"py{i}", [128, 512], F32)) for i in range(4)]
                stackH = contextlib.ExitStack()
                p_hh = stackH.enter_context(nc.psum_tensor(pfx + "phh", [128, 16], F32))

                xq_hal = ssb("xq_hal", [128, 8, 16], BF16)
                W("vector", "scalar", tok_xqT)
                for d8 in range(8):
                    E("vector", 0, "tensor_copy", xq_hal[:, d8, 0:8],
                      xqa(d8, 0, 1793, 256))
                    tok_halg = E("vector", 1, "tensor_copy", xq_hal[:, d8, 8:16],
                                 xqa(d8, 257, 2050, 256))

                w1_tok, w2_tok = {}, {}
                w1_use, w2_use = {}, {}
                conv_done, silu_done, m1_done, evac_done = {}, {}, {}, {}
                halo_done = {}
                yv_done = {}

                def load_w1(i, c):
                    k = i * 32 + c
                    t_use = w1_use.get(k - 2, 0)
                    if t_use:
                        W("sync", "tensor", t_use)
                    w1_tok[k] = E("sync", 16, "dma_start",
                                  out=w1c[:, k % 2, :, :], in_=w1q_d[c, :, :, :])

                def load_w2(i, c):
                    k = i * 32 + c
                    t_use = w2_use.get(k - 2, 0)
                    if t_use:
                        W("sync", "tensor", t_use)
                    w2_tok[k] = E("sync", 16, "dma_start",
                                  out=w2c[:, k % 2, :], in_=w2q_d[128 * c:128 * (c + 1), :])

                def mm1_tile(i):
                    for c in range(32):
                        k = i * 32 + c
                        load_w1(i, c)
                        par = k % 2
                        pm = p_mm[c % 2]
                        W("tensor", "sync", w1_tok[k])
                        if i == 0 and c == 0:
                            W("tensor", "scalar", tok_xqT)
                            W("tensor", "vector", tok_halg)
                        pe = evac_done.get((i, c - 2), 0)
                        if pe:
                            W("tensor", "scalar", pe)
                        for d8 in range(8):
                            E("tensor", 0 if d8 < 7 else 1, "matmul", pm[:, :],
                              w1c[:, par, d8, :], xqa(d8, 1 + FT * i, 1 + FT * i + FT),
                              start=(d8 == 0), stop=(d8 == 7))
                        tok_mm = P.cnt["tensor"]
                        if i == 0:
                            hd = halo_done.get(c - 1, 0)
                            if hd:
                                W("tensor", "scalar", hd)
                            for d8 in range(8):
                                E("tensor", 0 if d8 < 7 else 1, "matmul", p_hh[:, :],
                                  w1c[:, par, d8, :], xq_hal[:, d8, :],
                                  start=(d8 == 0), stop=(d8 == 7))
                        w1_use[k] = P.cnt["tensor"]
                        tok_hal = P.cnt["tensor"]
                        W("scalar", "tensor", tok_mm)
                        cd = conv_done.get((i, c - 2), 0)
                        if cd:
                            W("scalar", "vector", cd)
                        E("scalar", 0, "copy", hch[:, c % 2, 1:FT + 1], pm[:, :])
                        if i == 0:
                            W("scalar", "tensor", tok_hal)
                            halo_done[c] = E("scalar", 1, "copy", hh_halo[:, c, :], p_hh[:, :])
                            DR("scalar")
                        E("scalar", 0, "copy", hch[:, c % 2, 0:1], hh_halo[:, c, i:i + 1])
                        tok_hcopy = E("scalar", 1, "copy", hch[:, c % 2, FT + 1:FT + 2],
                                      hh_halo[:, c, 8 + i:9 + i])
                        evac_done[(i, c)] = tok_hcopy
                        DR("scalar")
                        tok_y1 = E("scalar", 1, "activation", ybuf[:, 0, :],
                                   hch[:, c % 2, 1:FT + 1], AF.Identity,
                                   bias=cbt[:, c:c + 1], scale=cwt[:, c, 1:2])
                        W("vector", "scalar", tok_y1)
                        E("vector", 0, "scalar_tensor_tensor", ybuf[:, 1, :],
                          hch[:, c % 2, 0:FT], cwt[:, c, 0:1], ybuf[:, 0, :],
                          ALU.mult, ALU.add)
                        tok_y3 = E("vector", 1, "scalar_tensor_tensor", ybuf[:, 0, :],
                                   hch[:, c % 2, 2:FT + 2], cwt[:, c, 2:3], ybuf[:, 1, :],
                                   ALU.mult, ALU.add)
                        conv_done[(i, c)] = tok_y3
                        W("scalar", "vector", tok_y3)
                        sqp = sq_done.get((i - 1, c), 0)
                        if sqp:
                            W("scalar", "vector", sqp)
                        tok_sil = E("scalar", 1, "activation", sful[:, c, :],
                                    ybuf[:, 0, :], AF.Silu)
                        silu_done[(i, c)] = tok_sil
                        tok_abs = E("scalar", 1, "activation", absb[:, :],
                                    sful[:, c, :], AF.Abs)
                        W("vector", "scalar", tok_abs)
                        if c == 0 and (i, "m1b") in m1_done:
                            W("vector", "vector", m1_done[(i, "m1b")])
                        if c == 0:
                            tms = E("vector", 1, "memset", m1[:, 1, :], 0.0)
                            W("vector", "vector", tms)
                        m1_done[(i, c)] = E("vector", 1, "tensor_tensor", m1[:, c % 2, :],
                                            m1[:, (c + 1) % 2, :], absb[:, :], ALU.max)

                prev_mm2 = {"d2": 0, "s2rw": 0, "s2B": 0}

                def mm2_tile(i):
                    W("vector", "vector", m1_done[(i, 31)])
                    tok_m1b = E("vector", 1, "tensor_copy", m1bf[:, :], m1[:, 1, :])
                    m1_done[(i + 1, "m1b")] = tok_m1b
                    ts_stack = contextlib.ExitStack()
                    p_trx = ts_stack.enter_context(
                        nc.psum_tensor(pfx + f"ptx{i}", [128, 128], BF16))
                    for t2 in range(2):
                        W("tensor", "vector", tok_m1b)
                        if prev_mm2["d2"]:
                            W("tensor", "vector", prev_mm2["d2"])
                        if t2 == 1:
                            W("tensor", "vector", P.cnt["vector"])
                        tok_tr2 = E("tensor", 1, "transpose", p_trx[:, :],
                                    m1bf[:, 128 * t2:128 * (t2 + 1)], c_idb[:, :])
                        W("vector", "tensor", tok_tr2)
                        E("vector", 1, "tensor_reduce", cols[:, 60 + t2:61 + t2],
                          p_trx[:, :], AX.X, ALU.max)
                        DR("vector")
                    E("vector", 1, "tensor_scalar", rec16[:, 0:2], cols[:, 60:62],
                      1e-5, None, ALU.max)
                    tms2 = E("vector", 1, "memset", rec16[:, 2:16], 1.0)
                    W("vector", "vector", tms2)
                    E("vector", 1, "reciprocal", s2tm[:, 0:16], rec16[:, 0:16])
                    DR("vector")
                    E("vector", 1, "tensor_scalar", s2tm[:, 2:4], s2tm[:, 0:2],
                      127.0, None, ALU.mult)
                    tok_d2 = E("vector", 1, "tensor_scalar", cols[:, 62:64], rec16[:, 0:2],
                               invs2_col, 0.5 / 127.0, ALU.mult, ALU.mult)
                    DR("vector")
                    prev_mm2["d2"] = tok_d2
                    ts_stack.close()
                    ts2 = contextlib.ExitStack()
                    p_s2r = ts2.enter_context(nc.psum_tensor(pfx + f"ps2r{i}", [2, 128], F32))
                    tok_s2c = P.cnt["vector"]
                    W("tensor", "vector", tok_s2c)
                    if prev_mm2["s2rw"]:
                        W("tensor", "vector", prev_mm2["s2rw"])
                    tok_s2r = E("tensor", 1, "matmul", p_s2r[:, :], s2tm[:, 2:4],
                                c_idf[:, :], start=True, stop=True)
                    W("vector", "tensor", tok_s2r)
                    tok_s2rw = E("vector", 1, "tensor_copy", s2rows[:, :], p_s2r[0:2, :])
                    prev_mm2["s2rw"] = tok_s2rw
                    ts2.close()
                    ts3 = contextlib.ExitStack()
                    p_s2x = ts3.enter_context(nc.psum_tensor(pfx + f"ps2x{i}", [128, 128], F32))
                    for t2 in range(2):
                        W("tensor", "vector", tok_s2rw)
                        if prev_mm2["s2B"]:
                            W("tensor", "scalar", prev_mm2["s2B"])
                        if t2 == 1:
                            W("tensor", "scalar", P.cnt["scalar"])
                        tok_s2b = E("tensor", 1, "matmul", p_s2x[:, :],
                                    c_sel2[0:2, 128 * t2:128 * (t2 + 1)], s2rows[:, :],
                                    start=True, stop=True)
                        W("scalar", "tensor", tok_s2b)
                        tok_s2B = E("scalar", 1, "copy", s2B[:, 128 * t2:128 * (t2 + 1)],
                                    p_s2x[:, :])
                    ts3.close()
                    prev_mm2["s2B"] = tok_s2B

                    pv = yv_done.get(i - 1, 0)
                    for c in range(32):
                        k = i * 32 + c
                        load_w2(i, c)
                        W("vector", "scalar", silu_done[(i, c)])
                        if c == 0:
                            W("vector", "scalar", tok_s2B)
                        pu = w2_use.get(k - 2, 0)
                        if pu:
                            W("vector", "tensor", pu)
                        E("vector", 0, "tensor_tensor", tqb[:, :], sful[:, c, :],
                          s2B[:, :], ALU.mult)
                        tok_sq = E("vector", 1, "tensor_scalar", sqc[:, c % 2, :],
                                   tqb[:, :], MAGIC, MAGIC, ALU.add, ALU.subtract)
                        sq_done[(i, c)] = tok_sq
                        W("tensor", "vector", tok_sq)
                        W("tensor", "sync", w2_tok[k])
                        if c == 0 and pv:
                            W("tensor", "vector", pv)
                        for tsub in range(2):
                            for oh in range(2):
                                E("tensor", 0 if (tsub, oh) != (1, 1) else 1, "matmul",
                                  p_y[2 * tsub + oh][:, :],
                                  sqc[:, c % 2, 128 * tsub:128 * (tsub + 1)],
                                  w2c[:, k % 2, 512 * oh:512 * (oh + 1)],
                                  start=(c == 0), stop=(c == 31))
                        w2_use[k] = P.cnt["tensor"]
                    tok_y = P.cnt["tensor"]
                    W("vector", "tensor", tok_y)
                    for tsub in range(2):
                        r = 2 * i + tsub
                        d2s = cols[:, 62 + tsub:63 + tsub]
                        E("vector", 0, "scalar_tensor_tensor", xres[:, spare[0], 0:512],
                          p_y[2 * tsub][:, :], d2s, xres[:, slot[r], 0:512],
                          ALU.mult, ALU.add)
                        tok_res = E("vector", 1, "scalar_tensor_tensor",
                                    xres[:, spare[0], 512:1024], p_y[2 * tsub + 1][:, :],
                                    d2s, xres[:, slot[r], 512:1024], ALU.mult, ALU.add)
                        slot[r], spare[0] = spare[0], slot[r]
                    yv_done[i] = P.cnt["vector"]

                sq_done = {}
                mm1_tile(0)
                stackH.close()
                mm2_tile(0)
                for i in range(1, NFT):
                    mm1_tile(i)
                    mm2_tile(i)
                if DEBUG_DUMP:
                    P.barrier()
                    for ddbg in range(8):
                        E("gpsimd", 16, "dma_start", out=dbg_xqT[:, 2050 * ddbg:2050 * (ddbg + 1)],
                          in_=xqT[:, 2052 * ddbg:2052 * ddbg + 2050])
                    E("gpsimd", 16, "dma_start", out=dbg_sful[:, :], in_=sful[:, :, :])
                    E("gpsimd", 16, "dma_start", out=dbg_s2B[:, :], in_=s2B[:, :])
                    E("gpsimd", 16, "dma_start", out=dbg_cols[:, 0:10], in_=cols[:, 48:58])
                    E("gpsimd", 16, "dma_start", out=dbg_hh[:, :], in_=hh_halo[:, :, :])
                    for cdbg in range(32):
                        E("gpsimd", 16, "dma_start", out=dbg_w1q[cdbg, :, :, :],
                          in_=w1q_d[cdbg, :, :, :])
                    P.barrier()


        def emit_attn(pfx, kind, W_qkvo, extra):
            WqT_d, WkT_d, WvT_d, WoT_d = W_qkvo
            sub = contextlib.ExitStack()
            with sub:
                SS = sub.enter_context

                def ssb(name, shape, dt=F32):
                    return SS(nc.sbuf_tensor(pfx + name, shape, dt))

                wbufA = ssb("wbufA", [128, 8, D], BF16)
                wbufB = ssb("wbufB", [128, 8, D], BF16)
                wtmp = ssb("wtmp", [128, D])
                xhT_r = ssb("xhT_r", [128, 1, 8, 256], BF16)
                qstg = ssb("qstg", [128, 8, 256], BF16)
                kstg = ssb("kstg", [128, 2, D], BF16)
                khstg = ssb("khstg", [128, 2, D], BF16)
                if kind == "gla":
                    ksq = ssb("ksq", [128, 16, 64])
                    kss = ssb("kss", [128, 16])
                    kss2 = ssb("kss2", [128, 16])
                qr = ssb("qr", [128, 4, 8, 128], BF16)
                khr = ssb("khr", [128, 4, D], BF16)
                vr = ssb("vr", [128, 4, D], BF16)
                khT = ssb("khT", [128, 1, 8, 128], BF16)
                ShHi = ssb("ShHi", [128, D], BF16)
                crow = ssb("crow", [1, 16])
                Or = ssb("Or", [128, 2, D], BF16)
                Pt = ssb("Pt", [128, 2, 128], BF16)
                Sa = ssb("Sa", [64, D]); Sb = ssb("Sb", [64, D])
                Sh = ssb("Sh", [64, D], BF16)
                Shin = ssb("Shin", [64, D], BF16)
                fga = ssb("fga", [4, 16])
                coefs = ssb("coefs", [4, 16])
                coefr = ssb("coefr", [1, 4, 16])
                coefB = ssb("coefB", [64, 4, 16])
                if kind == "gla":
                    fstore = ssb("fstore", [128, 16, 16])
                    Fst = ssb("Fst", [128, 16, 16])
                zst = ssb("zst", [128, 16])
                zt1 = ssb("zt1", [128, 16])
                zt2 = ssb("zt2", [128, 16])
                EexpT = ssb("EexpT", [128, 16])
                EinvT = ssb("EinvT", [128, 16])
                if kind == "gla":
                    cBs = ssb("cBs", [128, 16])
                    Ftl = ssb("Ftl", [128, 16])
                    EcB = ssb("EcB", [128, 16])
                    EendB = ssb("EendB", [128, 16])
                    ElamB = ssb("ElamB", [128, 16])
                    lam_s = ssb("lam_s", [1, 16])
                    labs = ssb("labs", [1, 2, 16])
                dstmp = ssb("dstmp", [64, 64])
                EA = ssb("EA", [128, 16])
                eab = ssb("eab", [128, 16])
                if kind == "gla":
                    g1f = ssb("g1f", [64, 8, 256], BF16)
                    wf_s = ssb("wf_s", [128, 8, 16], BF16)
                    wg1_s = ssb("wg1_s", [128, 8, 64], BF16)
                    wg2_s = ssb("wg2_s", [64, D], BF16)
                    lnwc = ssb("lnwc", [128, 8])
                if kind == "dec":
                    gamr = ssb("gamr", [1, 16])
                    lngB = ssb("lngB", [128, 16])
                    posc = ssb("posc", [128, 2])

                Sg = kstg
                Kt = khstg
                gatet = kstg
                ogb = xh
                ohT = khT
                is_gla = kind == "gla"

                if is_gla:
                    for d8 in range(8):
                        E("gpsimd", 16, "dma_start", out=wf_s[:, d8, :],
                          in_=extra["WfT"][128 * d8:128 * (d8 + 1), :])
                        E("gpsimd", 16, "dma_start", out=wg1_s[:, d8, :],
                          in_=extra["Wg1T"][128 * d8:128 * (d8 + 1), :])
                        E("gpsimd", 16, "dma_start", out=lnwc[:, d8:d8 + 1],
                          in_=extra["lnw"][128 * d8:128 * (d8 + 1), :])
                    E("gpsimd", 16, "dma_start", out=wg2_s[:, :], in_=extra["Wg2T"][:, :])
                else:
                    bq_s = ssb("bq_s", [128, 8])
                    bk_s = ssb("bk_s", [1, D])
                    bv_s = ssb("bv_s", [1, D])
                    bo_s = ssb("bo_s", [1, D])
                    E("gpsimd", 16, "dma_start", out=gamr[0:1, :], in_=extra["gam_row"][:, :])
                    E("gpsimd", 16, "dma_start", out=posc[:, 0:1], in_=extra["poscol"][:, :])
                    for d8 in range(8):
                        E("gpsimd", 16, "dma_start", out=bq_s[:, d8:d8 + 1],
                          in_=extra["bqc"][128 * d8:128 * (d8 + 1), :])
                    E("gpsimd", 16, "dma_start", out=bk_s[0:1, :], in_=extra["bkr"][:, :])
                    E("gpsimd", 16, "dma_start", out=bv_s[0:1, :], in_=extra["bvr"][:, :])
                    E("gpsimd", 16, "dma_start", out=bo_s[0:1, :], in_=extra["bor"][:, :])
                tok_wres = P.cnt["gpsimd"]

                stA = contextlib.ExitStack()
                p_tr2 = [stA.enter_context(nc.psum_tensor(pfx + f"ptr{i}", [128, 128], BF16))
                         for i in range(2)]
                p_kv = [stA.enter_context(nc.psum_tensor(pfx + f"pkv{i}", [128, 512], F32))
                        for i in range(2)]
                p_q = [stA.enter_context(nc.psum_tensor(pfx + f"pq{i}", [128, 256], F32))
                       for i in range(2)]
                p_z = stA.enter_context(nc.psum_tensor(pfx + "pz", [128, 16], F32))
                p_g1 = stA.enter_context(nc.psum_tensor(pfx + "pg1", [64, 256], F32))

                tok_xh = [0] * (NT + 20)
                tok_tr = [0] * (NT + 20)
                tr_lastA = [0] * (NT + 20)
                tok_wldA = {}
                tok_wuseA = {}
                tok_qdma = {}
                tok_kvdma = {}

                def ln_tile_plain(i, src_ap, dep_v=0):
                    Pn = 128
                    xhb = xh[:, i % 2, :]
                    xqb = xqt[:, i % 2, :]
                    dump = sqdump[:, :]

                    def CL(j):
                        return cols[:, 52 + j:53 + j]

                    def SL(j):
                        return scol[:, j:j + 1]

                    if i >= 2:
                        W("scalar", "vector", tok_xh[i - 2])
                    if dep_v:
                        W("scalar", "vector", dep_v)
                    tok_ms = E("scalar", 1, "activation", dump, src_ap, AF.Identity,
                               accum_out=SL(0))
                    W("vector", "scalar", tok_ms)
                    E("vector", 1, "tensor_scalar", CL(0), SL(0), 1.0 / D, None, ALU.mult)
                    DR("vector")
                    tok_nm = E("vector", 1, "tensor_scalar", CL(1), CL(0), -1.0, None, ALU.mult)
                    W("scalar", "vector", tok_nm)
                    tok_ss = E("scalar", 1, "activation", dump, src_ap, AF.Square,
                               bias=CL(1), scale=1.0, accum_out=SL(1))
                    W("vector", "scalar", tok_ss)
                    tok_ve = E("vector", 1, "tensor_scalar", CL(2), SL(1), 1.0 / D, 1e-5,
                               ALU.mult, ALU.add)
                    W("scalar", "vector", tok_ve)
                    E("scalar", 1, "activation", SL(2), CL(2), AF.Ln)
                    DR("scalar")
                    tok_rs = E("scalar", 1, "activation", SL(3), SL(2), AF.Exp, scale=-0.5)
                    W("vector", "scalar", tok_rs)
                    E("vector", 1, "tensor_copy", CL(3), SL(3))
                    DR("vector")
                    if i >= 2:
                        W("vector", "tensor", tr_lastA[i - 2])
                    tok_xh[i] = E("vector", 1, "tensor_scalar", xqb, src_ap, CL(0), CL(3),
                                  ALU.subtract, ALU.mult)
                    return tok_xh[i]

                evA = []
                fuse_tok = {}
                g1ev = {}
                projend = {}
                kv_prev = {}
                fz_copy = {}
                for t8 in range(NFT):
                    par = t8 % 2
                    for half in range(2):
                        i = 2 * t8 + half
                        tokx = ln_tile_plain(i, xres[:, slot[i], :])
                        for d8 in range(8):
                            kk = len(evA)
                            W("tensor", "vector", tokx)
                            if kk >= 2:
                                W("tensor", "scalar", evA[kk - 2])
                            tok_t = E("tensor", 1, "transpose", p_tr2[kk % 2][:, :],
                                      xqt[:, i % 2, 128 * d8:128 * (d8 + 1)], c_idb[:, :])
                            W("scalar", "tensor", tok_t)
                            if half == 0 and d8 == 0 and t8 >= 1:
                                W("scalar", "tensor", projend[t8 - 1])
                            evA.append(E("scalar", 1, "copy",
                                         xhT_r[:, 0, d8, 128 * half:128 * (half + 1)],
                                         p_tr2[kk % 2][:, :]))
                        tr_lastA[i] = P.cnt["tensor"]
                    tok_xhT = evA[-1]

                    pu = tok_wuseA.get(("q", t8 - 2), 0)
                    if pu:
                        W("gpsimd", "tensor", pu)
                    for d8 in range(8):
                        tok_wldA[("q", t8)] = E("gpsimd", 16, "dma_start",
                                                out=wbufA[:, d8, :],
                                                in_=WqT_d[128 * d8:128 * (d8 + 1), :])
                    for oc in range(8):
                        W("tensor", "gpsimd", tok_wldA[("q", t8)])
                        W("tensor", "scalar", tok_xhT)
                        if oc >= 2:
                            W("tensor", "scalar", tok_qev[oc - 2])
                        elif t8 > 0:
                            W("tensor", "gpsimd", tok_qdma[t8 - 1])
                        if oc == 0:
                            tok_qev = [0] * 8
                        for d8 in range(8):
                            E("tensor", 0 if d8 < 7 else 1, "matmul", p_q[oc % 2][:, :],
                              wbufA[:, d8, 128 * oc:128 * (oc + 1)],
                              xhT_r[:, 0, d8, :], start=(d8 == 0), stop=(d8 == 7))
                        tok_mmq = P.cnt["tensor"]
                        W("scalar", "tensor", tok_mmq)
                        for half in range(2):
                            fn = AF.Silu if is_gla else AF.Identity
                            kwq = {} if is_gla else {"bias": bq_s[:, oc:oc + 1]}
                            tok_qev[oc] = E("scalar", 1, "activation",
                                            qstg[:, oc, 128 * half:128 * (half + 1)],
                                            p_q[oc % 2][:, 128 * half:128 * (half + 1)],
                                            fn, **kwq)
                    tok_wuseA[("q", t8)] = P.cnt["tensor"]
                    W("gpsimd", "scalar", tok_qev[7])
                    E("gpsimd", 16, "dma_start", out=q_sp[2 * t8, :, :, :],
                      in_=qstg[:, :, 0:128])
                    tok_qdma[t8] = E("gpsimd", 16, "dma_start", out=q_sp[2 * t8 + 1, :, :, :],
                                     in_=qstg[:, :, 128:256])

                    pu = tok_wuseA.get(("k", t8 - 2), 0)
                    if pu:
                        W("gpsimd", "tensor", pu)
                    for d8 in range(8):
                        tok_wldA[("k", t8)] = E("gpsimd", 16, "dma_start",
                                                out=wbufB[:, d8, :],
                                                in_=WkT_d[128 * d8:128 * (d8 + 1), :])
                    tok_kev = {}
                    for st2 in range(2):
                        for oh in range(2):
                            W("tensor", "gpsimd", tok_wldA[("k", t8)])
                            pidx = 2 * st2 + oh
                            if pidx >= 2:
                                W("tensor", "scalar", tok_kev[(pidx // 2 - 1, pidx % 2)])
                            elif t8 >= 1:
                                W("tensor", "scalar", kv_prev.get(("k", t8 - 1), 0))
                            pk = tok_kvdma.get(("k", t8 - 1), 0)
                            if pk and st2 == 0 and oh == 0:
                                W("tensor", "gpsimd", pk)
                            if not is_gla:
                                W("tensor", "vector", tok_ones)
                                W("tensor", "gpsimd", tok_wres)
                                E("tensor", 0, "matmul", p_kv[(2 * st2 + oh) % 2][:, :],
                                  ones_row[0:1, :], bk_s[0:1, 512 * oh:512 * (oh + 1)],
                                  start=True, stop=False)
                            for d8 in range(8):
                                E("tensor", 0 if d8 < 7 else 1, "matmul",
                                  p_kv[(2 * st2 + oh) % 2][:, :],
                                  xhT_r[:, 0, d8, 128 * st2:128 * (st2 + 1)],
                                  wbufB[:, d8, 512 * oh:512 * (oh + 1)],
                                  start=(d8 == 0 and is_gla), stop=(d8 == 7))
                            tok_mmk = P.cnt["tensor"]
                            W("scalar", "tensor", tok_mmk)
                            tok_kev[(st2, oh)] = E("scalar", 1, "activation",
                                                   kstg[:, st2, 512 * oh:512 * (oh + 1)],
                                                   p_kv[(2 * st2 + oh) % 2][:, :],
                                                   AF.Silu if is_gla else AF.Identity)
                    tok_wuseA[("k", t8)] = P.cnt["tensor"]
                    kv_prev[("k", t8)] = tok_kev[(1, 1)]
                    for st2 in range(2):
                        ch = 2 * t8 + st2
                        if is_gla:
                            W("scalar", "scalar", tok_kev[(st2, 1)])
                            W("scalar", "vector", fuse_tok.get(2 * t8 + st2 - 2, 0))
                            tok_sq = E("scalar", 1, "activation", ksq[:, :, :],
                                       kstg[:, st2, :], AF.Square)
                            W("vector", "scalar", tok_sq)
                            E("vector", 1, "tensor_reduce", kss[:, :], ksq[:, :, :],
                              AX.X, ALU.add)
                            DR("vector")
                            tok_kc = E("vector", 1, "tensor_scalar", kss2[:, :], kss[:, :],
                                       1e-24, None, ALU.max)
                            W("scalar", "vector", tok_kc)
                            E("scalar", 1, "activation", zt1[:, :], kss2[:, :], AF.Ln)
                            DR("scalar")
                            tok_sqr = E("scalar", 1, "activation", zt2[:, :], zt1[:, :],
                                        AF.Exp, scale=0.5)
                            W("vector", "scalar", tok_sqr)
                            E("vector", 1, "tensor_scalar", kss[:, :], zt2[:, :],
                              1e-12, None, ALU.max)
                            DR("vector")
                            E("vector", 1, "reciprocal", kss2[:, :], kss[:, :])
                            DR("vector")
                            for hh in range(16):
                                E("vector", 0 if hh < 15 else 1, "tensor_scalar",
                                  khstg[:, st2, 64 * hh:64 * (hh + 1)],
                                  kstg[:, st2, 64 * hh:64 * (hh + 1)],
                                  kss2[:, hh:hh + 1], None, ALU.mult)
                            tok_kh = P.cnt["vector"]
                            W("gpsimd", "vector", tok_kh)
                            tok_kvdma[("k", t8)] = E("gpsimd", 16, "dma_start",
                                                     out=k_sp[ch, :, :], in_=khstg[:, st2, :])
                        else:
                            W("gpsimd", "scalar", tok_kev[(st2, 1)])
                            tok_kvdma[("k", t8)] = E("gpsimd", 16, "dma_start",
                                                     out=k_sp[ch, :, :], in_=kstg[:, st2, :])

                    pu = tok_wuseA.get(("q", t8), 0)
                    W("gpsimd", "tensor", pu)
                    for d8 in range(8):
                        tok_wldA[("v", t8)] = E("gpsimd", 16, "dma_start",
                                                out=wbufA[:, d8, :],
                                                in_=WvT_d[128 * d8:128 * (d8 + 1), :])
                    tok_vev = {}
                    for st2 in range(2):
                        for oh in range(2):
                            W("tensor", "gpsimd", tok_wldA[("v", t8)])
                            pidx = 2 * st2 + oh
                            if pidx >= 2:
                                W("tensor", "scalar", tok_vev[(pidx // 2 - 1, pidx % 2)])
                            else:
                                W("tensor", "scalar", tok_kev[(1, pidx)])
                            pk = tok_kvdma.get(("v", t8 - 1), 0)
                            if pk and st2 == 0 and oh == 0:
                                W("tensor", "gpsimd", pk)
                            if not is_gla:
                                W("tensor", "gpsimd", tok_wres)
                                E("tensor", 0, "matmul", p_kv[(2 * st2 + oh) % 2][:, :],
                                  ones_row[0:1, :], bv_s[0:1, 512 * oh:512 * (oh + 1)],
                                  start=True, stop=False)
                            for d8 in range(8):
                                E("tensor", 0 if d8 < 7 else 1, "matmul",
                                  p_kv[(2 * st2 + oh) % 2][:, :],
                                  xhT_r[:, 0, d8, 128 * st2:128 * (st2 + 1)],
                                  wbufA[:, d8, 512 * oh:512 * (oh + 1)],
                                  start=(d8 == 0 and is_gla), stop=(d8 == 7))
                            tok_mmv = P.cnt["tensor"]
                            W("scalar", "tensor", tok_mmv)
                            W("scalar", "gpsimd", tok_kvdma[("k", t8)])
                            tok_vev[(st2, oh)] = E("scalar", 1, "activation",
                                                   kstg[:, st2, 512 * oh:512 * (oh + 1)]
                                                   if False else khstg[:, st2, 512 * oh:512 * (oh + 1)],
                                                   p_kv[(2 * st2 + oh) % 2][:, :],
                                                   AF.Silu if is_gla else AF.Identity)
                        ch = 2 * t8 + st2
                        W("gpsimd", "scalar", tok_vev[(st2, 1)])
                        tok_kvdma[("v", t8)] = E("gpsimd", 16, "dma_start",
                                                 out=v_sp[ch, :, :], in_=khstg[:, st2, :])
                    tok_wuseA[("v", t8)] = P.cnt["tensor"]

                    if is_gla:
                        for st2 in range(2):
                            ch = 2 * t8 + st2
                            W("tensor", "gpsimd", tok_wres)
                            if ch >= 1:
                                W("tensor", "scalar", fz_copy.get(ch - 1, 0))
                            for d8 in range(8):
                                E("tensor", 0 if d8 < 7 else 1, "matmul", p_z[:, :],
                                  xhT_r[:, 0, d8, 128 * st2:128 * (st2 + 1)],
                                  wf_s[:, d8, :], start=(d8 == 0), stop=(d8 == 7))
                            tok_zmm = P.cnt["tensor"]
                            W("scalar", "tensor", tok_zmm)
                            if ch >= 1:
                                W("scalar", "vector", fuse_tok.get(ch - 1, 0))
                            fz_copy[ch] = E("scalar", 1, "copy", zst[:, :], p_z[:, :])
                            DR("scalar")
                            E("scalar", 1, "activation", zt1[:, :], zst[:, :], AF.Abs)
                            DR("scalar")
                            E("scalar", 1, "activation", zt2[:, :], zt1[:, :], AF.Exp,
                              scale=-1.0)
                            DR("scalar")
                            tok_lg = E("scalar", 1, "activation", zt1[:, :], zt2[:, :],
                                       AF.Ln, bias=1.0)
                            W("vector", "scalar", tok_lg)
                            fuse_tok[ch] = E("vector", 1, "scalar_tensor_tensor",
                                             fstore[:, ch, :], zst[:, :], 0.0,
                                             zt1[:, :], ALU.min, ALU.subtract)
                        W("tensor", "gpsimd", tok_wres)
                        if t8 >= 1:
                            W("tensor", "scalar", g1ev.get(t8 - 1, 0))
                        for d8 in range(8):
                            E("tensor", 0 if d8 < 7 else 1, "matmul", p_g1[:, :],
                              wg1_s[:, d8, :], xhT_r[:, 0, d8, :],
                              start=(d8 == 0), stop=(d8 == 7))
                        tok_g1 = P.cnt["tensor"]
                        W("scalar", "tensor", tok_g1)
                        g1ev[t8] = E("scalar", 1, "copy", g1f[:, t8, :], p_g1[:, :])
                    projend[t8] = P.cnt["tensor"]

                tok_phaseA_t = P.cnt["tensor"]
                tok_phaseA_g = P.cnt["gpsimd"]
                P.barrier()
                stA.close()

                stB = contextlib.ExitStack()
                pSC2 = [stB.enter_context(nc.psum_tensor(pfx + f"pSC{i}", [128, 128], F32))
                        for i in range(2)]
                pOV2 = [stB.enter_context(nc.psum_tensor(pfx + f"pOV{i}", [128, 64], F32))
                        for i in range(2)]
                pDS = stB.enter_context(nc.psum_tensor(pfx + "pDS", [64, 64], F32))
                pF2 = stB.enter_context(nc.psum_tensor(pfx + "pF2", [128, 16], F32))
                pB1 = stB.enter_context(nc.psum_tensor(pfx + "pB1", [1, 16], F32))
                pTRb = stB.enter_context(nc.psum_tensor(pfx + "pTRb", [128, 128], BF16))

                tmsS = E("vector", 1, "memset", Sa[:, :], 0.0)
                W("vector", "vector", tmsS)
                if not is_gla:
                    W("scalar", "gpsimd", tok_wres)
                    E("scalar", 1, "activation", zt1[0:1, :], gamr[0:1, :], AF.Ln)
                    DR("scalar")
                    tok_lgm = P.cnt["scalar"]
                    W("tensor", "scalar", tok_lgm)
                    W("tensor", "vector", tok_ones)
                    tok_lgb = E("tensor", 1, "matmul", pF2[:, :], ones_row[0:1, :],
                                zt1[0:1, :], start=True, stop=True)
                    W("vector", "tensor", tok_lgb)
                    E("vector", 1, "tensor_copy", lngB[:, :], pF2[:, :])
                    DR("vector")
                if is_gla:
                    tmsL = E("vector", 1, "memset", labs[0:1, 0, :], 0.0)
                    W("vector", "vector", tmsL)
                    tok_lab = tmsL
                else:
                    tok_lab = 0

                chunk_done = {}
                odma = {}
                fcopy_tok = {}
                khT_ev = {}
                shhi_use = {}
                prev_E3 = 0
                prev_Fv = 0
                ring_ld = {}
                vtok_P = 0
                vtok_O = 0
                vtok_S = 0
                for ch in range(DBG_NCH):
                    t8 = ch // 2
                    W("sync", "gpsimd", tok_qdma.get(t8, 0))
                    W("sync", "gpsimd", tok_phaseA_g)
                    cdp = chunk_done.get(ch - 4, 0)
                    if cdp:
                        W("sync", "tensor", cdp)
                        W("sync", "vector", chunk_done.get((ch - 4, "v"), 0))
                    E("sync", 16, "dma_start", out=qr[:, ch % 4, :, :], in_=q_sp[ch, :, :, :])
                    E("sync", 16, "dma_start", out=khr[:, ch % 4, :], in_=k_sp[ch, :, :])
                    ring_ld[ch] = E("sync", 16, "dma_start", out=vr[:, ch % 4, :],
                                    in_=v_sp[ch, :, :])

                    if is_gla:
                        W("tensor", "vector", fuse_tok[ch])
                        if ch >= 1:
                            W("tensor", "scalar", fcopy_tok[ch - 1])
                            W("tensor", "scalar", prev_E3)
                            W("tensor", "vector", prev_Fv)
                        tok_F = E("tensor", 1, "matmul", pF2[:, :], c_trf[:, :],
                                  fstore[:, ch, :], start=True, stop=True)
                        W("tensor", "vector", tok_ones)
                        tok_lam = E("tensor", 1, "matmul", pB1[:, :], cols[:, 5:6],
                                    fstore[:, ch, :], start=True, stop=True)
                        W("scalar", "tensor", tok_F)
                        fcopy_tok[ch] = E("scalar", 1, "copy", Fst[:, ch, :], pF2[:, :])
                        W("gpsimd", "scalar", fcopy_tok[ch])
                        tok_crow = E("gpsimd", 16, "dma_start", out=crow[0:1, :],
                                     in_=Fst[64:65, ch, :])
                        W("tensor", "gpsimd", tok_crow)
                        W("tensor", "vector", tok_ones)
                        tok_cB = E("tensor", 1, "matmul", pF2[:, :], ones_row[0:1, :],
                                   crow[0:1, :], start=True, stop=True)
                        W("vector", "tensor", tok_cB)
                        E("vector", 1, "tensor_copy", cBs[:, :], pF2[:, :])
                        W("vector", "tensor", tok_lam)
                        E("vector", 1, "tensor_copy", lam_s[0:1, :], pB1[0:1, :])
                        DR("vector")
                        E("vector", 1, "tensor_tensor", Ftl[:, :], Fst[:, ch, :],
                          cBs[:, :], ALU.subtract)
                        E("vector", 1, "tensor_tensor", fga[0:1, :], lam_s[0:1, :],
                          cBs[0:1, :], ALU.subtract)
                        E("vector", 1, "tensor_copy", t1b[0:1, 16 * ch:16 * (ch + 1)],
                          labs[0:1, ch % 2, :])
                        DR("vector")
                        tok_lab = E("vector", 1, "tensor_tensor", labs[0:1, (ch + 1) % 2, :],
                                    labs[0:1, ch % 2, :], lam_s[0:1, :], ALU.add)
                        tok_Ft = P.cnt["vector"]
                        W("scalar", "vector", tok_Ft)
                        if ch >= 1:
                            W("scalar", "vector", chunk_done[(ch - 1, "v")])
                        E("scalar", 1, "activation", EexpT[:, :], Ftl[:, :], AF.Exp)
                        E("scalar", 1, "activation", EinvT[:, :], Ftl[:, :], AF.Exp, scale=-1.0)
                        tok_E1 = E("scalar", 1, "activation", EcB[:, :], cBs[:, :], AF.Exp)
                        W("tensor", "vector", tok_Ft)
                        if True:
                            tok_eb = E("tensor", 1, "matmul", pF2[:, :], ones_row[0:1, :],
                                       fga[0:1, :], start=True, stop=True)
                        W("scalar", "tensor", tok_eb)
                        tok_E2 = E("scalar", 1, "activation", EendB[:, :], pF2[:, :], AF.Exp)
                        W("tensor", "scalar", tok_E2)
                        tok_lb = E("tensor", 1, "matmul", pF2[:, :], ones_row[0:1, :],
                                   lam_s[0:1, :], start=True, stop=True)
                        W("scalar", "tensor", tok_lb)
                        tok_E3 = E("scalar", 1, "activation", ElamB[:, :], pF2[:, :], AF.Exp)
                        prev_E3 = tok_E3
                        prev_Fv = tok_Ft
                        tok_E = tok_E3
                    else:
                        E("vector", 1, "tensor_scalar", posc[:, 1:2], posc[:, 0:1],
                          float(128 * ch), None, ALU.add)
                        DR("vector")
                        tok_g16 = E("vector", 1, "tensor_scalar", zt2[:, :], lngB[:, :],
                                    posc[:, 1:2], None, ALU.mult)
                        if ch >= 1:
                            W("scalar", "vector", chunk_done[(ch - 1, "v")])
                        W("scalar", "vector", tok_g16)
                        tok_E = E("scalar", 1, "activation", EexpT[:, :], zt2[:, :], AF.Exp)

                    Sprev = Sa if ch % 2 == 0 else Sb
                    Snew = Sb if ch % 2 == 0 else Sa
                    W("vector", "scalar", tok_E)
                    if is_gla:
                        for hh in range(16):
                            E("vector", 0 if hh < 15 else 1, "tensor_scalar",
                              Sh[:, 64 * hh:64 * (hh + 1)], Sprev[:, 64 * hh:64 * (hh + 1)],
                              EcB[0:64, hh:hh + 1], None, ALU.mult)
                        tok_Sh = P.cnt["vector"]
                    else:
                        tok_Sh = E("vector", 1, "tensor_copy", Sh[:, :], Sprev[:, :])
                    DR("vector")
                    W("gpsimd", "vector", tok_Sh)
                    if ch >= 1:
                        W("gpsimd", "tensor", shhi_use.get(ch - 1, 0))
                    tok_shhi = E("gpsimd", 16, "dma_start", out=ShHi[64:128, :],
                                 in_=Sh[0:64, :])

                    W("tensor", "sync", ring_ld[ch])
                    for d8 in range(8):
                        prev_ev = khT_ev.get((ch, d8 - 1), 0) if d8 > 0 else \
                            khT_ev.get((ch - 1, 7), 0)
                        if prev_ev:
                            W("tensor", "scalar", prev_ev)
                        tok_tk = E("tensor", 1, "transpose", pTRb[:, :],
                                   khr[:, ch % 4, 128 * d8:128 * (d8 + 1)], c_idb[:, :])
                        W("scalar", "tensor", tok_tk)
                        khT_ev[(ch, d8)] = E("scalar", 1, "copy", khT[:, 0, d8, :],
                                             pTRb[:, :])

                    vtok_P2 = {}
                    vtok_O2 = {}
                    vtok_K2 = {}

                    def emit_sc(hh):
                        po = 64 * (hh % 2)
                        blkh = hh // 2
                        W("tensor", "scalar", khT_ev[(ch, blkh)])
                        pp2 = vtok_P2.get(hh - 2, vtok_P if hh < 2 else 0)
                        if pp2:
                            W("tensor", "vector", pp2)
                        return E("tensor", 1, "matmul", pSC2[hh % 2][:, :],
                                 khT[po:po + 64, 0, blkh, :],
                                 qr[po:po + 64, ch % 4, blkh, :],
                                 start=True, stop=True)

                    def emit_P(hh, tok_sc):
                        W("vector", "tensor", tok_sc)
                        if is_gla:
                            v = E("vector", 1, "scalar_tensor_tensor", Pt[:, hh % 2, :],
                                  pSC2[hh % 2][:, :], EinvT[:, hh:hh + 1], c_trb[:, :],
                                  ALU.mult, ALU.mult)
                            E("vector", 0, "tensor_scalar",
                              Kt[:, ch % 2, 64 * hh:64 * (hh + 1)],
                              khr[:, ch % 4, 64 * hh:64 * (hh + 1)],
                              EinvT[:, hh:hh + 1], None, ALU.mult)
                            vk = E("vector", 1, "engine_nop") if False else P.cnt["vector"]
                            P.cnt["vector"] += 0
                        else:
                            v = E("vector", 1, "tensor_tensor", Pt[:, hh % 2, :],
                                  pSC2[hh % 2][:, :], c_trb[:, :], ALU.mult)
                        vtok_P2[hh] = v
                        vtok_K2[hh] = v
                        return v

                    def emit_OV(hh):
                        po = 64 * (hh % 2)
                        blkh = hh // 2
                        W("tensor", "vector", vtok_P2[hh])
                        W("tensor", "vector", tok_Sh)
                        po2 = vtok_O2.get(hh - 2, vtok_O if hh < 2 else 0)
                        if po2:
                            W("tensor", "vector", po2)
                        E("tensor", 0, "matmul", pOV2[hh % 2][:, :], Pt[:, hh % 2, :],
                          vr[:, ch % 4, 64 * hh:64 * (hh + 1)], start=True, stop=False)
                        if hh % 2 == 1:
                            W("tensor", "gpsimd", tok_shhi)
                            rhsS = ShHi[64:128, 64 * hh:64 * (hh + 1)]
                        else:
                            rhsS = Sh[0:64, 64 * hh:64 * (hh + 1)]
                        return E("tensor", 1, "matmul", pOV2[hh % 2][:, :],
                                 qr[po:po + 64, ch % 4, blkh, :],
                                 rhsS, start=False, stop=True)

                    def emit_Oev(hh, tok_ov):
                        W("vector", "tensor", tok_ov)
                        if hh == 0 and ch >= 2:
                            W("vector", "gpsimd", odma[ch - 2])
                        v = E("vector", 1, "tensor_scalar",
                              Or[:, ch % 2, 64 * hh:64 * (hh + 1)], pOV2[hh % 2][:, :],
                              EexpT[:, hh:hh + 1], None, ALU.mult)
                        vtok_O2[hh] = v
                        return v

                    def emit_dS(hh, vtok_S):
                        if is_gla:
                            W("tensor", "vector", vtok_K2[hh])
                            lhsK = Kt[:, ch % 2, 64 * hh:64 * (hh + 1)]
                        else:
                            lhsK = khr[:, ch % 4, 64 * hh:64 * (hh + 1)]
                        if vtok_S:
                            W("tensor", "vector", vtok_S)
                        return E("tensor", 1, "matmul", pDS[:, :], lhsK,
                                 vr[:, ch % 4, 64 * hh:64 * (hh + 1)],
                                 start=True, stop=True)

                    def emit_Sup(hh, tok_ds):
                        W("vector", "tensor", tok_ds)
                        if is_gla:
                            E("vector", 1, "tensor_scalar", dstmp[:, :], pDS[:, :],
                              EendB[0:64, hh:hh + 1], None, ALU.mult)
                            DR("vector")
                            return E("vector", 1, "scalar_tensor_tensor",
                                     Snew[:, 64 * hh:64 * (hh + 1)],
                                     Sprev[:, 64 * hh:64 * (hh + 1)],
                                     ElamB[0:64, hh:hh + 1], dstmp[:, :],
                                     ALU.mult, ALU.add)
                        return E("vector", 1, "tensor_tensor",
                                 Snew[:, 64 * hh:64 * (hh + 1)],
                                 Sprev[:, 64 * hh:64 * (hh + 1)], pDS[:, :], ALU.add)

                    for hp in range(8):
                        h0, h1 = 2 * hp, 2 * hp + 1
                        t0_ = emit_sc(h0)
                        t1_ = emit_sc(h1)
                        emit_P(h0, t0_)
                        ov0 = emit_OV(h0)
                        emit_P(h1, t1_)
                        ov1 = emit_OV(h1)
                        emit_Oev(h0, ov0)
                        ds0 = emit_dS(h0, vtok_S)
                        emit_Oev(h1, ov1)
                        vtok_S = emit_Sup(h0, ds0)
                        ds1 = emit_dS(h1, vtok_S)
                        vtok_S = emit_Sup(h1, ds1)
                        vtok_O = vtok_O2[h1]
                        vtok_P = vtok_P2[h1]
                    W("gpsimd", "vector", vtok_O)
                    if ch >= 2:
                        pass
                    odma[ch] = E("gpsimd", 16, "dma_start", out=o_sp[ch, :, :],
                                 in_=Or[:, ch % 2, :])
                    chunk_done[ch] = P.cnt["tensor"]
                    chunk_done[(ch, "v")] = P.cnt["vector"]
                    shhi_use[ch] = P.cnt["tensor"]

                tok_S_final = vtok_S
                tok_B_t = P.cnt["tensor"]
                tok_B_v = P.cnt["vector"]
                stB.close()
                if DBG_NCH < 16:
                    P.barrier()
                    return

                stC = contextlib.ExitStack()
                pC1 = stC.enter_context(nc.psum_tensor(pfx + "pC1", [4, 16], F32))
                pC2 = stC.enter_context(nc.psum_tensor(pfx + "pC2", [64, 16], F32))
                tmsR = E("vector", 1, "memset", wtmp[0:1, :], 0.0)
                W("vector", "vector", tmsR)
                if is_gla:
                    W("vector", "vector", tok_lab)
                    tok_lr = E("vector", 1, "tensor_copy", wtmp[0:1, 0:16],
                               labs[0:1, 0, :])
                else:
                    tok_lr = tmsR
                Sfin = Sa if 16 % 2 == 0 else Sb
                W("gpsimd", "vector", tok_S_final)
                W("gpsimd", "vector", tok_lr)
                cci = extra["cc_in"]; cco = extra["cc_out"]
                E("gpsimd", 16, "dma_start", out=cci[0:64, :], in_=Sfin[:, :])
                E("gpsimd", 16, "dma_start", out=cci[64:65, :], in_=wtmp[0:1, :])
                tok_ccin = E("gpsimd", 16, "dma_start", out=cci[65:66, :], in_=wtmp[0:1, :])
                ccbase = extra["ccbase"][0]
                P.EX("gpsimd", "ccsem", 1, "collective_compute", "AllGather", ALU.bypass,
                     ins=[extra["cc_in"].opt()], outs=[extra["cc_out"].opt()],
                     replica_groups=[[0, 1, 2, 3], [4, 5, 6, 7]])
                P.ops["gpsimd"].append(("w", "ccsem", ccbase + 1))
                extra["ccbase"][0] = ccbase + 1
                tok_sgld = {}
                for j in range(4):
                    E("gpsimd", 16, "dma_start", out=fga[j:j + 1, :],
                      in_=cco[66 * j + 64:66 * j + 65, 0:16])
                tok_gath = P.cnt["gpsimd"]
                if is_gla:
                    W("sync", "gpsimd", tok_gath)
                    tok_mrk = E("sync", 16, "dma_start", out=zt1[0:4, 0:4],
                                in_=extra["MrankT"][:, :])
                    W("tensor", "sync", tok_mrk)
                    W("tensor", "gpsimd", tok_gath)
                    tok_gmm = E("tensor", 1, "matmul", pC1[:, :], zt1[0:4, 0:4],
                                fga[0:4, :], start=True, stop=True)
                    W("scalar", "tensor", tok_gmm)
                    tok_cf = E("scalar", 1, "activation", coefs[:, :], pC1[:, :], AF.Exp)
                    W("sync", "gpsimd", tok_gath)
                    tok_rmc = E("sync", 16, "dma_start", out=zt2[0:4, 0:1],
                                in_=extra["rmcol"][:, :])
                    W("vector", "scalar", tok_cf)
                    W("vector", "sync", tok_rmc)
                    tok_cf2 = E("vector", 1, "tensor_scalar", fga[0:4, :], coefs[:, :],
                                zt2[0:4, 0:1], None, ALU.mult)
                    W("gpsimd", "vector", tok_cf2)
                    for j in range(4):
                        E("gpsimd", 16, "dma_start", out=coefr[0:1, j, :],
                          in_=fga[j:j + 1, :])
                    tok_cr = P.cnt["gpsimd"]
                    prevcb = 0
                    for j in range(4):
                        W("tensor", "gpsimd", tok_cr)
                        W("tensor", "vector", tok_ones)
                        if prevcb:
                            W("tensor", "scalar", prevcb)
                        tok_cb2 = E("tensor", 1, "matmul", pC2[:, :],
                                    ones_row[0:1, 0:64], coefr[0:1, j, :],
                                    start=True, stop=True)
                        W("scalar", "tensor", tok_cb2)
                        prevcb = E("scalar", 1, "copy", coefB[:, j, :], pC2[:, :])
                    tok_coefB = prevcb
                    W("vector", "scalar", tok_coefB)
                    sgacc = {}
                    for j in range(4):
                        if j >= 2:
                            W("gpsimd", "vector", sgacc[j - 2])
                        tok_sgld[j] = E("gpsimd", 16, "dma_start", out=Sg[0:64, j % 2, :],
                                        in_=cco[66 * j:66 * j + 64, :])
                        W("vector", "gpsimd", tok_sgld[j])
                        for hh in range(16):
                            sgs = Sg[0:64, j % 2, 64 * hh:64 * (hh + 1)]
                            if j == 0:
                                E("vector", 0 if hh < 15 else 1, "tensor_scalar",
                                  Sa[:, 64 * hh:64 * (hh + 1)], sgs,
                                  coefB[:, 0, hh:hh + 1], None, ALU.mult)
                            else:
                                E("vector", 0 if hh < 15 else 1, "scalar_tensor_tensor",
                                  (Sa if j % 2 == 0 else Sb)[:, 64 * hh:64 * (hh + 1)],
                                  sgs, coefB[:, j, hh:hh + 1],
                                  (Sb if j % 2 == 0 else Sa)[:, 64 * hh:64 * (hh + 1)],
                                  ALU.mult, ALU.add)
                        sgacc[j] = P.cnt["vector"]
                        DR("vector")
                else:
                    W("sync", "gpsimd", tok_gath)
                    tok_rm4 = E("sync", 16, "dma_start", out=coefB[:, 0, 0:4],
                                in_=extra["rm4B"][:, :])
                    W("vector", "sync", tok_rm4)
                    W("vector", "gpsimd", tok_gath)
                    sgacc = {}
                    for j in range(4):
                        if j >= 2:
                            W("gpsimd", "vector", sgacc[j - 2])
                        tok_sgld[j] = E("gpsimd", 16, "dma_start", out=Sg[0:64, j % 2, :],
                                        in_=cco[66 * j:66 * j + 64, :])
                        W("vector", "gpsimd", tok_sgld[j])
                        for hh in range(16):
                            sgs = Sg[0:64, j % 2, 64 * hh:64 * (hh + 1)]
                            if j == 0:
                                E("vector", 0 if hh < 15 else 1, "tensor_scalar",
                                  Sa[:, 64 * hh:64 * (hh + 1)], sgs,
                                  coefB[:, 0, 0:1], None, ALU.mult)
                            else:
                                E("vector", 0 if hh < 15 else 1, "scalar_tensor_tensor",
                                  (Sa if j % 2 == 0 else Sb)[:, 64 * hh:64 * (hh + 1)],
                                  sgs, coefB[:, 0, j:j + 1],
                                  (Sb if j % 2 == 0 else Sa)[:, 64 * hh:64 * (hh + 1)],
                                  ALU.mult, ALU.add)
                        sgacc[j] = P.cnt["vector"]
                        DR("vector")
                tok_sin = E("vector", 1, "tensor_copy", Shin[:, :], Sb[:, :])
                W("gpsimd", "vector", tok_sin)
                W("gpsimd", "tensor", tok_B_t)
                tok_sinhi = E("gpsimd", 16, "dma_start", out=ShHi[64:128, :],
                              in_=Shin[0:64, :])
                stC.close()

                stD = contextlib.ExitStack()
                pFX2 = [stD.enter_context(nc.psum_tensor(pfx + f"pFX{i}", [128, 64], F32))
                        for i in range(2)]
                pW2 = [stD.enter_context(nc.psum_tensor(pfx + f"pW{i}", [128, 512], F32))
                       for i in range(2)]
                pF3 = stD.enter_context(nc.psum_tensor(pfx + "pF3", [128, 16], F32))
                pTRd = stD.enter_context(nc.psum_tensor(pfx + "pTRd", [128, 128], BF16))
                pGT = stD.enter_context(nc.psum_tensor(pfx + "pGT", [128, 512], F32))

                tok_wo = 0
                for d8 in range(8):
                    W("sync", "tensor", tok_phaseA_t)
                    if tok_wo:
                        W("sync", "vector", tok_wo)
                    tok_wold = E("sync", 16, "dma_start", out=wtmp[:, :],
                                 in_=WoT_d[128 * d8:128 * (d8 + 1), :])
                    W("vector", "sync", tok_wold)
                    if is_gla:
                        W("vector", "gpsimd", tok_wres)
                        tok_wo = E("vector", 1, "tensor_scalar", wbufA[:, d8, :],
                                   wtmp[:, :], lnwc[:, d8:d8 + 1], None, ALU.mult)
                    else:
                        tok_wo = E("vector", 1, "tensor_copy", wbufA[:, d8, :], wtmp[:, :])

                vtok_fx = 0
                prev_fx_v = 0
                ln_i = NT
                fix_done = {}
                og_done = {}
                ea_done = {}
                gate_ev = {}
                ohT_ev = {}
                res_done = {}
                for ch in range(16):
                    t8 = ch // 2
                    W("sync", "gpsimd", odma[ch])
                    if ch < 4:
                        W("sync", "tensor", tok_B_t)
                        W("sync", "vector", tok_B_v)
                    if ch >= 4:
                        W("sync", "tensor", fix_done.get(ch - 4, 0))
                    E("sync", 16, "dma_start", out=qr[:, ch % 4, :, :], in_=q_sp[ch, :, :, :])
                    if ch >= 2:
                        W("sync", "vector", og_done.get(ch - 2, 0))
                    tok_old = E("sync", 16, "dma_start", out=Or[:, ch % 2, :],
                                in_=o_sp[ch, :, :])
                    if is_gla:
                        W("tensor", "vector", tok_lr)
                        if ch >= 1:
                            W("tensor", "vector", ea_done.get(ch - 1, 0))
                        tok_fb = E("tensor", 1, "matmul", pF3[:, :], ones_row[0:1, :],
                                   t1b[0:1, 16 * ch:16 * (ch + 1)], start=True, stop=True)
                        W("vector", "tensor", tok_fb)
                        tok_ea1 = E("vector", 1, "tensor_tensor", eab[:, :],
                                    Fst[:, ch, :], pF3[:, :], ALU.add)
                        ea_done[ch] = tok_ea1
                        W("scalar", "vector", tok_ea1)
                        if ch >= 1:
                            W("scalar", "vector", prev_fx_v)
                        tok_EA = E("scalar", 1, "activation", EA[:, :], eab[:, :], AF.Exp)
                    else:
                        if ch >= 1:
                            W("vector", "vector", ea_done.get(ch - 1, 0))
                        E("vector", 1, "tensor_scalar", posc[:, 1:2], posc[:, 0:1],
                          float(128 * ch), None, ALU.add)
                        DR("vector")
                        tok_g16 = E("vector", 1, "tensor_scalar", zt2[:, :], lngB[:, :],
                                    posc[:, 1:2], None, ALU.mult)
                        ea_done[ch] = tok_g16
                        W("scalar", "vector", tok_g16)
                        if ch >= 1:
                            W("scalar", "vector", prev_fx_v)
                        tok_EA = E("scalar", 1, "activation", EA[:, :], zt2[:, :], AF.Exp)
                    W("tensor", "vector", tok_sin)
                    W("tensor", "sync", tok_old)
                    W("vector", "scalar", tok_EA)
                    W("vector", "sync", tok_old)
                    vtok_fx2 = {}
                    for hh in range(16):
                        po = 64 * (hh % 2)
                        blkh = hh // 2
                        pFX = pFX2[hh % 2]
                        pfx2 = vtok_fx2.get(hh - 2, vtok_fx if hh < 2 else 0)
                        if pfx2:
                            W("tensor", "vector", pfx2)
                        if hh % 2 == 1:
                            W("tensor", "gpsimd", tok_sinhi)
                            rhsS2 = ShHi[64:128, 64 * hh:64 * (hh + 1)]
                        else:
                            rhsS2 = Shin[0:64, 64 * hh:64 * (hh + 1)]
                        tok_fx = E("tensor", 1, "matmul", pFX[:, :],
                                   qr[po:po + 64, ch % 4, blkh, :],
                                   rhsS2, start=True, stop=True)
                        W("vector", "tensor", tok_fx)
                        vtok_fx = E("vector", 1, "scalar_tensor_tensor",
                                    Kt[:, ch % 2, 64 * hh:64 * (hh + 1)], pFX[:, :],
                                    EA[:, hh:hh + 1],
                                    Or[:, ch % 2, 64 * hh:64 * (hh + 1)],
                                    ALU.mult, ALU.add)
                        vtok_fx2[hh] = vtok_fx
                    tok_Ofix = vtok_fx
                    prev_fx_v = tok_Ofix

                    if is_gla:
                        for oh in range(2):
                            W("tensor", "scalar", g1ev[t8])
                            W("tensor", "gpsimd", tok_wres)
                            if oh == 1:
                                W("tensor", "scalar", gate_ev[ch])
                            elif ch >= 1:
                                W("tensor", "scalar", gate_ev.get(ch - 1, 0))
                            tok_g = E("tensor", 1, "matmul", pGT[:, :],
                                      g1f[:, t8, 128 * (ch % 2):128 * (ch % 2 + 1)],
                                      wg2_s[:, 512 * oh:512 * (oh + 1)],
                                      start=True, stop=True)
                            W("scalar", "tensor", tok_g)
                            gate_ev[ch] = E("scalar", 1, "activation",
                                            gatet[:, ch % 2, 512 * oh:512 * (oh + 1)],
                                            pGT[:, :], AF.Sigmoid)
                        W("vector", "scalar", gate_ev[ch])
                        W("vector", "vector", tok_Ofix)
                        tok_og = E("vector", 1, "tensor_tensor", ogb[:, ch % 2, :],
                                   Kt[:, ch % 2, :], gatet[:, ch % 2, :], ALU.mult)
                        og_done[ch] = tok_og
                        tok_oh = ln_tile_plain(ln_i, ogb[:, ch % 2, :], dep_v=tok_og)
                        ln_i += 1
                        src_oh = xqt[:, (ln_i - 1) % 2, :]
                    else:
                        og_done[ch] = tok_Ofix
                        tok_oh = tok_Ofix
                        src_oh = Kt[:, ch % 2, :]
                    for d8 in range(8):
                        kk = ch * 8 + d8
                        W("tensor", "vector", tok_oh)
                        prev_ev2 = ohT_ev.get((ch, d8 - 1), 0) if d8 > 0 else \
                            ohT_ev.get((ch - 1, 7), 0)
                        if prev_ev2:
                            W("tensor", "scalar", prev_ev2)
                        tok_t2 = E("tensor", 1, "transpose", pTRd[:, :],
                                   src_oh[:, 128 * d8:128 * (d8 + 1)], c_idb[:, :])
                        W("scalar", "tensor", tok_t2)
                        ohT_ev[(ch, d8)] = E("scalar", 1, "copy", khT[:, 0, d8, :]
                                             if False else ohT[:, 0, d8, :],
                                             pTRd[:, :])
                    if is_gla:
                        tr_lastA[ln_i - 1] = P.cnt["tensor"]
                    for jh in range(2):
                        W("tensor", "scalar", ohT_ev[(ch, 7)])
                        W("tensor", "vector", tok_wo)
                        if ch >= 1:
                            W("tensor", "vector", res_done.get(ch - 1, 0))
                        first = True
                        if not is_gla:
                            W("tensor", "vector", tok_ones)
                            E("tensor", 0, "matmul", pW2[jh][:, :], ones_row[0:1, :],
                              bo_s[0:1, 512 * jh:512 * (jh + 1)],
                              start=True, stop=False)
                            first = False
                        for o8 in range(8):
                            E("tensor", 0 if o8 < 7 else 1, "matmul", pW2[jh][:, :],
                              ohT[:, 0, o8, :],
                              wbufA[:, o8, 512 * jh:512 * (jh + 1)],
                              start=(o8 == 0 and first), stop=(o8 == 7))
                    tok_wmm = P.cnt["tensor"]
                    W("vector", "tensor", tok_wmm)
                    r = ch
                    E("vector", 0, "tensor_tensor", xres[:, spare[0], 0:512],
                      pW2[0][:, :], xres[:, slot[r], 0:512], ALU.add)
                    res_done[ch] = E("vector", 1, "tensor_tensor",
                                     xres[:, spare[0], 512:1024], pW2[1][:, :],
                                     xres[:, slot[r], 512:1024], ALU.add)
                    slot[r], spare[0] = spare[0], slot[r]
                    fix_done[ch] = P.cnt["tensor"]
                stD.close()

        if n_sub >= 1:
            emit_l0_ffn("f1_")

        F8 = mybir.dt.float8e5

        def emit_l1_prep():
            sub = contextlib.ExitStack()
            with sub:
                SS = sub.enter_context
                q8 = SS(nc.sbuf_tensor("p8_q8", [128, D], F8))
                b2stage = SS(nc.sbuf_tensor("p8_b2s", [1, D], F32))
                b2r8 = SS(nc.sbuf_tensor("p8_b2r8", [1, D], F8))
                qb = SS(nc.sbuf_tensor("p8_qb", [128, 2, D], BF16))
                b8 = SS(nc.sbuf_tensor("p8_b8", [128, 1], F8))
                chunks = []
                for j in range(8):
                    for q in range(4):
                        chunks.append(("w1", w1T if False else W1T1, (j, q)))
                for j in range(32):
                    chunks.append(("w2", W2T1, (j,)))
                nq = len(chunks)
                v_base = P.cnt["vector"]
                g_base = P.cnt["gpsimd"]
                tok_v = [v_base + 2 * (i + 1) for i in range(nq)]
                tok_s = []
                acc = g_base
                for i, (w, src_t, key) in enumerate(chunks):
                    acc += 16 * (8 if w == "w1" else 1)
                    tok_s.append(acc)
                tok_l = []
                for i, (w, src_t, key) in enumerate(chunks):
                    if w == "w1":
                        j, q = key
                        src_ap = src_t[128 * j:128 * (j + 1), 1024 * q:1024 * (q + 1)]
                    else:
                        (j,) = key
                        src_ap = src_t[128 * j:128 * (j + 1), :]
                    if i >= 2:
                        W("sync", "vector", tok_v[i - 2])
                    tok_l.append(E("sync", 16, "dma_start", out=xh[:, i % 2, :], in_=src_ap))
                for i, (w, src_t, key) in enumerate(chunks):
                    W("vector", "sync", tok_l[i])
                    if i >= 2:
                        W("vector", "gpsimd", tok_s[i - 2])
                    E("vector", 1, "tensor_copy", q8[:, :], xh[:, i % 2, :])
                    DR("vector")
                    t_ = E("vector", 1, "tensor_copy", qb[:, i % 2, :], q8[:, :])
                    assert t_ == tok_v[i]
                for i, (w, src_t, key) in enumerate(chunks):
                    W("gpsimd", "vector", tok_v[i])
                    if w == "w1":
                        j, q = key
                        for t in range(8):
                            E("gpsimd", 16, "dma_start", out=w1q8_d[8 * q + t, :, j, :],
                              in_=qb[:, i % 2, 128 * t:128 * (t + 1)])
                        assert P.cnt["gpsimd"] == tok_s[i]
                    else:
                        (j,) = key
                        t_ = E("gpsimd", 16, "dma_start",
                               out=w2q8_d[128 * j:128 * (j + 1), :],
                               in_=qb[:, i % 2, :])
                        assert t_ == tok_s[i]
                for j in range(32):
                    tok_b = E("sync", 16, "dma_start", out=cols[:, 4:5],
                              in_=b1v[128 * j:128 * (j + 1), :])
                    W("vector", "sync", tok_b)
                    E("vector", 1, "tensor_copy", b8[:, 0:1], cols[:, 4:5])
                    DR("vector")
                    tok_bv = E("vector", 1, "tensor_copy", b1c32[:, j:j + 1], b8[:, 0:1])
                    W("sync", "vector", tok_bv)
                tok_b2l = E("sync", 16, "dma_start", out=cols[:, 4:5], in_=b2v[0:128, :])
                for j in range(8):
                    tok_b2 = E("sync", 16, "dma_start", out=b2stage[0:1, 128 * j:128 * (j + 1)],
                               in_=b2v[128 * j:128 * (j + 1), 0:1])
                W("vector", "sync", tok_b2)
                E("vector", 1, "tensor_copy", b2r8[0:1, :], b2stage[0:1, :])
                DR("vector")
                tok_fin = E("vector", 1, "tensor_copy", b2row[0:1, :], b2r8[0:1, :])
            return tok_fin

        def emit_l1_ffn(pfx, tok_prep):
            sub = contextlib.ExitStack()
            with sub:
                SS = sub.enter_context

                def ssb(name, shape, dt=F32):
                    return SS(nc.sbuf_tensor(pfx + name, shape, dt))

                xqT = ssb("xqT", [128, 8 * 2048], BF16)
                hfull = ssb("hfull", [128, 32, FT], BF16)
                h8b = ssb("h8b", [128, 2, FT], BF16)
                h8f = ssb("h8f", [128, FT], F8)
                x8f = ssb("x8f", [128, D], F8)
                w1c = ssb("w1c", [128, 2, 8, 128], BF16)
                w2c = ssb("w2c", [128, 2, D], BF16)
                mstg = ssb("mstg", [128, 2, 2, D], BF16)
                ybb = ssb("ybb", [128, 512], BF16)
                ybm = ssb("ybm", [128, 512])

                p_tr = [SS(nc.psum_tensor(pfx + f"ptr{i}", [128, 128], BF16)) for i in range(2)]
                p_mm = [SS(nc.psum_tensor(pfx + f"pmm{i}", [128, FT], F32)) for i in range(2)]
                p_y = [SS(nc.psum_tensor(pfx + f"py{i}", [128, 512], F32)) for i in range(4)]

                def xqa(d, a, b):
                    return xqT[:, 2048 * d + a:2048 * d + b]

                tok_xq = [0] * (NT + 4)
                tok_ev = []
                tr_last = [0] * (NT + 4)
                for i in range(NT):
                    Pn = 128
                    src = xres[:, slot[i], :]
                    xhb = xh[:, i % 2, :]
                    xqb = xqt[:, i % 2, :]
                    dump = sqdump[:, :]

                    def CL(j):
                        return cols[:, 52 + j:53 + j]

                    def SL(j):
                        return scol[:, j:j + 1]

                    if i >= 2:
                        W("scalar", "vector", tok_xq[i - 2])
                    tok_ms = E("scalar", 1, "activation", dump, src, AF.Identity,
                               accum_out=SL(0))
                    W("vector", "scalar", tok_ms)
                    E("vector", 1, "tensor_scalar", CL(0), SL(0), 1.0 / D, None, ALU.mult)
                    DR("vector")
                    tok_nm = E("vector", 1, "tensor_scalar", CL(1), CL(0), -1.0, None, ALU.mult)
                    W("scalar", "vector", tok_nm)
                    tok_ss = E("scalar", 1, "activation", dump, src, AF.Square,
                               bias=CL(1), scale=1.0, accum_out=SL(1))
                    W("vector", "scalar", tok_ss)
                    tok_ve = E("vector", 1, "tensor_scalar", CL(2), SL(1), 1.0 / D, 1e-5,
                               ALU.mult, ALU.add)
                    W("scalar", "vector", tok_ve)
                    E("scalar", 1, "activation", SL(2), CL(2), AF.Ln)
                    DR("scalar")
                    tok_rs = E("scalar", 1, "activation", SL(3), SL(2), AF.Exp, scale=-0.5)
                    W("vector", "scalar", tok_rs)
                    E("vector", 1, "tensor_copy", CL(3), SL(3))
                    DR("vector")
                    E("vector", 1, "tensor_scalar", xhb, src, CL(0), CL(3),
                      ALU.subtract, ALU.mult)
                    DR("vector")
                    E("vector", 1, "tensor_copy", x8f[:, :], xhb)
                    DR("vector")
                    if i >= 2:
                        W("vector", "tensor", tr_last[i - 2])
                    tok_xq[i] = E("vector", 1, "tensor_copy", xqb, x8f[:, :])
                    for d8 in range(8):
                        kk = len(tok_ev)
                        W("tensor", "vector", tok_xq[i])
                        if kk >= 2:
                            W("tensor", "scalar", tok_ev[kk - 2])
                        tok_t = E("tensor", 1, "transpose", p_tr[kk % 2][:, :],
                                  xqb[:, 128 * d8:128 * (d8 + 1)], c_idb[:, :])
                        W("scalar", "tensor", tok_t)
                        tok_ev.append(E("scalar", 1, "copy",
                                        xqa(d8, 128 * i, 128 * (i + 1)),
                                        p_tr[kk % 2][:, :]))
                    tr_last[i] = P.cnt["tensor"]
                tok_xqT = tok_ev[-1]

                w1_tok, w2_tok = {}, {}
                w1_use, w2_use = {}, {}
                relu_done, q8_done, yv_done, m_tok = {}, {}, {}, {}
                for i in range(NFT):
                    pv = yv_done.get(i - 2, 0)
                    if pv:
                        W("sync", "vector", pv)
                    E("sync", 16, "dma_start", out=mstg[:, i % 2, 0, :],
                      in_=maskC[FT * i:FT * i + 128, :])
                    m_tok[i] = E("sync", 16, "dma_start", out=mstg[:, i % 2, 1, :],
                                 in_=maskC[FT * i + 128:FT * (i + 1), :])
                    for c in range(32):
                        k = i * 32 + c
                        t_use = w1_use.get(k - 2, 0)
                        if t_use:
                            W("sync", "tensor", t_use)
                        w1_tok[k] = E("sync", 16, "dma_start", out=w1c[:, k % 2, :, :],
                                      in_=w1q8_d[c, :, :, :])
                        W("tensor", "sync", w1_tok[k])
                        if i == 0 and c == 0:
                            W("tensor", "scalar", tok_xqT)
                            W("tensor", "vector", tok_prep)
                        pe = relu_done.get((i, c - 2), 0)
                        if pe:
                            W("tensor", "scalar", pe)
                        pq = q8_done.get((i - 1, c), 0)
                        if pq:
                            W("tensor", "vector", pq)
                        for d8 in range(8):
                            E("tensor", 0 if d8 < 7 else 1, "matmul", p_mm[c % 2][:, :],
                              w1c[:, k % 2, d8, :], xqa(d8, FT * i, FT * (i + 1)),
                              start=(d8 == 0), stop=(d8 == 7))
                        w1_use[k] = P.cnt["tensor"]
                        tok_mm = P.cnt["tensor"]
                        W("scalar", "tensor", tok_mm)
                        relu_done[(i, c)] = E("scalar", 1, "activation", hfull[:, c, :],
                                              p_mm[c % 2][:, :], AF.Relu,
                                              bias=b1c32[:, c:c + 1], scale=1.0)
                    for c in range(32):
                        k = i * 32 + c
                        t_use = w2_use.get(k - 2, 0)
                        if t_use:
                            W("sync", "tensor", t_use)
                        w2_tok[k] = E("sync", 16, "dma_start", out=w2c[:, k % 2, :],
                                      in_=w2q8_d[128 * c:128 * (c + 1), :])
                        W("vector", "scalar", relu_done[(i, c)])
                        pu = w2_use.get(k - 2, 0)
                        if pu:
                            W("vector", "tensor", pu)
                        E("vector", 1, "tensor_copy", h8f[:, :], hfull[:, c, :])
                        DR("vector")
                        tok_h8 = E("vector", 1, "tensor_copy", h8b[:, c % 2, :], h8f[:, :])
                        q8_done[(i, c)] = tok_h8
                        W("tensor", "vector", tok_h8)
                        W("tensor", "sync", w2_tok[k])
                        if c == 0:
                            pv2 = yv_done.get(i - 1, 0)
                            if pv2:
                                W("tensor", "vector", pv2)
                            W("tensor", "vector", tok_ones)
                            W("tensor", "vector", tok_prep)
                            for tsub in range(2):
                                for ohh in range(2):
                                    E("tensor", 0, "matmul", p_y[2 * tsub + ohh][:, :],
                                      ones_row[0:1, :],
                                      b2row[0:1, 512 * ohh:512 * (ohh + 1)],
                                      start=True, stop=False)
                        for tsub in range(2):
                            for ohh in range(2):
                                E("tensor", 0 if (tsub, ohh) != (1, 1) else 1, "matmul",
                                  p_y[2 * tsub + ohh][:, :],
                                  h8b[:, c % 2, 128 * tsub:128 * (tsub + 1)],
                                  w2c[:, k % 2, 512 * ohh:512 * (ohh + 1)],
                                  start=False, stop=(c == 31))
                        w2_use[k] = P.cnt["tensor"]
                    tok_y = P.cnt["tensor"]
                    W("vector", "tensor", tok_y)
                    W("vector", "sync", m_tok[i])
                    for tsub in range(2):
                        r = 2 * i + tsub
                        for ohh in range(2):
                            E("vector", 1, "tensor_copy", ybb[:, 0:512],
                              p_y[2 * tsub + ohh][:, :])
                            DR("vector")
                            E("vector", 1, "tensor_tensor", ybm[:, 0:512], ybb[:, 0:512],
                              mstg[:, i % 2, tsub, 512 * ohh:512 * (ohh + 1)], ALU.mult)
                            DR("vector")
                            E("vector", 1, "tensor_tensor",
                              xres[:, spare[0], 512 * ohh:512 * (ohh + 1)], ybm[:, 0:512],
                              xres[:, slot[r], 512 * ohh:512 * (ohh + 1)], ALU.add)
                        slot[r], spare[0] = spare[0], slot[r]
                    yv_done[i] = P.cnt["vector"]
            return None

        ccbase_track = [0]
        if n_sub >= 2:
            P.barrier()
            emit_attn("a0_", "gla", (WqT0, WkT0, WvT0, WoT0), {
                "WfT": WfT0, "Wg1T": Wg1T0, "Wg2T": Wg2T0, "lnw": lnw0,
                "MrankT": MrankT, "rmcol": rmcol, "rm4B": rm4B,
                "cc_in": cc1_in, "cc_out": cc1_out, "ccbase": ccbase_track,
            })
        if n_sub >= 3:
            P.barrier()
            stH = contextlib.ExitStack()
            pHL = stH.enter_context(nc.psum_tensor("pHL", [1, 512], F32))
            gbuf = stH.enter_context(nc.sbuf_tensor("gbuf", [8, D], F32))
            selb = stH.enter_context(nc.sbuf_tensor("selb", [8, 2], F32))
            hstg = stH.enter_context(nc.sbuf_tensor("hstg", [1, D], F32))
            E("gpsimd", 16, "dma_start", out=cc2_in[0:1, :], in_=xres[0:1, slot[0], :])
            tok_h0 = E("gpsimd", 16, "dma_start", out=cc2_in[1:2, :],
                       in_=xres[127:128, slot[15], :])
            ccb = ccbase_track[0]
            P.EX("gpsimd", "ccsem", 1, "collective_compute", "AllGather", ALU.bypass,
                 ins=[cc2_in.opt()], outs=[cc2_out.opt()],
                 replica_groups=[[0, 1, 2, 3], [4, 5, 6, 7]])
            P.ops["gpsimd"].append(("w", "ccsem", ccb + 1))
            ccbase_track[0] = ccb + 1
            E("gpsimd", 16, "dma_start", out=gbuf[:, :], in_=cc2_out[:, :])
            E("sync", 16, "dma_start", out=selb[:, 0:1], in_=sel_lo[:, :])
            tok_hg = E("sync", 16, "dma_start", out=selb[:, 1:2], in_=sel_hi[:, :])
            W("tensor", "gpsimd", P.cnt["gpsimd"])
            W("tensor", "sync", tok_hg)
            tok_hrow = 0
            for which in range(2):
                for half in range(2):
                    tok_mmh = E("tensor", 1, "matmul", pHL[:, :],
                                selb[:, which:which + 1],
                                gbuf[:, 512 * half:512 * (half + 1)],
                                start=True, stop=True)
                    W("vector", "tensor", tok_mmh)
                    if which == 0:
                        tok_hrow = E("vector", 1, "tensor_copy",
                                     halo_rows[0:1, 512 * half:512 * (half + 1)], pHL[:, :])
                    else:
                        tok_hrow = E("vector", 1, "tensor_copy",
                                     hstg[0:1, 512 * half:512 * (half + 1)], pHL[:, :])
                    W("tensor", "vector", tok_hrow)
            W("gpsimd", "vector", tok_hrow)
            tok_hdma = E("gpsimd", 16, "dma_start", out=halo_rows[1:2, :], in_=hstg[0:1, :])
            P.barrier()
            stH.close()
            emit_l0_ffn("f2_")
        if n_sub >= 4:
            P.barrier()
            tok_prep = emit_l1_prep()
            P.barrier()
            emit_l1_ffn("g1_", tok_prep)
        if n_sub >= 5:
            P.barrier()
            emit_attn("a1_", "dec", (WqT1, WkT1, WvT1, WoT1), {
                "gam_row": gam_row, "poscol": poscol, "rm4B": rm4B,
                "bqc": bqc1, "bkr": bkr1, "bvr": bvr1, "bor": bor1,
                "cc_in": cc3_in, "cc_out": cc3_out, "ccbase": ccbase_track,
            })
        if n_sub >= 6:
            P.barrier()
            emit_l1_ffn("g2_", tok_prep)

        P.barrier()
        tok_out = 0
        for i in range(NT):
            tok_out = E("sync", 16, "dma_start", out=out_d[128 * i:128 * (i + 1), :],
                        in_=xres[:, slot[i], :])
        P.ops["sync"].append(("w", "sync", tok_out))

        P.replay(blk, sems)
    return nc


_IDN_BF = np.eye(128).astype(BF)
_IDN_F = np.eye(128, dtype=np.float32)
_SEL2 = np.zeros((2, 256), np.float32)
_SEL2[0, 0:128] = 1.0
_SEL2[1, 128:256] = 1.0
_TRIU_F = np.triu(np.ones((128, 128), np.float32))
_TRIU_B = np.triu(np.ones((128, 128))).astype(BF)
_MASKC = None


def _get_maskc():
    global _MASKC
    if _MASKC is None:
        keep = _keep_mask(42, 0.8, (2, 4 * T, D))
        _MASKC = np.ascontiguousarray((keep.astype(np.float32) * np.float32(1.25)
                                       * np.float32(0.5)).astype(BF))
    return _MASKC


def _prep_inputs(inputs):
    if N_SUB >= 4:
        _get_maskc()
    x = np.asarray(inputs["L0_attn_x"], np.float32)
    w1T = np.ascontiguousarray(np.asarray(inputs["L0_ffn_w1"], np.float32).T)
    w2T = np.ascontiguousarray(np.asarray(inputs["L0_ffn_w2"], np.float32).T)
    convw = np.ascontiguousarray(np.asarray(inputs["L0_ffn_conv_w"], np.float32).reshape(DH, 3))
    convb = np.ascontiguousarray(np.asarray(inputs["L0_ffn_conv_b"], np.float32).reshape(DH, 1))
    in_maps = []
    for c in range(8):
        b, q = c // 4, c % 4
        xs = np.zeros((T + 2, D), np.float32)
        xs[1:T + 1] = x[b, q * T:(q + 1) * T]
        if q > 0:
            xs[0] = x[b, q * T - 1]
        if q < 3:
            xs[T + 1] = x[b, (q + 1) * T]
        m = {
            "x_in": xs, "w1T": w1T, "w2T": w2T, "convw": convw, "convb": convb,
            "idn_bf": _IDN_BF, "idn_f": _IDN_F, "sel2": _SEL2,
            "triu_f": _TRIU_F, "triu_b": _TRIU_B,
        }
        if N_SUB >= 2:
            rank = q
            mrT = np.zeros((4, 4), np.float32)
            for jj in range(4):
                for ll in range(jj + 1, rank):
                    mrT[ll, jj] = 1.0
            rm = np.zeros((4, 1), np.float32)
            rm[:rank, 0] = 1.0
            sl = np.zeros((8, 1), np.float32)
            sh = np.zeros((8, 1), np.float32)
            if q > 0:
                sl[2 * (q - 1) + 1, 0] = 1.0
            if q < 3:
                sh[2 * (q + 1), 0] = 1.0
            m["sel_lo"] = sl
            m["sel_hi"] = sh
            m.update({
                "WqT0": np.ascontiguousarray(np.asarray(inputs["L0_attn_Wq"], np.float32).T),
                "WkT0": np.ascontiguousarray(np.asarray(inputs["L0_attn_Wk"], np.float32).T),
                "WvT0": np.ascontiguousarray(np.asarray(inputs["L0_attn_Wv"], np.float32).T),
                "WoT0": np.ascontiguousarray(np.asarray(inputs["L0_attn_Wo"], np.float32).T),
                "WfT0": np.ascontiguousarray(np.asarray(inputs["L0_attn_Wf"], np.float32).T),
                "Wg1T0": np.ascontiguousarray(np.asarray(inputs["L0_attn_Wg1"], np.float32).T),
                "Wg2T0": np.ascontiguousarray(np.asarray(inputs["L0_attn_Wg2"], np.float32).T),
                "lnw0": np.ascontiguousarray(np.asarray(inputs["L0_attn_ln_w"], np.float32).reshape(D, 1)),
                "MrankT": mrT, "rm4B": np.broadcast_to(rm.T, (64, 4)).copy(),
                "rmcol": rm,
            })
        if N_SUB >= 4:
            m.update({
                "W1T1": np.ascontiguousarray(np.asarray(inputs["L1_ffn_W1"], np.float32).T),
                "W2T1": np.ascontiguousarray(np.asarray(inputs["L1_ffn_W2"], np.float32).T),
                "b1v": np.ascontiguousarray(np.asarray(inputs["L1_ffn_b1"], np.float32).reshape(DH, 1)),
                "b2v": np.ascontiguousarray(np.asarray(inputs["L1_ffn_b2"], np.float32).reshape(D, 1)),
                "maskC": _MASKC[b, q * T:(q + 1) * T],
            })
        if N_SUB >= 5:
            m.update({
                "WqT1": np.ascontiguousarray(np.asarray(inputs["L1_attn_Wq"], np.float32).T),
                "WkT1": np.ascontiguousarray(np.asarray(inputs["L1_attn_Wk"], np.float32).T),
                "WvT1": np.ascontiguousarray(np.asarray(inputs["L1_attn_Wv"], np.float32).T),
                "WoT1": np.ascontiguousarray(np.asarray(inputs["L1_attn_Wo"], np.float32).T),
                "bqc1": np.ascontiguousarray(np.asarray(inputs["L1_attn_bq"], np.float32).reshape(D, 1)),
                "bkr1": np.ascontiguousarray(np.asarray(inputs["L1_attn_bk"], np.float32).reshape(1, D)),
                "bvr1": np.ascontiguousarray(np.asarray(inputs["L1_attn_bv"], np.float32).reshape(1, D)),
                "bor1": np.ascontiguousarray(np.asarray(inputs["L1_attn_bo"], np.float32).reshape(1, D)),
                "gam_row": np.ascontiguousarray(np.asarray(inputs["L1_attn_gamma"], np.float32).reshape(1, H)),
                "poscol": (q * T + np.arange(128, dtype=np.float32)).reshape(128, 1),
            })
        in_maps.append(m)
    return in_maps


def kernel(**inputs):
    in_maps = _prep_inputs(inputs)
    nc = build_program(N_SUB)
    res = run_bass_kernel_spmd(nc, in_maps, core_ids=list(range(8)))
    out = np.zeros((2, 4 * T, D), np.float32)
    for c in range(8):
        b, q = c // 4, c % 4
        out[b, q * T:(q + 1) * T] = np.asarray(res.results[c]["out"], np.float32)
    return out
